# version 18
# baseline (speedup 1.0000x reference)
import numpy as np
import ml_dtypes
from contextlib import ExitStack

import concourse.bass as bass
import concourse.mybir as mybir
from concourse.bass_utils import run_bass_kernel_spmd

F32 = mybir.dt.float32
BF16 = mybir.dt.bfloat16
AF = mybir.ActivationFunctionType
ALU = mybir.AluOpType
AX = mybir.AxisListType
BF = ml_dtypes.bfloat16

B, S, D = 2, 4096, 2048
NCORES = 8
NTOK = 1024
HALO = 512
NT = NTOK // 128
NHT = HALO // 128
ATT = NT + NHT
NH, NKV, HD = 32, 8, 64
E0, E1 = 8192, 5632
EC0, EC1 = E0 // 128, E1 // 128
KD = 16
NEG = -1.0e30

ENGS = ("sync", "vector", "scalar", "tensor", "gpsimd")

DEBUG_DUMP = False
ALL_BLOCKS = ("l0a", "l0f", "l1a", "l1f")
UNFUSED = [("l0a",), ("l0f",), ("l1a",), ("l1f",)]
FUSED = True


class Ctx:
    def __init__(self, nc, stack):
        self.nc = nc
        self.stack = stack
        self.scope = stack
        self.ops = {e: [] for e in ENGS}
        self._slots = {}
        self.handles = {}
        self.cnt = {}
        self.seen = {e: {} for e in ENGS}
        self.gens = {}
        self.q_stash = {}

    def _sem(self, name):
        if name not in self.handles:
            self.handles[name] = self.stack.enter_context(self.nc.semaphore(name))
            self.cnt[name] = 0
        return self.handles[name]

    def op(self, eng, fn, inc=None, inc_n=1):
        if inc is None:
            self.ops[eng].append(fn)
            return None
        h = self._sem(inc)
        self.cnt[inc] += inc_n
        tgt = self.cnt[inc]

        def wrapped(e, fn=fn, h=h, n=inc_n):
            fn(e).then_inc(h, n)

        self.ops[eng].append(wrapped)
        return (inc, tgt)

    def dma(self, eng, out, in_, stream, slots=4, **kw):
        st = self._slots.setdefault(stream, {"i": 0, "prev": {}})
        i = st["i"] % slots
        st["i"] += 1
        sem = f"{stream}.{i}"
        self.wait(eng, st["prev"].get(i))
        tok = self.op(eng, lambda e: e.dma_start(out=out, in_=in_, **kw), sem, 16)
        st["prev"][i] = tok
        return tok

    def stream_toks(self, stream):
        st = self._slots.get(stream)
        return list(st["prev"].values()) if st else []

    def wait(self, eng, token):
        if token is None:
            return
        if isinstance(token, (list, tuple)) and token and isinstance(token[0], tuple):
            for t in token:
                self.wait(eng, t)
            return
        sem, v = token
        if self.seen[eng].get(sem, 0) >= v:
            return
        self.seen[eng][sem] = v
        h = self._sem(sem)
        self.ops[eng].append(lambda e, h=h, v=v: e.wait_ge(h, v))

    def waits(self, eng, tokens):
        for t in tokens:
            self.wait(eng, t)

    def tok(self, sem):
        return (sem, self.cnt.get(sem, 0))

    def barrier_all(self):
        toks = [(s, self.cnt[s]) for s in list(self.cnt)
                if not (s.startswith("g_") or s.startswith("QO."))]
        for e in ENGS:
            for tk in toks:
                self.wait(e, tk)

    def pump(self, name, n=1):
        g = self.gens.get(name)
        if g is None:
            return
        try:
            for _ in range(n):
                next(g)
        except StopIteration:
            self.gens[name] = None

    def drain(self, name):
        g = self.gens.get(name)
        if g is not None:
            for _ in g:
                pass
            self.gens[name] = None

    def dump(self, name, ap, token, eng="scalar"):
        if not DEBUG_DUMP:
            return None
        t = self.nc.dram_tensor("dbg_" + name, list(ap.shape), ap.dtype,
                                kind="ExternalOutput").ap()
        self.wait(eng, token)
        dtok = self.dma(eng, t[:], ap, "SS")
        self.dbg_toks.append(dtok)
        return dtok

    def sb(self, name, shape, dt=F32):
        return self.scope.enter_context(self.nc.sbuf_tensor(name, shape, dt))

    def ps(self, name, shape, dt=F32):
        return self.scope.enter_context(self.nc.psum_tensor(name, shape, dt))

    def run(self, block):
        for eng in ENGS:
            ops = self.ops[eng]

            def section(e, ops=ops):
                for f in ops:
                    f(e)

            getattr(block, eng)(section)


def emit_ln_tiles(ctx, name, xsrc, n, apply_out, per_tile=None, guard_lag=2,
                  pre_apply=None):
    bnb = ctx.sb(f"{name}_bnb", [128, 24 * n])
    mv = ctx.sb(f"{name}_mv", [128, 2 * n])
    rst = ctx.sb(f"{name}_rst", [128, n])
    vsem, ssem = "sV", "sS"
    tok = None
    for t in range(n):
        ap, rdy = xsrc(t)
        ctx.wait("vector", rdy)
        for i in range(4):
            tok = ctx.op("vector", lambda e, ap=ap, t=t, i=i: e.bn_stats(
                out=bnb[:, t * 24 + i * 6: t * 24 + (i + 1) * 6],
                in_=ap[:, i * 512:(i + 1) * 512]), inc=vsem)
    ctx.wait("vector", tok)
    for t in range(n):
        tok = ctx.op("vector", lambda e, t=t: e.bn_aggr(
            out=mv[:, 2 * t:2 * t + 2], in_=bnb[:, 24 * t:24 * t + 24]), inc=vsem)
    ctx.wait("scalar", tok)
    varv = mv[:].rearrange("p (t two) -> p t two", two=2)[:, :, 1]
    t1 = ctx.op("scalar", lambda e: e.activation(
        out=rst[:], in_=varv, func=AF.Ln, bias=ctx.eps5[:], scale=1.0), inc=ssem)
    ctx.wait("scalar", t1)
    t2 = ctx.op("scalar", lambda e: e.activation(
        out=rst[:], in_=rst[:], func=AF.Exp, scale=-0.5), inc=ssem)
    ctx.wait("vector", t2)
    toks, guards = [], {}
    for t in range(n):
        ap, _ = xsrc(t)
        if t == 0 and pre_apply is not None:
            ctx.wait("vector", pre_apply)
        if t - guard_lag in guards:
            ctx.wait("vector", guards[t - guard_lag])
        ot = apply_out(t)
        atok = ctx.op("vector", lambda e, ap=ap, ot=ot, t=t: e.tensor_scalar(
            out=ot, in0=ap, scalar1=mv[:, 2 * t:2 * t + 1],
            scalar2=rst[:, t:t + 1], op0=ALU.subtract, op1=ALU.mult), inc=vsem)
        toks.append(atok)
        if per_tile is not None:
            guards[t] = per_tile(t, atok)
    return toks


def emit_postln_residual(ctx, name, fsrc_list, xres_ap, tmp):
    done = []

    def per_tile(t, atok):
        xt = xres_ap(fsrc_list[t][2])
        ctx.wait("vector", atok)
        rtok = ctx.op("vector", lambda e, xt=xt: e.tensor_tensor(
            out=xt, in0=xt, in1=tmp[:], op=ALU.add), inc="sV")
        done.append(rtok)
        return rtok

    emit_ln_tiles(ctx, name, lambda t: fsrc_list[t][:2], len(fsrc_list),
                  lambda t: tmp[:], per_tile, guard_lag=1)
    return done


def gen_quant_scales(ctx, which):
    P = 7
    WS_ = 512
    buf = [ctx.sb(f"qs_buf{i}", [128, WS_]) for i in range(P)]
    acc, red, cr, tot = ctx.qs_acc, ctx.qs_red, ctx.qs_cr, ctx.qs_tot
    z = ctx.op("vector", lambda e: e.memset(acc[:], 0.0), inc="sV")
    base = {0: 0, 1: 4, 2: 8, 3: 12, 4: 15, 5: 18}
    pieces = []
    if "l0" in which:
        pieces += [(m, i) for m in (0, 1, 2) for i in range(4)]
    if "l1" in which:
        pieces += [(m, i) for m in (3, 4, 5) for i in range(3)]
    subs = [(m, i, s) for (m, i) in pieces for s in range(4096 // WS_)]
    rtoks, dtoks = {}, {}
    atok = [z]

    def red_job(k):
        m = subs[k][0]
        par = k % P
        ctx.wait("vector", dtoks[k])
        ctx.wait("vector", atok[0])
        rt = ctx.op("vector", lambda e, par=par: e.tensor_reduce(
            out=red[:], in_=buf[par][:], axis=AX.X, op=ALU.add,
            apply_absolute_value=True), inc="sV")
        rtoks[k] = rt
        ctx.wait("vector", rt)
        atok[0] = ctx.op("vector", lambda e, m=m: e.tensor_tensor(
            out=acc[:, m:m + 1], in0=acc[:, m:m + 1], in1=red[:], op=ALU.add),
            inc="sV")

    D = 2
    for n, (m, i, s) in enumerate(subs):
        par = n % P
        ctx.wait("gpsimd", rtoks.get(n - P))
        dtoks[n] = ctx.dma("gpsimd", buf[par][:],
                           ctx.in_qsl[base[m] + i][:, s * WS_:(s + 1) * WS_],
                           "QW", slots=P)
        if n >= D:
            red_job(n - D)
        yield
    for k in range(max(0, len(subs) - D), len(subs)):
        red_job(k)
    ctx.wait("gpsimd", atok[0])
    crt = ctx.op("gpsimd", lambda e: e.tensor_reduce(
        out=cr[:], in_=acc[:], axis=AX.C, op=ALU.add), inc="gq")
    ctx.waits("vector", [ctx.qsrat_tok, ctx.qseps_tok])
    ctx.wait("vector", crt)
    t1 = ctx.op("vector", lambda e: e.tensor_tensor(
        out=tot[:].rearrange("p (a b) -> p a b", a=2),
        in0=cr[0:1, :].unsqueeze(1).broadcast_to([1, 2, 8]),
        in1=ctx.in_qsrat_sb[:].rearrange("p (a b) -> p a b", a=2),
        op=ALU.mult), inc="sV")
    ctx.wait("vector", t1)
    t2 = ctx.op("vector", lambda e: e.tensor_tensor(
        out=tot[:], in0=tot[:], in1=ctx.in_qseps_sb[:], op=ALU.add),
        inc="sV")
    ctx.wait("gpsimd", t2)
    d1 = ctx.dma("gpsimd", ctx.cc_qs_in.ap(), tot[:], "QO")
    ctx.wait("gpsimd", d1)
    c1 = ctx.op("gpsimd", lambda e: e.collective_compute(
        "AllReduce", ALU.add, replica_groups=[list(range(NCORES))],
        ins=[ctx.cc_qs_in.ap()], outs=[ctx.cc_qs_out.ap()]), inc="g_cc",
        inc_n=1)
    ctx.wait("gpsimd", c1)
    ctx.qs_ready = ctx.dma(
        "gpsimd", ctx.thscols[:],
        ctx.cc_qs_out.ap()[0:1, :].partition_broadcast(128).squeeze(1), "QO")


def gen_shard_all(ctx, lis):
    W = 512
    wqb = [ctx.sb(f"sq_w{i}", [128, W]) for i in range(3)]
    sgq = [ctx.sb(f"sq_s{i}", [128, W], BF16) for i in range(3)]
    tqb = [ctx.sb(f"sq_t{i}", [128, W], BF16) for i in range(3)]
    jobs = []
    for li in lis:
        o = 0 if li == 0 else 3
        ths = [ctx.thscols[:, o + m:o + m + 1] for m in range(3)]
        ECb = 8 if li == 0 else 6
        E2b = 4 if li == 0 else 3
        wup = ctx.in_wup0 if li == 0 else ctx.in_wup1
        wgt = ctx.in_wgt0 if li == 0 else ctx.in_wgt1
        wdn = ctx.in_wdn0 if li == 0 else ctx.in_wdn1
        agi = (ctx.ag0_in if li == 0 else ctx.ag1_in).ap()
        for i in range(ECb):
            for h2 in range(2048 // W):
                jobs.append((wup[i][:, h2 * W:(h2 + 1) * W], ths[0],
                             agi[i][:, h2 * W:(h2 + 1) * W], None))
        for i in range(ECb):
            for h2 in range(2048 // W):
                jobs.append((wgt[i][:, h2 * W:(h2 + 1) * W], ths[1],
                             agi[ECb + i][:, h2 * W:(h2 + 1) * W], None))
        for j in range(E2b):
            for dc in range(4):
                idx = j * 4 + dc
                for h2 in range(1024 // W):
                    jobs.append((wdn[j, dc][:, h2 * W:(h2 + 1) * W], ths[2],
                                 agi[2 * ECb + idx // 2]
                                 [:, (idx % 2) * 1024 + h2 * W:
                                  (idx % 2) * 1024 + (h2 + 1) * W],
                                 None))
        jobs[-1] = jobs[-1][:3] + (li,)
    ctx.wait("sync", ctx.qs_ready)
    qtok, stok = {}, {}
    stash = []
    for n, (srcap, th, dst, li_end) in enumerate(jobs):
        par = n % 3
        ctx.wait("sync", qtok.get(n - 3))
        dw = ctx.dma("sync", wqb[par][:], srcap, "QW", slots=6)
        ctx.wait("scalar", dw)
        ctx.wait("scalar", qtok.get(n - 3))
        sg = ctx.op("scalar", lambda e, par=par, th=th: e.activation(
            out=sgq[par][:], in_=wqb[par][:], func=AF.Sign,
            bias=th, scale=1.0), inc="sS")
        ctx.wait("vector", sg)
        ctx.wait("vector", stok.get(n - 3))
        qt_ = ctx.op("vector", lambda e, par=par, th=th: e.scalar_tensor_tensor(
            out=tqb[par][:], in0=wqb[par][:], scalar=th,
            in1=sgq[par][:], op0=ALU.is_gt, op1=ALU.min), inc="sV")
        qtok[n] = qt_
        ctx.wait("scalar", qt_)
        stok[n] = ctx.dma("scalar", dst, tqb[par][:], "QB")
        stash.append(stok[n])
        if li_end is not None:
            agi_t = ctx.ag0_in if li_end == 0 else ctx.ag1_in
            ago_t = ctx.ag0_out if li_end == 0 else ctx.ag1_out
            ctx.waits("gpsimd", list(stash))
            stash = []
            ctx.op("gpsimd", lambda e, agi=agi_t, ago=ago_t: e.collective_compute(
                "AllGather", ALU.bypass, replica_groups=[list(range(NCORES))],
                ins=[agi.ap()], outs=[ago.ap()]),
                inc=f"g_ag{li_end}", inc_n=1)
        yield


def emit_ffn(ctx, li):
    nc = ctx.nc
    E = E0 if li == 0 else E1
    EC = EC0 if li == 0 else EC1
    pre = f"f{li}"
    vs, ss, ts, ds, gs = "sV", "sS", "sT", "WS", "SS"
    o = 0 if li == 0 else 3
    th_up = ctx.thscols[:, o:o + 1]
    th_gt = ctx.thscols[:, o + 1:o + 2]
    th_dn = ctx.thscols[:, o + 2:o + 3]
    s_up = ctx.thscols[:, 8 + o:9 + o]
    s_gt = ctx.thscols[:, 9 + o:10 + o]
    s_dn = ctx.thscols[:, 10 + o:11 + o]

    hbm_hid = ctx.hbm_hid0 if li == 0 else ctx.hbm_hid1
    agv = (ctx.ag0_out if li == 0 else ctx.ag1_out).ap()
    ag_tok = ("g_ag0" if li == 0 else "g_ag1", 1)
    ECb = 8 if li == 0 else 6
    E2b = 4 if li == 0 else 3
    NS = 3 * ECb

    res_toks = []
    ctx.wait("vector", ctx.qs_ready)
    ctx.wait("scalar", ctx.qs_ready)
    with ExitStack() as ph:
        old_scope, ctx.scope = ctx.scope, ph
        hid = ctx.sb(pre + "_hid", [128, EC * 256], BF16)
        aptmp = ctx.sb(pre + "_aptmp", [128, 2048])
        mls = ExitStack()
        ctx.scope = mls
        lnb = ctx.sb(pre + "_lnb", [128, 2048], BF16)
        lnT = ctx.sb(pre + "_lnT", [128, KD * 1024], BF16)
        lnTv = lnT[:].rearrange("p (k w) -> p k w", w=1024)

        with ExitStack() as trs:
            ptr = [trs.enter_context(nc.psum_tensor(pre + f"_ptr{i}", [128, 1024], BF16))
                   for i in range(2)]

            def tr_tile(t, atok):
                last_ev = None
                for g in range(2):
                    par = (2 * t + g) % 2
                    ctx.wait("tensor", atok)
                    tk = None
                    for j in range(8):
                        tk = ctx.op("tensor", lambda e, j=j, g=g, par=par: e.transpose(
                            out=ptr[par][:, j * 128:(j + 1) * 128],
                            in_=lnb[:, (8 * g + j) * 128:(8 * g + j + 1) * 128],
                            identity=ctx.identb[:]), inc=ts if j == 7 else None)
                    ctx.wait("scalar", tk)
                    last_ev = ctx.op("scalar", lambda e, t=t, g=g, par=par: e.copy(
                        out=lnTv[:, 8 * g:8 * g + 8, 128 * t:128 * (t + 1)],
                        in_=ptr[par][:].rearrange("p (k w) -> p k w", w=128)),
                        inc=ss)
                    ctx.wait("tensor", last_ev)
                return last_ev

            emit_ln_tiles(ctx, pre + "ln",
                          lambda t: (ctx.xres_ap(t), ctx.xres_tok),
                          NT, lambda t: lnb[:], tr_tile, guard_lag=1)
        lnT_ready = ctx.tok(ss)
        ctx.dump(pre + "lnT", lnT[:], lnT_ready)

        with ExitStack() as ugs:
            ctx.scope = ugs
            pps = lambda n: ugs.enter_context(nc.psum_tensor(pre + n, [128, 512], F32))
            psu = [[pps(f"_pu{p}{hf}") for hf in range(2)] for p in range(2)]
            psg = [[pps(f"_pg{p}{hf}") for hf in range(2)] for p in range(2)]
            wub = [ctx.sb(pre + f"_wub{i}", [128, 2048], BF16) for i in range(3)]
            wgb = [ctx.sb(pre + f"_wgb{i}", [128, 2048], BF16) for i in range(3)]
            sil = [ctx.sb(pre + f"_sil{i}", [128, 512]) for i in range(2)]
            hstg = [ctx.sb(pre + f"_hst{i}", [128, 1024], BF16) for i in range(3)]

            ctx.wait("sync", ctx.qs_ready)
            ctx.wait("sync", ag_tok)
            mm_par = {}
            glu_toks = {}
            glu_par = {}
            stash_par = {}
            hid_last = None
            for ec in range(EC):
                par = ec % 3
                r, i = divmod(ec, ECb)
                ctx.wait("sync", mm_par.get(par))
                du = ctx.dma("sync", wub[par][:], agv[r * NS + i], ds)
                dg = ctx.dma("sync", wgb[par][:], agv[r * NS + ECb + i], ds)
                ctx.wait("tensor", du)
                ctx.wait("tensor", dg)
                ctx.wait("tensor", glu_toks.get(ec - 2))
                if ec == 0:
                    ctx.wait("tensor", lnT_ready)
                mt = None
                for kk in range(16):
                    st = (kk == 0)
                    sp = (kk == 15)
                    for hf in range(2):
                        ctx.op("tensor", lambda e, par=par, kk=kk, st=st, sp=sp,
                               pb=psu[ec % 2][hf], hf=hf: e.matmul(
                            out=pb[:], lhsT=wub[par][:, kk * 128:(kk + 1) * 128],
                            rhs=lnTv[:, kk, hf * 512:(hf + 1) * 512],
                            start=st, stop=sp), inc=ts)
                    for hf in range(2):
                        mt = ctx.op("tensor", lambda e, par=par, kk=kk, st=st,
                                    sp=sp, pb=psg[ec % 2][hf], hf=hf: e.matmul(
                            out=pb[:], lhsT=wgb[par][:, kk * 128:(kk + 1) * 128],
                            rhs=lnTv[:, kk, hf * 512:(hf + 1) * 512],
                            start=st, stop=sp), inc=ts)
                mm_par[par] = mt
                mm_done = mt
                gt = None
                for hf in range(2):
                    pb_u = psu[ec % 2][hf]
                    pb_g = psg[ec % 2][hf]
                    parS = hf
                    ctx.wait("scalar", mm_done)
                    ctx.wait("scalar", glu_par.get(parS))
                    if li == 0:
                        sv = ctx.op("scalar", lambda e, pb=pb_u, parS=parS: e.activation(
                            out=sil[parS][:], in_=pb[:], func=AF.Silu,
                            scale=s_up), inc=ss)
                    else:
                        sv = ctx.op("scalar", lambda e, pb=pb_u, parS=parS: e.activation(
                            out=sil[parS][:], in_=pb[:], func=AF.Sigmoid), inc=ss)
                    ctx.wait("vector", sv)
                    for sub in range(2):
                        qi = hf * 2 + sub
                        qof = sub * 256
                        if qi == 0:
                            dst = hid[:, ec * 256:(ec + 1) * 256]
                        else:
                            if ec % 4 == 0:
                                ctx.wait("vector", stash_par.get(qi - 1))
                            dst = hstg[qi - 1][:, (ec % 4) * 256:(ec % 4 + 1) * 256]
                        gt = ctx.op("vector", lambda e, pb=pb_g, qof=qof, parS=parS,
                                    dst=dst, li=li: e.scalar_tensor_tensor(
                            out=dst, in0=pb[:, qof:qof + 256],
                            scalar=(s_gt if li == 0 else 1.0),
                            in1=sil[parS][:, qof:qof + 256],
                            op0=ALU.mult, op1=ALU.mult), inc=vs)
                        glu_par[parS] = gt
                        if qi == 0:
                            hid_last = gt
                        elif ec % 4 == 3:
                            ctx.wait("scalar", gt)
                            stash_par[qi - 1] = ctx.dma(
                                "scalar", hbm_hid[qi - 1, ec // 4], hstg[qi - 1][:], gs)
                glu_toks[ec] = gt
            hid_ready = ctx.tok(vs)
            stash_done = ctx.stream_toks(gs)
            hd = ctx.dump(pre + "hid", hid[:], hid_ready)
            if hd is not None:
                hid_ready = hd
        ctx.barrier_all()
        ctx.scope = ph
        mls.close()

        with ExitStack() as dns:
            ctx.scope = dns
            pps = lambda n: dns.enter_context(nc.psum_tensor(pre + n, [128, 512], F32))
            psd = [[[pps(f"_pd{q_}{t_}{d_}") for d_ in range(2)]
                    for t_ in range(2)] for q_ in range(2)]
            dwq = [ctx.sb(pre + f"_dwq{i}", [128, 1024], BF16) for i in range(3)]
            hidB = ctx.sb(pre + "_hidB", [128, EC * 256], BF16)
            fout = [[ctx.sb(pre + f"_fo{q_}{t_}", [128, 2048], BF16)
                     for t_ in range(2)] for q_ in range(2)]
            hids = [hid, hidB]
            EC2 = EC // 2

            dmm3 = {}
            prev_ev = None
            dmm3_ec = {}
            for qp in range(2):
                hload = {0: {}, 1: {}}
                for qq in range(2):
                    quar = 2 * qp + qq
                    if quar == 0:
                        continue
                    for eb in range(EC // 4):
                        if quar == 1 and eb == 0:
                            ctx.wait("sync", stash_done)
                        ctx.wait("sync", dmm3_ec.get(2 * eb + 1))
                        hload[qq][eb] = ctx.dma(
                            "sync", hids[qq][:, eb * 1024:(eb + 1) * 1024],
                            hbm_hid[quar - 1, eb], ds)
                mt = None
                for dh in range(2):
                    for dc2 in range(2):
                        dc = 2 * dh + dc2
                        for ecp in range(EC2):
                            n = dc * EC2 + ecp
                            par3 = n % 3
                            r, j = divmod(ecp, E2b)
                            idx = j * 4 + dc
                            dsrc = agv[r * NS + 2 * ECb + idx // 2][
                                :, (idx % 2) * 1024:(idx % 2) * 1024 + 1024]
                            ctx.wait("sync", dmm3.get(par3))
                            dq = ctx.dma("sync", dwq[par3][:], dsrc, ds)
                            ctx.wait("tensor", dq)
                            if qp == 0 and dh == 0 and dc2 == 0 and ecp == 0:
                                ctx.wait("tensor", hid_ready)
                            if ecp == 0:
                                ctx.wait("tensor", prev_ev)
                            lt = None
                            for e2 in range(2):
                                ec = 2 * ecp + e2
                                st = (ecp == 0 and e2 == 0)
                                sp = (ecp == EC2 - 1 and e2 == 1)
                                for qq in range(2):
                                    if 2 * qp + qq > 0:
                                        ctx.wait("tensor", hload[qq].get(ec // 4))
                                    for tt in range(2):
                                        lt = ctx.op("tensor", lambda e, par3=par3,
                                                    qq=qq, tt=tt, dc2=dc2, ec=ec,
                                                    e2=e2, st=st, sp=sp: e.matmul(
                                            out=psd[qq][tt][dc2][:],
                                            lhsT=hids[qq][:, ec * 256 + tt * 128:
                                                          ec * 256 + (tt + 1) * 128],
                                            rhs=dwq[par3][:, e2 * 512:(e2 + 1) * 512],
                                            start=st, stop=sp), inc=ts)
                            mt = lt
                            dmm3[par3] = lt
                            if dh == 1 and dc2 == 1:
                                dmm3_ec[ecp] = lt
                    ev = None
                    if res_toks:
                        ctx.wait("scalar", res_toks[-1])
                    for qq in range(2):
                        for tt in range(2):
                            for dc2 in range(2):
                                dc = 2 * dh + dc2
                                ctx.wait("scalar", mt)
                                if li == 0:
                                    ev = ctx.op("scalar", lambda e, qq=qq, tt=tt,
                                                dc=dc, dc2=dc2: e.activation(
                                        out=fout[qq][tt][:, dc * 512:(dc + 1) * 512],
                                        in_=psd[qq][tt][dc2][:], func=AF.Copy,
                                        scale=s_dn, bias=0.0), inc=ss)
                                else:
                                    ev = ctx.op("scalar", lambda e, qq=qq, tt=tt,
                                                dc=dc, dc2=dc2: e.copy(
                                        out=fout[qq][tt][:, dc * 512:(dc + 1) * 512],
                                        in_=psd[qq][tt][dc2][:]), inc=ss)
                    prev_ev = ev
                for qq in range(2):
                    quar = 2 * qp + qq
                    fsrc = [(fout[qq][0][:], prev_ev, quar * 2 + 0),
                            (fout[qq][1][:], prev_ev, quar * 2 + 1)]
                    rt = emit_postln_residual(ctx, f"{pre}_po{quar}", fsrc,
                                              ctx.xres_ap, aptmp)
                    res_toks.extend(rt)
                    ctx.xres_tok = rt[-1]
        ctx.scope = old_scope
    return res_toks


def emit_l0_attn(ctx):
    nc = ctx.nc
    vs, ss, ts, ds, gs = "sV", "sS", "sT", "WS", "SS"

    res_toks = []
    with ExitStack() as ph:
        old_scope, ctx.scope = ctx.scope, ph
        vtm = ctx.sb("a_vtm", [128, ATT * 512], BF16)
        kT = ctx.sb("a_kT", [128, 4 * ATT * 128], BF16)
        kTv = kT[:].rearrange("p (g w) -> p g w", w=ATT * 128)
        cosb = ctx.sb("a_cos", [128, ATT * 64])
        sinb = ctx.sb("a_sin", [128, ATT * 64])
        knwb = ctx.sb("a_knw", [128, 512])
        msq = ctx.sb("a_msq", [128, 32])
        rst8 = ctx.sb("a_rst8", [128, 32])

        c1 = ctx.dma("scalar", knwb[:],
                     ctx.in_knw[0:1, :].partition_broadcast(128).squeeze(1), gs)
        c3 = None
        for j in range(ATT):
            ctx.dma("scalar", cosb[:, j * 64:(j + 1) * 64],
                    ctx.in_cos[j * 128:(j + 1) * 128, :], gs)
            c3 = ctx.dma("scalar", sinb[:, j * 64:(j + 1) * 64],
                         ctx.in_sin[j * 128:(j + 1) * 128, :], gs)
        consts_rdy = ctx.stream_toks(gs)

        with ExitStack() as s2:
            ctx.scope = s2
            ln0T = ctx.sb("a_ln0T", [128, ATT * 2048], BF16)
            with ExitStack() as s1:
                ctx.scope = s1
                xhalo = ctx.sb("a_xh", [128, 2048])
                lnb = ctx.sb("a_lnb", [128, 2048], BF16)
                ktmp = ctx.sb("a_ktmp", [128, ATT * 512], BF16)
                kbf = ctx.sb("a_kbf", [128, 512], BF16)
                k32 = ctx.sb("a_k32", [128, 512])
                ksq = ctx.sb("a_ksq", [128, 512])

                with ExitStack() as trs:
                    ptr2 = [trs.enter_context(nc.psum_tensor(
                        "a_ptr" + str(i), [128, 1024], BF16)) for i in range(2)]

                    def tr0(j, atok):
                        ev = None
                        for g in range(2):
                            par = (2 * j + g) % 2
                            ctx.wait("tensor", atok)
                            tk = None
                            for p in range(8):
                                tk = ctx.op("tensor", lambda e, p=p, g=g, par=par: e.transpose(
                                    out=ptr2[par][:, p * 128:(p + 1) * 128],
                                    in_=lnb[:, (8 * g + p) * 128:(8 * g + p + 1) * 128],
                                    identity=ctx.identb[:]), inc=ts if p == 7 else None)
                            ctx.wait("scalar", tk)
                            ev = ctx.op("scalar", lambda e, j=j, g=g, par=par: e.copy(
                                out=ln0T[:, j * 2048 + g * 1024: j * 2048 + (g + 1) * 1024],
                                in_=ptr2[par][:]), inc=ss)
                            ctx.wait("tensor", ev)
                        return ev

                    tr_cell = [None]

                    def tr0g(j, a):
                        tr_cell[0] = tr0(j, a)
                        return tr_cell[0]

                    prev_apply = [None]
                    for j in range(NHT):
                        if prev_apply[0] is not None:
                            ctx.wait("scalar", prev_apply[0])
                        dtok = ctx.dma("scalar", xhalo[:],
                                       ctx.in_x[j * 128:(j + 1) * 128, :], gs)
                        toks = emit_ln_tiles(ctx, f"a0lnh{j}",
                                             lambda t, dtok=dtok: (xhalo[:], dtok),
                                             1, lambda t: lnb[:],
                                             lambda t, a, j=j: tr0g(j, a),
                                             guard_lag=1, pre_apply=tr_cell[0])
                        prev_apply[0] = toks[-1]
                    emit_ln_tiles(ctx, "a0ln",
                                  lambda t: (ctx.xres_ap(t), ctx.xres_tok),
                                  NT, lambda t: lnb[:],
                                  lambda t, a: tr0g(t + NHT, a), guard_lag=1,
                                  pre_apply=tr_cell[0])
                ln0T_ready = ctx.tok(ss)

                with ExitStack() as kvps:
                    pkv = [kvps.enter_context(nc.psum_tensor(
                        f"a_pkv{i}", [128, 128], F32)) for i in range(2)]
                    wkvb = ctx.sb("a_wkv0", [128, 2048])
                    wkvc = [ctx.sb(f"a_wkvc{i}", [128, 2048], BF16) for i in range(2)]
                    cast_tok, ev_tok, mm_tok = {}, {}, {}
                    for hc in range(8):
                        par = hc % 2
                        ctx.wait("sync", cast_tok.get(hc - 1))
                        dw = ctx.dma("sync", wkvb[:], ctx.in_wkv[hc], ds)
                        ctx.wait("scalar", dw)
                        ctx.wait("scalar", mm_tok.get(hc - 2))
                        ck = ctx.op("scalar", lambda e, par=par: e.copy(
                            out=wkvc[par][:], in_=wkvb[:]), inc=ss)
                        cast_tok[hc] = ck
                        ctx.wait("tensor", ck)
                        if hc == 0:
                            ctx.wait("tensor", ln0T_ready)
                        wv = wkvc[par][:].rearrange("p (k w) -> p k w", w=128)
                        for j in range(ATT):
                            ctx.pump("scales", 2)
                            pp = (hc * ATT + j) % 2
                            ctx.wait("tensor", ev_tok.get((hc * ATT + j) - 2))
                            mm = None
                            for k in range(16):
                                mm = ctx.op("tensor", lambda e, k=k, j=j, pp=pp, wv=wv: e.matmul(
                                    out=pkv[pp][:],
                                    lhsT=ln0T[:, j * 2048 + k * 128: j * 2048 + (k + 1) * 128],
                                    rhs=wv[:, k, :], start=(k == 0), stop=(k == 15)),
                                    inc=ts)
                            mm_tok[hc] = mm
                            ctx.wait("scalar", mm)
                            if hc < 4:
                                ev = ctx.op("scalar", lambda e, j=j, hc=hc, pp=pp: e.copy(
                                    out=ktmp[:, j * 512 + hc * 128: j * 512 + (hc + 1) * 128],
                                    in_=pkv[pp][:]), inc=ss)
                            else:
                                ev = ctx.op("scalar", lambda e, j=j, hc=hc, pp=pp: e.copy(
                                    out=vtm[:, j * 512 + (hc - 4) * 128: j * 512 + (hc - 3) * 128],
                                    in_=pkv[pp][:]), inc=ss)
                            ev_tok[hc * ATT + j] = ev
                ctx.drain("scales")
                v_ready = ctx.tok(ss)

                ctx.wait("vector", consts_rdy)
                with ExitStack() as kps:
                    pktr = kps.enter_context(nc.psum_tensor("a_pktr", [128, 512], BF16))
                    tr_ev = None
                    for j in range(ATT):
                        kap = ktmp[:, j * 512:(j + 1) * 512]
                        ctx.wait("scalar", v_ready)
                        sqk = ctx.op("scalar", lambda e, kap=kap: e.activation(
                            out=ksq[:], in_=kap, func=AF.Square), inc=ss)
                        ctx.wait("vector", sqk)
                        rd = ctx.op("vector", lambda e: e.tensor_reduce(
                            out=msq[:, 0:8],
                            in_=ksq[:].rearrange("p (h w) -> p h w", w=64),
                            axis=AX.X, op=ALU.add), inc=vs)
                        ctx.wait("scalar", rd)
                        l1 = ctx.op("scalar", lambda e: e.activation(
                            out=rst8[:, 0:8], in_=msq[:, 0:8], func=AF.Ln,
                            bias=ctx.eps6[:], scale=1.0 / 64), inc=ss)
                        ctx.wait("scalar", l1)
                        l2 = ctx.op("scalar", lambda e: e.activation(
                            out=rst8[:, 0:8], in_=rst8[:, 0:8], func=AF.Exp,
                            scale=-0.5), inc=ss)
                        ctx.wait("vector", l2)
                        a1 = ctx.op("vector", lambda e, kap=kap: e.tensor_tensor(
                            out=k32[:].rearrange("p (h w) -> p h w", w=64),
                            in0=kap.rearrange("p (h w) -> p h w", w=64),
                            in1=rst8[:, 0:8].unsqueeze(2).broadcast_to([128, 8, 64]),
                            op=ALU.mult), inc=vs)
                        ctx.wait("vector", a1)
                        a2 = ctx.op("vector", lambda e: e.tensor_tensor(
                            out=k32[:], in0=k32[:], in1=knwb[:], op=ALU.mult), inc=vs)
                        ctx.wait("vector", a2)
                        kv4 = k32[:].rearrange("p (h two w) -> p h two w", two=2, w=32)
                        t4 = ksq[:].rearrange("p (h two w) -> p h two w", two=2, w=32)
                        sin4 = sinb[:, j * 64:(j + 1) * 64].rearrange(
                            "p (two w) -> p two w", two=2).unsqueeze(1).broadcast_to(
                            [128, 8, 2, 32])
                        cos3 = cosb[:, j * 64:(j + 1) * 64].unsqueeze(1).broadcast_to(
                            [128, 8, 64])
                        r1 = ctx.op("vector", lambda e, t4=t4, kv4=kv4, sin4=sin4: e.tensor_tensor(
                            out=t4[:, :, 0, :], in0=kv4[:, :, 1, :],
                            in1=sin4[:, :, 0, :], op=ALU.mult), inc=vs)
                        r2 = ctx.op("vector", lambda e, t4=t4, kv4=kv4, sin4=sin4: e.tensor_tensor(
                            out=t4[:, :, 1, :], in0=kv4[:, :, 0, :],
                            in1=sin4[:, :, 1, :], op=ALU.mult), inc=vs)
                        ctx.wait("vector", r2)
                        r3 = ctx.op("vector", lambda e, cos3=cos3: e.tensor_tensor(
                            out=k32[:].rearrange("p (h w) -> p h w", w=64),
                            in0=k32[:].rearrange("p (h w) -> p h w", w=64),
                            in1=cos3, op=ALU.mult), inc=vs)
                        ctx.wait("vector", r3)
                        r4 = ctx.op("vector", lambda e: e.tensor_tensor(
                            out=k32[:], in0=k32[:], in1=ksq[:], op=ALU.add),
                            inc=vs)
                        ctx.wait("scalar", r4)
                        cst = ctx.op("scalar", lambda e: e.copy(
                            out=kbf[:], in_=k32[:]), inc=ss)
                        ctx.wait("tensor", cst)
                        ctx.wait("tensor", tr_ev)
                        tk = None
                        for g in range(4):
                            tk = ctx.op("tensor", lambda e, g=g: e.transpose(
                                out=pktr[:, g * 128:(g + 1) * 128],
                                in_=kbf[:, g * 128:(g + 1) * 128],
                                identity=ctx.identb[:]), inc=ts if g == 3 else None)
                        ctx.wait("scalar", tk)
                        tr_ev = ctx.op("scalar", lambda e, j=j: e.copy(
                            out=kTv[:, :, j * 128:(j + 1) * 128],
                            in_=pktr[:].rearrange("p (g w) -> p g w", w=128)),
                            inc=ss)
                        ctx.wait("tensor", tr_ev)
                kT_ready = ctx.tok(ss)
                ctx.scope = s2
                ctx.barrier_all()

            with ExitStack() as qps:
                ctx.scope = qps
                pq = [qps.enter_context(nc.psum_tensor(
                    f"a_pq{i}", [128, 128], F32)) for i in range(2)]
                qtmp = ctx.sb("a_qtmp", [128, NT * 2048], BF16)
                wqb = [ctx.sb(f"a_wqb{i}", [128, 2048]) for i in range(2)]
                wqc = [ctx.sb(f"a_wqc{i}", [128, 2048], BF16) for i in range(2)]
                cast_tok, ev_tok, mm_tok = {}, {}, {}
                for oc in range(16):
                    par = oc % 2
                    ctx.wait("sync", cast_tok.get(oc - 2))
                    dw = ctx.dma("sync", wqb[par][:], ctx.in_wq[oc], ds)
                    ctx.wait("scalar", dw)
                    ctx.wait("scalar", mm_tok.get(oc - 2))
                    ck = ctx.op("scalar", lambda e, par=par: e.copy(
                        out=wqc[par][:], in_=wqb[par][:]), inc=ss)
                    cast_tok[oc] = ck
                    ctx.wait("tensor", ck)
                    wv = wqc[par][:].rearrange("p (k w) -> p k w", w=128)
                    for t in range(NT):
                        pp = (oc * NT + t) % 2
                        ctx.wait("tensor", ev_tok.get((oc * NT + t) - 2))
                        mm = None
                        for k in range(16):
                            j = t + NHT
                            mm = ctx.op("tensor", lambda e, k=k, j=j, pp=pp, wv=wv: e.matmul(
                                out=pq[pp][:],
                                lhsT=ln0T[:, j * 2048 + k * 128: j * 2048 + (k + 1) * 128],
                                rhs=wv[:, k, :], start=(k == 0), stop=(k == 15)),
                                inc=ts)
                        mm_tok[oc] = mm
                        ctx.wait("scalar", mm)
                        ev = ctx.op("scalar", lambda e, t=t, oc=oc, pp=pp: e.copy(
                            out=qtmp[:, t * 2048 + oc * 128: t * 2048 + (oc + 1) * 128],
                            in_=pq[pp][:]), inc=ss)
                        ev_tok[oc * NT + t] = ev
                qmm_done = ctx.tok(ss)
                ctx.dump("a_qtmp", qtmp[:], qmm_done)
                ctx.dump("a_ln0T", ln0T[:], qmm_done)
                ctx.wait("scalar", qmm_done)
                for t in range(NT):
                    ctx.dma("scalar", ctx.hbm_qtmp[t],
                            qtmp[:, t * 2048:(t + 1) * 2048], gs)
                qstash = ctx.stream_toks(gs)
            ctx.scope = ph
        ctx.barrier_all()

        attnT = ctx.sb("a_attnT", [128, KD * 1024], BF16)
        attnTv = attnT[:].rearrange("p (k w) -> p k w", w=1024)

        with ExitStack() as ats:
            ctx.scope = ats
            rop = ctx.sb("a_rop", [128, 2048])
            sq = ctx.sb("a_sq", [128, 2048])
            qnwb = ctx.sb("a_qnw", [128, 2048])
            masks = ctx.sb("a_msk", [128, NT * 640], BF16)
            qb = ctx.sb("a_qb", [128, 2048], BF16)
            qT = ctx.sb("a_qT", [128, 2048], BF16)
            probs = [ctx.sb(f"a_probs{i}", [128, 640], BF16) for i in range(3)]
            probsT = [ctx.sb(f"a_probsT{i}", [128, 640], BF16) for i in range(2)]
            scmax = ctx.sb("a_scmax", [128, 4])
            ssum = ctx.sb("a_ssum", [128, 8])
            rec = ctx.sb("a_rec", [128, 8])
            atm = ctx.sb("a_atm", [128, 2048], BF16)
            qcur = [ctx.sb(f"a_qcur{i}", [128, 2048], BF16) for i in range(2)]
            cq = ctx.dma("scalar", qnwb[:],
                         ctx.in_qnw[0:1, :].partition_broadcast(128).squeeze(1), gs)
            cm = None
            for i in range(NT):
                cm = ctx.dma("scalar", masks[:, i * 640:(i + 1) * 640],
                             ctx.in_masks[i], gs)
            ctx.wait("vector", cm)
            ctx.wait("vector", cq)
            ctx.op("vector", lambda e: e.tensor_scalar(
                out=qnwb[:], in0=qnwb[:], scalar1=0.125, scalar2=None,
                op0=ALU.mult), inc="sV")
            psc = [ats.enter_context(nc.psum_tensor(
                f"a_psc{i}", [128, 1024], F32)) for i in range(2)]
            ppv = [ats.enter_context(nc.psum_tensor(
                f"a_ppv{i}", [128, 512], F32)) for i in range(2)]
            ptr3 = [ats.enter_context(nc.psum_tensor(
                f"a_ptr3{i}", [128, 1024], BF16)) for i in range(2)]
            exp_tok, ptr_done, pv_ev, grp_ev, rc_tok = {}, {}, {}, {}, {}
            pvm_tok, ptr3_free, pev_tok = {}, {}, {}
            qload, qfree = {}, {}
            if DEBUG_DUMP:
                dbg_qcur = ctx.sb("dbgb_qcur", [128, 2048], BF16)
                dbg_sq = ctx.sb("dbgb_sq", [128, 2048])
                dbg_msq = ctx.sb("dbgb_msq", [128, 32 * NT])
                dbg_rst = ctx.sb("dbgb_rst", [128, 32 * NT])
                dbg_ssum = ctx.sb("dbgb_ssum", [128, NH])
                dbg_smax = ctx.sb("dbgb_smax", [128, NH])
            for t in range(NT):
                qap = qcur[t % 2][:]
                ctx.wait("scalar", qstash)
                ctx.wait("scalar", qfree.get(t - 2))
                qload[t] = ctx.dma("scalar", qap, ctx.hbm_qtmp[t], gs)
                ctx.wait("scalar", qload[t])
                sqk = ctx.op("scalar", lambda e, qap=qap: e.activation(
                    out=sq[:], in_=qap, func=AF.Square), inc=ss)
                ctx.wait("vector", sqk)
                if DEBUG_DUMP and t == 0:
                    ctx.op("vector", lambda e, qap=qap: e.tensor_copy(
                        out=dbg_qcur[:], in_=qap), inc=vs)
                    ctx.op("vector", lambda e: e.tensor_copy(
                        out=dbg_sq[:], in_=sq[:]), inc=vs)
                rd = ctx.op("vector", lambda e: e.tensor_reduce(
                    out=msq[:], in_=sq[:].rearrange("p (h w) -> p h w", w=64),
                    axis=AX.X, op=ALU.add), inc=vs)
                ctx.wait("scalar", rd)
                l1 = ctx.op("scalar", lambda e: e.activation(
                    out=rst8[:], in_=msq[:], func=AF.Ln, bias=ctx.eps6[:],
                    scale=1.0 / 64), inc=ss)
                ctx.wait("scalar", l1)
                l2 = ctx.op("scalar", lambda e: e.activation(
                    out=rst8[:], in_=rst8[:], func=AF.Exp, scale=-0.5), inc=ss)
                ctx.wait("vector", l2)
                if DEBUG_DUMP:
                    ctx.op("vector", lambda e, t=t: e.tensor_copy(
                        out=dbg_msq[:, t * 32:(t + 1) * 32], in_=msq[:]), inc=vs)
                    ctx.op("vector", lambda e, t=t: e.tensor_copy(
                        out=dbg_rst[:, t * 32:(t + 1) * 32], in_=rst8[:]), inc=vs)
                a1 = ctx.op("vector", lambda e, qap=qap: e.tensor_tensor(
                    out=rop[:].rearrange("p (h w) -> p h w", w=64),
                    in0=qap.rearrange("p (h w) -> p h w", w=64),
                    in1=rst8[:].unsqueeze(2).broadcast_to([128, 32, 64]),
                    op=ALU.mult), inc=vs)
                ctx.wait("vector", a1)
                qfree[t] = a1
                a2 = ctx.op("vector", lambda e: e.tensor_tensor(
                    out=rop[:], in0=rop[:], in1=qnwb[:], op=ALU.mult), inc=vs)
                ctx.wait("vector", a2)
                q4 = rop[:].rearrange("p (h two w) -> p h two w", two=2, w=32)
                t4 = sq[:].rearrange("p (h two w) -> p h two w", two=2, w=32)
                jj = t + NHT
                sin4 = sinb[:, jj * 64:(jj + 1) * 64].rearrange(
                    "p (two w) -> p two w", two=2).unsqueeze(1).broadcast_to(
                    [128, 32, 2, 32])
                cos3 = cosb[:, jj * 64:(jj + 1) * 64].unsqueeze(1).broadcast_to(
                    [128, 32, 64])
                r1 = ctx.op("vector", lambda e, t4=t4, q4=q4, sin4=sin4: e.tensor_tensor(
                    out=t4[:, :, 0, :], in0=q4[:, :, 1, :], in1=sin4[:, :, 0, :],
                    op=ALU.mult), inc=vs)
                r2 = ctx.op("vector", lambda e, t4=t4, q4=q4, sin4=sin4: e.tensor_tensor(
                    out=t4[:, :, 1, :], in0=q4[:, :, 0, :], in1=sin4[:, :, 1, :],
                    op=ALU.mult), inc=vs)
                ctx.wait("vector", r2)
                r3 = ctx.op("vector", lambda e, cos3=cos3: e.tensor_tensor(
                    out=rop[:].rearrange("p (h w) -> p h w", w=64),
                    in0=rop[:].rearrange("p (h w) -> p h w", w=64),
                    in1=cos3, op=ALU.mult), inc=vs)
                ctx.wait("vector", r3)
                r4 = ctx.op("vector", lambda e: e.tensor_tensor(
                    out=rop[:], in0=rop[:], in1=sq[:], op=ALU.add), inc=vs)
                ctx.wait("scalar", r4)
                cst = ctx.op("scalar", lambda e: e.copy(out=qb[:], in_=rop[:]),
                             inc=ss)
                ctx.wait("tensor", cst)
                for g in range(2):
                    ctx.wait("tensor", ptr3_free.get(g))
                    tk = None
                    for p in range(8):
                        tk = ctx.op("tensor", lambda e, p=p, g=g: e.transpose(
                            out=ptr3[g][:, p * 128:(p + 1) * 128],
                            in_=qb[:, (8 * g + p) * 128:(8 * g + p + 1) * 128],
                            identity=ctx.identb[:]), inc=ts if p == 7 else None)
                    ctx.wait("scalar", tk)
                    ptr3_free[g] = ctx.op("scalar", lambda e, g=g: e.copy(
                        out=qT[:, g * 1024:(g + 1) * 1024], in_=ptr3[g][:]),
                        inc=ss)
                qT_ready = ptr3_free[1]

                def emit_scores(h):
                    hg = t * NH + h
                    par = hg % 2
                    s4 = hg % 4
                    g = h // 4
                    base = 64 * (g % 2)
                    pc = (h // 8) * 4 + h % 4
                    lq = qT[base:base + 64, 128 * pc:128 * pc + 128]
                    rk = kTv[64 * (g % 2):64 * (g % 2) + 64, g // 2,
                             128 * t:128 * t + 640]
                    ctx.wait("tensor", exp_tok.get(hg - 2))
                    if h == 0:
                        ctx.wait("tensor", qT_ready)
                    if hg == 0:
                        ctx.wait("tensor", kT_ready)
                    ctx.op("tensor", lambda e, lq=lq, rk=rk, par=par: e.matmul(
                        out=psc[par][:, 0:512], lhsT=lq, rhs=rk[:, 0:512],
                        start=True, stop=False), inc=ts)
                    ctx.op("tensor", lambda e, lq=lq, rk=rk, par=par: e.matmul(
                        out=psc[par][:, 512:640], lhsT=lq, rhs=rk[:, 512:640],
                        start=True, stop=False), inc=ts)
                    ctx.op("tensor", lambda e, par=par, t=t: e.matmul(
                        out=psc[par][:, 0:512], lhsT=ctx.identb[:],
                        rhs=masks[:, t * 640:t * 640 + 512],
                        start=False, stop=True), inc=ts)
                    m2 = ctx.op("tensor", lambda e, par=par, t=t: e.matmul(
                        out=psc[par][:, 512:640], lhsT=ctx.identb[:],
                        rhs=masks[:, t * 640 + 512:t * 640 + 640],
                        start=False, stop=True), inc=ts)
                    ctx.wait("vector", m2)
                    ctx.wait("vector", exp_tok.get(hg - 4))
                    mx = ctx.op("vector", lambda e, par=par, s4=s4: e.tensor_reduce(
                        out=scmax[:, s4:s4 + 1], in_=psc[par][:, 0:640],
                        axis=AX.X, op=ALU.max, negate=True), inc=vs)
                    return mx

                def emit_exp(h, mx):
                    hg = t * NH + h
                    s4 = hg % 4
                    pb = hg % 3
                    par = hg % 2
                    ctx.wait("scalar", ptr_done.get(hg - 3))
                    ctx.wait("scalar", rc_tok.get(hg - 8))
                    ctx.wait("scalar", mx)
                    ex = ctx.op("scalar", lambda e, s4=s4, pb=pb, par=par,
                                h=h: e.activation(
                        out=probs[pb][:], in_=psc[par][:, 0:640], func=AF.Exp,
                        scale=1.0, bias=scmax[:, s4:s4 + 1],
                        accum_out=ssum[:, h % 8:h % 8 + 1]), inc=ss)
                    exp_tok[hg] = ex
                    return ex

                def emit_rc(h):
                    hg = t * NH + h
                    ctx.wait("vector", exp_tok[hg])
                    rc = ctx.op("vector", lambda e, h=h: e.reciprocal(
                        out=rec[:, h % 8:h % 8 + 1],
                        in_=ssum[:, h % 8:h % 8 + 1]), inc=vs)
                    rc_tok[hg] = rc

                def emit_use_a(h):
                    hg = t * NH + h
                    pb = hg % 3
                    pb2 = hg % 2
                    ctx.wait("tensor", exp_tok[hg])
                    ctx.wait("tensor", ptr3_free.get(pb2))
                    tk = None
                    for c in range(5):
                        tk = ctx.op("tensor", lambda e, c=c, pb=pb, pb2=pb2: e.transpose(
                            out=ptr3[pb2][:, c * 128:(c + 1) * 128],
                            in_=probs[pb][:, c * 128:(c + 1) * 128],
                            identity=ctx.identb[:]), inc=ts if c == 4 else None)
                    ptr_done[hg] = tk
                    ctx.wait("vector", tk)
                    ctx.wait("vector", pvm_tok.get(hg - 2))
                    pev = ctx.op("vector", lambda e, pb2=pb2: e.tensor_copy(
                        out=probsT[pb2][:], in_=ptr3[pb2][:, 0:640]), inc=vs)
                    ptr3_free[pb2] = pev
                    pev_tok[hg] = pev

                def emit_use_b(h):
                    hg = t * NH + h
                    g = h // 4
                    grp = (hg // 8) % 2
                    pb2 = hg % 2
                    ctx.wait("tensor", pev_tok[hg])
                    ctx.wait("tensor", grp_ev.get(hg // 8 - 2))
                    if hg == 0:
                        ctx.wait("tensor", v_ready)
                    pv = None
                    for c in range(5):
                        pv = ctx.op("tensor", lambda e, c=c, g=g, grp=grp, h=h,
                                    t=t, pb2=pb2: e.matmul(
                            out=ppv[grp][:, 64 * (h % 8):64 * (h % 8) + 64],
                            lhsT=probsT[pb2][:, c * 128:(c + 1) * 128],
                            rhs=vtm[:, (t + c) * 512 + g * 64:(t + c) * 512 + g * 64 + 64],
                            start=(c == 0), stop=(c == 4)),
                            inc=ts)
                    pvm_tok[hg] = pv
                    ctx.wait("vector", pv)
                    ctx.wait("vector", rc_tok[hg])
                    evv = ctx.op("vector", lambda e, h=h, grp=grp: e.tensor_scalar(
                        out=atm[:, 64 * h:64 * h + 64],
                        in0=ppv[grp][:, 64 * (h % 8):64 * (h % 8) + 64],
                        scalar1=rec[:, h % 8:h % 8 + 1], scalar2=None,
                        op0=ALU.mult), inc=vs)
                    pv_ev[hg] = evv
                    if h % 8 == 7:
                        grp_ev[hg // 8] = evv
                    return evv

                for h in range(NH):
                    ctx.pump("shard")
                    mx_h = emit_scores(h)
                    if h >= 1:
                        emit_rc(h - 1)
                    emit_exp(h, mx_h)
                    if h >= 3:
                        emit_use_b(h - 3)
                    if h >= 2:
                        emit_use_a(h - 2)
                emit_rc(NH - 1)
                emit_use_b(NH - 3)
                emit_use_a(NH - 2)
                emit_use_b(NH - 2)
                emit_use_a(NH - 1)
                last_ev = emit_use_b(NH - 1)

                ctx.wait("tensor", last_ev)
                for g in range(2):
                    ctx.wait("tensor", ptr3_free.get(g))
                    tk = None
                    for p in range(8):
                        tk = ctx.op("tensor", lambda e, p=p, g=g: e.transpose(
                            out=ptr3[g][:, p * 128:(p + 1) * 128],
                            in_=atm[:, (8 * g + p) * 128:(8 * g + p + 1) * 128],
                            identity=ctx.identb[:]), inc=ts if p == 7 else None)
                    ctx.wait("scalar", tk)
                    ptr3_free[g] = ctx.op("scalar", lambda e, g=g, t=t: e.copy(
                        out=attnTv[:, 8 * g:8 * g + 8, 128 * t:128 * (t + 1)],
                        in_=ptr3[g][:].rearrange("p (k w) -> p k w", w=128)),
                        inc=ss)
            ctx.drain("shard")
        attnT_ready = ctx.tok(ss)
        if DEBUG_DUMP:
            vtok = ctx.tok("sV")
            ctx.dump("a_qcur", dbg_qcur[:], vtok)
            ctx.dump("a_sq", dbg_sq[:], vtok)
            ctx.dump("a_msq", dbg_msq[:], vtok)
            ctx.dump("a_rst", dbg_rst[:], vtok)
            ctx.dump("a_ssumD", dbg_ssum[:], vtok)
            ctx.dump("a_smaxD", dbg_smax[:], vtok)
        ctx.dump("a_attnT", attnT[:], attnT_ready)
        ctx.dump("a_kT", kT[:], attnT_ready)
        ctx.dump("a_vtm", vtm[:], attnT_ready)
        ctx.barrier_all()

        with ExitStack() as pjs:
            ctx.scope = pjs
            ppj = [pjs.enter_context(nc.psum_tensor(
                f"a_ppj{i}", [128, 512], F32)) for i in range(8)]
            wob = [ctx.sb(f"a_wob{i}", [128, 512]) for i in range(2)]
            woc = [ctx.sb(f"a_woc{i}", [128, 512], BF16) for i in range(2)]
            fat = ctx.sb("a_fat", [128, NT * 2048], BF16)
            aptmp = ctx.sb("a_aptmp", [128, 2048])
            cast_tok, mm_tok = {}, {}
            prev_ev = None
            for dc in range(4):
                mt = None
                for p in range(16):
                    n = dc * 16 + p
                    par = n % 2
                    ctx.wait("sync", cast_tok.get(n - 2))
                    dw = ctx.dma("sync", wob[par][:],
                                 ctx.in_wout[p][:, dc * 512:(dc + 1) * 512], ds)
                    ctx.wait("scalar", dw)
                    ctx.wait("scalar", mm_tok.get(n - 2))
                    ck = ctx.op("scalar", lambda e, par=par: e.copy(
                        out=woc[par][:], in_=wob[par][:]), inc=ss)
                    cast_tok[n] = ck
                    ctx.wait("tensor", ck)
                    if n == 0:
                        ctx.wait("tensor", attnT_ready)
                    if p == 0:
                        ctx.wait("tensor", prev_ev)
                    for t in range(NT):
                        mt = ctx.op("tensor", lambda e, p=p, t=t, par=par: e.matmul(
                            out=ppj[t][:], lhsT=attnTv[:, p, 128 * t:128 * (t + 1)],
                            rhs=woc[par][:], start=(p == 0), stop=(p == 15)),
                            inc=ts)
                    mm_tok[n] = mt
                ctx.wait("scalar", mt)
                if res_toks:
                    ctx.wait("scalar", res_toks[-1])
                ev = None
                for t in range(NT):
                    ev = ctx.op("scalar", lambda e, t=t, dc=dc: e.copy(
                        out=fat[:, t * 2048 + dc * 512: t * 2048 + (dc + 1) * 512],
                        in_=ppj[t][:]), inc=ss)
                prev_ev = ev
            ctx.dump("a_fat", fat[:], prev_ev)
            fsrc = [(fat[:, t * 2048:(t + 1) * 2048], prev_ev, t) for t in range(NT)]
            res_toks = emit_postln_residual(ctx, "a0_po", fsrc, ctx.xres_ap, aptmp)
            ctx.xres_tok = res_toks[-1]
        ctx.scope = old_scope
    return res_toks


def emit_l1_attn(ctx):
    nc = ctx.nc
    vs, ss, ts, ds, gs = "sV", "sS", "sT", "WS", "SS"
    NC = 16

    res_toks = []
    with ExitStack() as ph:
        old_scope, ctx.scope = ctx.scope, ph
        lnb = ctx.sb("l1_lnb", [128, 2048], BF16)
        ln2T = ctx.sb("l1_ln2T", [128, KD * 1024], BF16)
        lnTv = ln2T[:].rearrange("p (k w) -> p k w", w=1024)
        qf = ctx.sb("l1_qf", [64, 1024])
        kf = ctx.sb("l1_kf", [64, 1024])
        zb = ctx.sb("l1_zb", [64, 1024])
        zd = ctx.sb("l1_zd", [64, 1024])
        az = ctx.sb("l1_az", [64, 1024])
        ex1 = ctx.sb("l1_ex1", [64, 1024])
        lgb = ctx.sb("l1_lgb", [64, 1024])
        dlt = ctx.sb("l1_dlt", [64, 1024])
        Lc = ctx.sb("l1_L", [64, 1024])
        v64 = ctx.sb("l1_v64", [64, NC * 128], BF16)
        atst = ctx.sb("l1_at", [64, NC * 64], BF16)
        qst = ctx.sb("l1_qst", [64, NC * 64], BF16)
        tmat = ctx.sb("l1_tmat", [64, NC * 128])
        pcs = ctx.sb("l1_pc", [64, NC])
        Lrel = ctx.sb("l1_lrel", [64, 64])
        Pfm = ctx.sb("l1_P", [64, 64])
        Png = ctx.sb("l1_Png", [64, 64])
        Pr2 = ctx.sb("l1_Pr2", [64, 64])
        kd_ = ctx.sb("l1_kd", [64, 64])
        qtb = ctx.sb("l1_qtb", [64, 64], BF16)
        ktb = ctx.sb("l1_ktb", [64, 64], BF16)
        k2b = ctx.sb("l1_k2b", [64, 64], BF16)
        k2t = ctx.sb("l1_k2t", [64, 64], BF16)
        Sa = ctx.sb("l1_Sa", [64, 128])
        Sbf = ctx.sb("l1_Sbf", [64, 128], BF16)
        pay = ctx.sb("l1_pay", [64, 129])
        gath = ctx.sb("l1_gath", [64, 3 * 129])
        aj = ctx.sb("l1_aj", [64, 1])
        mpj = ctx.sb("l1_mpj", [64, 128])
        obf = ctx.sb("l1_obf", [64, 128], BF16)
        ofm = ctx.sb("l1_ofm", [128, NC * 64], BF16)
        trim = ctx.sb("l1_trim", [64, 64], BF16)
        bbc = ctx.sb("l1_bbc", [64, 1])
        bdc = ctx.sb("l1_bdc", [64, 1])
        usec = ctx.sb("l1_use", [64, 6])
        wqc = ctx.sb("l1_wqc", [128, 1024], BF16)
        wkc = ctx.sb("l1_wkc", [128, 1024], BF16)
        wbc = ctx.sb("l1_wbc", [128, 1024], BF16)
        wdc = ctx.sb("l1_wdc", [128, 1024], BF16)
        wvc = ctx.sb("l1_wvc", [128, 2048], BF16)
        woc2 = ctx.sb("l1_woc", [128, 2048], BF16)
        wstg = [ctx.sb("l1_wstg0", [128, 2048])]
        fat = [ctx.sb(f"l1_fat{i}", [128, 2048], BF16) for i in range(2)]
        aptmp = ctx.sb("l1_aptmp", [128, 2048])

        dtr = ctx.dma("scalar", trim[:], ctx.in_trim[:], gs)
        db1 = ctx.dma("scalar", bbc[:], ctx.in_bb.unsqueeze(1), gs,
                      allow_slow_non_contiguous=True)
        db2 = ctx.dma("scalar", bdc[:], ctx.in_bd.unsqueeze(1), gs,
                      allow_slow_non_contiguous=True)
        du = ctx.dma("scalar", usec[:], ctx.in_l1use[:], gs)
        wlist = [(ctx.in_l1wq, wqc, 1024), (ctx.in_l1wk, wkc, 1024),
                 (ctx.in_l1wb, wbc, 1024), (ctx.in_l1wd, wdc, 1024),
                 (ctx.in_l1wv, wvc, 2048), (ctx.in_l1wo, woc2, 2048)]
        ctok = {}
        for i, (inp, dst, w_) in enumerate(wlist):
            ctx.wait("sync", ctok.get(i - 1))
            dw = ctx.dma("sync", wstg[0][:, 0:w_], inp[:], ds)
            ctx.wait("scalar", dw)
            ctok[i] = ctx.op("scalar", lambda e, d=dst, w_=w_: e.copy(
                out=d[:], in_=wstg[0][:, 0:w_]), inc=ss)
        w_rdy = ctok[len(wlist) - 1]

        with ExitStack() as trs:
            ptr2 = [trs.enter_context(nc.psum_tensor(
                "l1_ptr" + str(i), [128, 1024], BF16)) for i in range(2)]

            def trt(t, atok):
                ev = None
                for g in range(2):
                    par = (2 * t + g) % 2
                    ctx.wait("tensor", atok)
                    tk = None
                    for p in range(8):
                        tk = ctx.op("tensor", lambda e, p=p, g=g, par=par: e.transpose(
                            out=ptr2[par][:, p * 128:(p + 1) * 128],
                            in_=lnb[:, (8 * g + p) * 128:(8 * g + p + 1) * 128],
                            identity=ctx.identb[:]), inc=ts if p == 7 else None)
                    ctx.wait("scalar", tk)
                    ev = ctx.op("scalar", lambda e, t=t, g=g, par=par: e.copy(
                        out=lnTv[:, 8 * g:8 * g + 8, 128 * t:128 * (t + 1)],
                        in_=ptr2[par][:].rearrange("p (k w) -> p k w", w=128)),
                        inc=ss)
                    ctx.wait("tensor", ev)
                return ev

            emit_ln_tiles(ctx, "l1ln", lambda t: (ctx.xres_ap(t), ctx.xres_tok),
                          NT, lambda t: lnb[:], trt, guard_lag=1)
        lnT_rdy = ctx.tok(ss)

        with ExitStack() as prs:
            psp = prs.enter_context(nc.psum_tensor("l1_psp", [64, 512], F32))
            psp2 = prs.enter_context(nc.psum_tensor("l1_psp2", [64, 512], F32))
            psv = [prs.enter_context(nc.psum_tensor(
                f"l1_psv{i}", [64, 128], F32)) for i in range(2)]
            ctx.wait("tensor", lnT_rdy)
            ctx.wait("tensor", w_rdy)
            evp = None
            for dst, wsrc in ((qf, wqc), (kf, wkc), (zb, wbc), (zd, wdc)):
                for half in range(2):
                    pp = psp if half == 0 else psp2
                    ctx.wait("tensor", evp)
                    mm = None
                    for k in range(16):
                        mm = ctx.op("tensor", lambda e, k=k, half=half, pp=pp,
                                    w=wsrc: e.matmul(
                            out=pp[:], lhsT=w[:, k * 64:(k + 1) * 64],
                            rhs=lnTv[:, k, half * 512:(half + 1) * 512],
                            start=(k == 0), stop=(k == 15)), inc=ts)
                    ctx.wait("scalar", mm)
                    evp = ctx.op("scalar", lambda e, dst=dst, half=half, pp=pp: e.copy(
                        out=dst[:, half * 512:(half + 1) * 512], in_=pp[:]),
                        inc=ss)
            for c in range(NC):
                pp = psv[c % 2]
                ctx.wait("tensor", evp if c < 2 else vev[c - 2])
                mm = None
                for k in range(16):
                    mm = ctx.op("tensor", lambda e, k=k, c=c, pp=pp: e.matmul(
                        out=pp[:], lhsT=lnTv[:, k, c * 64:(c + 1) * 64],
                        rhs=wvc[:, k * 128:(k + 1) * 128],
                        start=(k == 0), stop=(k == 15)), inc=ts)
                ctx.wait("scalar", mm)
                t_ = ctx.op("scalar", lambda e, c=c, pp=pp: e.copy(
                    out=v64[:, c * 128:(c + 1) * 128], in_=pp[:]), inc=ss)
                if c == 0:
                    vev = {}
                vev[c] = t_
            proj_done = ctx.tok(ss)

        ctx.wait("vector", proj_done)
        ctx.wait("vector", db1)
        ctx.wait("vector", db2)
        g1 = ctx.op("vector", lambda e: e.tensor_scalar(
            out=zb[:], in0=zb[:], scalar1=bbc[:], scalar2=None, op0=ALU.add),
            inc=vs)
        g2 = ctx.op("vector", lambda e: e.tensor_scalar(
            out=zd[:], in0=zd[:], scalar1=bdc[:], scalar2=None, op0=ALU.add),
            inc=vs)
        for zsrc, dst, gtok, isb in ((zb, lgb, g1, True), (zd, dlt, g2, False)):
            ctx.wait("scalar", gtok)
            a1 = ctx.op("scalar", lambda e, z=zsrc: e.activation(
                out=az[:], in_=z[:], func=AF.Abs), inc=ss)
            ctx.wait("scalar", a1)
            a2 = ctx.op("scalar", lambda e: e.activation(
                out=ex1[:], in_=az[:], func=AF.Exp, scale=-1.0), inc=ss)
            ctx.wait("scalar", a2)
            a3 = ctx.op("scalar", lambda e: e.activation(
                out=az[:], in_=ex1[:], func=AF.Ln, bias=ctx.onecol[0:64, :],
                scale=1.0), inc=ss)
            ctx.wait("vector", a3)
            m1 = ctx.op("vector", lambda e, z=zsrc, isb=isb: e.tensor_scalar(
                out=ex1[:], in0=z[:], scalar1=0.0, scalar2=None,
                op0=(ALU.min if isb else ALU.max)), inc=vs)
            ctx.wait("vector", m1)
            m2 = ctx.op("vector", lambda e, d=dst, isb=isb: e.tensor_tensor(
                out=d[:], in0=ex1[:], in1=az[:],
                op=(ALU.subtract if isb else ALU.add)), inc=vs)
            ctx.wait("scalar", m2)
        lgb_rdy = m2
        ctx.wait("vector", lgb_rdy)
        mz = ctx.op("vector", lambda e: e.memset(az[:], 1.0), inc=vs)
        ctx.wait("vector", mz)
        Ltok = ctx.op("vector", lambda e: e.tensor_tensor_scan(
            out=Lc[:], data0=az[:], data1=lgb[:], initial=0.0,
            op0=ALU.mult, op1=ALU.add), inc=vs)

        with ExitStack() as cas:
            pat = cas.enter_context(nc.psum_tensor("l1_pat", [64, 64], F32))
            ptm = cas.enter_context(nc.psum_tensor("l1_ptm", [64, 128], F32))
            ptk = cas.enter_context(nc.psum_tensor("l1_ptk", [64, 64], BF16))
            ctx.wait("vector", Ltok)
            prev = {}
            for c in range(NC):
                cs = slice(c * 64, (c + 1) * 64)
                ctx.wait("vector", prev.get(("p3", c - 1)))
                if c == 0:
                    l1_ = ctx.op("vector", lambda e, cs=cs: e.tensor_scalar(
                        out=Lrel[:], in0=Lc[:, cs], scalar1=0.0, scalar2=None,
                        op0=ALU.subtract), inc=vs)
                else:
                    l1_ = ctx.op("vector", lambda e, cs=cs, c=c: e.tensor_scalar(
                        out=Lrel[:], in0=Lc[:, cs],
                        scalar1=Lc[:, c * 64 - 1:c * 64], scalar2=None,
                        op0=ALU.subtract), inc=vs)
                ctx.wait("scalar", l1_)
                ctx.wait("scalar", prev.get(("dve", c - 1)))
                p1 = ctx.op("scalar", lambda e: e.activation(
                    out=Pfm[:], in_=Lrel[:], func=AF.Exp), inc=ss)
                p2 = ctx.op("scalar", lambda e: e.activation(
                    out=Png[:], in_=Lrel[:], func=AF.Exp, scale=-1.0), inc=ss)
                p3 = ctx.op("scalar", lambda e: e.activation(
                    out=Pr2[:], in_=Lrel[:], func=AF.Exp, scale=-1.0,
                    bias=Lrel[:, 63:64]), inc=ss)
                prev[("p3", c)] = p3
                ctx.wait("vector", p3)
                kd1 = ctx.op("vector", lambda e, cs=cs: e.tensor_tensor(
                    out=kd_[:], in0=kf[:, cs], in1=dlt[:, cs], op=ALU.mult),
                    inc=vs)
                q1_ = ctx.op("vector", lambda e, cs=cs: e.tensor_tensor(
                    out=qtb[:], in0=qf[:, cs], in1=Pfm[:], op=ALU.mult), inc=vs)
                ctx.wait("vector", q1_)
                q2_ = ctx.op("vector", lambda e, cs=cs, c=c: e.tensor_copy(
                    out=qst[:, c * 64:(c + 1) * 64], in_=qtb[:]), inc=vs)
                ctx.wait("vector", kd1)
                k1_ = ctx.op("vector", lambda e: e.tensor_tensor(
                    out=ktb[:], in0=kd_[:], in1=Png[:], op=ALU.mult), inc=vs)
                ctx.wait("vector", prev.get(("k2ev", c - 1)))
                k2_ = ctx.op("vector", lambda e: e.tensor_tensor(
                    out=k2b[:], in0=kd_[:], in1=Pr2[:], op=ALU.mult), inc=vs)
                pcw = ctx.op("vector", lambda e, c=c: e.tensor_copy(
                    out=pcs[:, c:c + 1], in_=Pfm[:, 63:64]), inc=vs)
                prev[("pcw", c)] = pcw
                ctx.wait("tensor", k1_)
                ctx.wait("tensor", q1_)
                ctx.wait("tensor", prev.get(("atev", c - 1)))
                at_ = ctx.op("tensor", lambda e: e.matmul(
                    out=pat[:], lhsT=ktb[:], rhs=qtb[:], start=True, stop=True),
                    inc=ts)
                ctx.wait("vector", at_)
                ctx.wait("vector", dtr)
                atev = ctx.op("vector", lambda e, c=c: e.tensor_tensor(
                    out=atst[:, c * 64:(c + 1) * 64], in0=pat[:], in1=trim[:],
                    op=ALU.mult), inc=vs)
                prev[("atev", c)] = atev
                ctx.wait("tensor", k2_)
                ctx.wait("tensor", prev.get(("k2ev", c - 1)))
                tk2 = ctx.op("tensor", lambda e: e.transpose(
                    out=ptk[:], in_=k2b[:], identity=ctx.identb[0:64, 0:64]),
                    inc=ts)
                ctx.wait("scalar", tk2)
                k2ev = ctx.op("scalar", lambda e: e.copy(
                    out=k2t[:], in_=ptk[:]), inc=ss)
                prev[("k2ev", c)] = k2ev
                prev[("qtb", c)] = k2ev
                ctx.wait("tensor", k2ev)
                ctx.wait("tensor", proj_done)
                ctx.wait("tensor", prev.get(("tmev", c - 1)))
                tm_ = ctx.op("tensor", lambda e, c=c: e.matmul(
                    out=ptm[:], lhsT=k2t[:], rhs=v64[:, c * 128:(c + 1) * 128],
                    start=True, stop=True), inc=ts)
                ctx.wait("scalar", tm_)
                tmev = ctx.op("scalar", lambda e, c=c: e.copy(
                    out=tmat[:, c * 128:(c + 1) * 128], in_=ptm[:]), inc=ss)
                prev[("tmev", c)] = tmev
                ctx.wait("vector", tmev)
                ctx.wait("vector", prev.get(("pcw", c)))
                if c == 0:
                    sz = ctx.op("vector", lambda e: e.memset(Sa[:], 0.0), inc=vs)
                    ctx.wait("vector", sz)
                s1_ = ctx.op("vector", lambda e, c=c: e.tensor_scalar(
                    out=Sa[:], in0=Sa[:], scalar1=pcs[:, c:c + 1], scalar2=None,
                    op0=ALU.mult), inc=vs)
                ctx.wait("vector", s1_)
                s2_ = ctx.op("vector", lambda e, c=c: e.tensor_tensor(
                    out=Sa[:], in0=Sa[:], in1=tmat[:, c * 128:(c + 1) * 128],
                    op=ALU.add), inc=vs)
                ctx.wait("vector", s2_)
                prev[("dve", c)] = s2_
            pm = ctx.op("vector", lambda e: e.tensor_copy(
                out=pay[:, 0:128], in_=Sa[:]), inc=vs)
            ctx.wait("scalar", Ltok)
            pe_ = ctx.op("scalar", lambda e: e.activation(
                out=pay[:, 128:129], in_=Lc[:, 1023:1024], func=AF.Exp), inc=ss)
            ctx.wait("scalar", pm)
            ctx.wait("scalar", pe_)
            dpay = ctx.dma("scalar", ctx.cc_l1_in.ap(), pay[:], gs)
            ctx.wait("gpsimd", dpay)
            cc = ctx.op("gpsimd", lambda e: e.collective_compute(
                "AllGather", ALU.bypass,
                replica_groups=[[0, 1, 2, 3], [4, 5, 6, 7]],
                ins=[ctx.cc_l1_in.ap()], outs=[ctx.cc_l1_out.ap()]),
                inc="g_cc2", inc_n=1)
            ctx.wait("scalar", cc)
            dg_toks = []
            for j in range(3):
                dg_toks.append(ctx.dma("scalar", gath[:, j * 129:(j + 1) * 129],
                               ctx.cc_l1_out.ap()[j * 64:(j + 1) * 64, :], gs))
            ctx.waits("vector", dg_toks)
            ctx.wait("vector", du)
            sz2 = ctx.op("vector", lambda e: e.memset(Sa[:], 0.0), inc=vs)
            ctx.wait("vector", sz2)
            for j in range(3):
                a1_ = ctx.op("vector", lambda e, j=j: e.scalar_tensor_tensor(
                    out=aj[:], in0=gath[:, j * 129 + 128:j * 129 + 129],
                    scalar=usec[:, j:j + 1], in1=usec[:, 3 + j:4 + j],
                    op0=ALU.mult, op1=ALU.add), inc=vs)
                m1_ = ctx.op("vector", lambda e, j=j: e.tensor_scalar(
                    out=mpj[:], in0=gath[:, j * 129:j * 129 + 128],
                    scalar1=usec[:, j:j + 1], scalar2=None, op0=ALU.mult),
                    inc=vs)
                ctx.wait("vector", m1_)
                s1_ = ctx.op("vector", lambda e: e.tensor_scalar(
                    out=Sa[:], in0=Sa[:], scalar1=aj[:], scalar2=None,
                    op0=ALU.mult), inc=vs)
                ctx.wait("vector", s1_)
                s2_ = ctx.op("vector", lambda e: e.tensor_tensor(
                    out=Sa[:], in0=Sa[:], in1=mpj[:], op=ALU.add), inc=vs)
                ctx.wait("vector", s2_)
            combine_done = s2_

            pso = cas.enter_context(nc.psum_tensor("l1_pso", [64, 128], F32))
            pto = cas.enter_context(nc.psum_tensor("l1_pto", [128, 64], BF16))
            obev = None
            ctx.wait("vector", prev.get(("pcw", NC - 1)))
            ctx.wait("scalar", combine_done)
            for c in range(NC):
                sb_ = ctx.op("scalar", lambda e: e.copy(
                    out=Sbf[:], in_=Sa[:]), inc=ss)
                ctx.wait("tensor", sb_)
                ctx.wait("tensor", obev)
                o1 = ctx.op("tensor", lambda e, c=c: e.matmul(
                    out=pso[:], lhsT=atst[:, c * 64:(c + 1) * 64],
                    rhs=v64[:, c * 128:(c + 1) * 128], start=True, stop=False),
                    inc=ts)
                o2 = ctx.op("tensor", lambda e, c=c: e.matmul(
                    out=pso[:], lhsT=qst[:, c * 64:(c + 1) * 64], rhs=Sbf[:],
                    start=False, stop=True), inc=ts)
                ctx.wait("scalar", o2)
                oev = ctx.op("scalar", lambda e: e.copy(
                    out=obf[:], in_=pso[:]), inc=ss)
                ctx.wait("tensor", oev)
                to_ = ctx.op("tensor", lambda e: e.transpose(
                    out=pto[:, 0:64], in_=obf[:],
                    identity=ctx.identb[0:64, 0:64]), inc=ts)
                ctx.wait("scalar", to_)
                obev = ctx.op("scalar", lambda e, c=c: e.copy(
                    out=ofm[:, c * 64:(c + 1) * 64], in_=pto[:, 0:64]), inc=ss)
                ctx.wait("vector", o2)
                s1_ = ctx.op("vector", lambda e, c=c: e.tensor_scalar(
                    out=Sa[:], in0=Sa[:], scalar1=pcs[:, c:c + 1], scalar2=None,
                    op0=ALU.mult), inc=vs)
                ctx.wait("vector", s1_)
                s2_ = ctx.op("vector", lambda e, c=c: e.tensor_tensor(
                    out=Sa[:], in0=Sa[:], in1=tmat[:, c * 128:(c + 1) * 128],
                    op=ALU.add), inc=vs)
                ctx.wait("scalar", s2_)
            ofm_rdy = ctx.tok(ss)

        with ExitStack() as pjs:
            ctx.scope = pjs
            ppj = [pjs.enter_context(nc.psum_tensor(
                f"l1_ppj{i}", [128, 512], F32)) for i in range(2)]
            ev = None
            fguard = {}
            for t in range(NT):
                fpar = t % 2
                for dcw in range(4):
                    n = t * 4 + dcw
                    par = n % 2
                    ctx.wait("tensor", ofm_rdy)
                    ctx.wait("tensor", ev)
                    mm = ctx.op("tensor", lambda e, t=t, dcw=dcw, par=par: e.matmul(
                        out=ppj[par][:], lhsT=ofm[:, t * 128:(t + 1) * 128],
                        rhs=woc2[:, dcw * 512:(dcw + 1) * 512],
                        start=True, stop=True), inc=ts)
                    ctx.wait("scalar", mm)
                    if dcw == 0:
                        ctx.wait("scalar", fguard.get(t - 2))
                    ev = ctx.op("scalar", lambda e, dcw=dcw, fpar=fpar, par=par: e.copy(
                        out=fat[fpar][:, dcw * 512:(dcw + 1) * 512],
                        in_=ppj[par][:]), inc=ss)
                rt = emit_postln_residual(ctx, f"l1_po{t}",
                                          [(fat[fpar][:], ev, t)],
                                          ctx.xres_ap, aptmp)
                fguard[t] = rt[-1]
                res_toks.extend(rt)
            ctx.xres_tok = res_toks[-1]
        ctx.scope = old_scope
    return res_toks


def build(blocks):
    nc = bass.Bass("TRN2", target_bir_lowering=False)
    first = blocks[0] == "l0a"
    need_q = any(b in blocks for b in ("l0f", "l1f"))
    qwhich = [w for w, b in (("l0", "l0f"), ("l1", "l1f")) if b in blocks]

    with ExitStack() as stack:
        ctx = Ctx(nc, stack)
        ctx.dbg_toks = []
        dr = lambda n, s, d=F32, k="ExternalInput": nc.dram_tensor(n, s, d, kind=k).ap()

        if first:
            ctx.in_x = dr("x_in", [HALO + NTOK, D])
        else:
            ctx.in_x = dr("x_in", [NTOK, D])
        ctx.out_x = dr("x_out", [NTOK, D], F32, "ExternalOutput")
        ctx.in_ident = dr("ident", [128, 128], BF16)
        if "l0a" in blocks:
            ctx.in_wq = dr("wq_t", [16, 128, 2048])
            ctx.in_wkv = dr("wkv_t", [8, 128, 2048])
            ctx.in_wout = dr("wout_t", [16, 128, 2048])
            ctx.in_cos = dr("cos_t", [ATT * 128, 64])
            ctx.in_sin = dr("sin_t", [ATT * 128, 64])
            ctx.in_qnw = dr("qnw_t", [1, 2048])
            ctx.in_knw = dr("knw_t", [1, 512])
            ctx.in_masks = dr("masks_t", [NT, 128, 640], BF16)
            ctx.hbm_qtmp = nc.dram_tensor("hbm_qtmp", [NT, 128, 2048], BF16).ap()
        if "l1a" in blocks:
            ctx.in_l1wq = dr("l1wq", [128, 1024])
            ctx.in_l1wk = dr("l1wk", [128, 1024])
            ctx.in_l1wb = dr("l1wb", [128, 1024])
            ctx.in_l1wd = dr("l1wd", [128, 1024])
            ctx.in_l1wv = dr("l1wv", [128, 2048])
            ctx.in_l1wo = dr("l1wo", [128, 2048])
            ctx.in_bb = dr("l1bb", [64])
            ctx.in_bd = dr("l1bd", [64])
            ctx.in_trim = dr("l1trim", [64, 64], BF16)
            ctx.in_l1use = dr("l1use", [64, 6])
            ctx.cc_l1_in = nc.dram_tensor("cc_l1_in", [64, 129], F32)
            ctx.cc_l1_out = nc.dram_tensor("cc_l1_out", [256, 129], F32)
        if need_q:
            ctx.in_qsl = dr("qslices", [21, 128, 4096])
            ctx.in_qsrat = dr("qsrat", [1, 16])
            ctx.in_qseps = dr("qseps", [1, 16])
            ctx.cc_qs_in = nc.dram_tensor("cc_qs_in", [1, 16], F32)
            ctx.cc_qs_out = nc.dram_tensor("cc_qs_out", [1, 16], F32)
        if "l0f" in blocks:
            ctx.in_wup0 = dr("wup0s", [8, 128, 2048])
            ctx.in_wgt0 = dr("wgt0s", [8, 128, 2048])
            ctx.in_wdn0 = dr("wdn0s", [4, 4, 128, 1024])
            ctx.hbm_hid0 = nc.dram_tensor("hbm_hid0", [3, EC0 // 4, 128, 1024], BF16).ap()
            ctx.ag0_in = nc.dram_tensor("ag0_in", [24, 128, 2048], BF16)
            ctx.ag0_out = nc.dram_tensor("ag0_out", [192, 128, 2048], BF16,
                                         addr_space="Shared")
        if "l1f" in blocks:
            ctx.in_wup1 = dr("wup1s", [6, 128, 2048])
            ctx.in_wgt1 = dr("wgt1s", [6, 128, 2048])
            ctx.in_wdn1 = dr("wdn1s", [3, 4, 128, 1024])
            ctx.hbm_hid1 = nc.dram_tensor("hbm_hid1", [3, EC1 // 4, 128, 1024], BF16).ap()
            ctx.ag1_in = nc.dram_tensor("ag1_in", [18, 128, 2048], BF16)
            ctx.ag1_out = nc.dram_tensor("ag1_out", [144, 128, 2048], BF16,
                                         addr_space="Shared")

        xres = ctx.sb("xres", [128, NT * 2048])
        ctx.xres_ap = lambda t: xres[:, t * 2048:(t + 1) * 2048]
        ctx.identb = ctx.sb("identb", [128, 128], BF16)
        ctx.eps5 = ctx.sb("eps5", [128, 1])
        ctx.eps6 = ctx.sb("eps6", [128, 1])
        ctx.onecol = ctx.sb("onecol", [128, 1])
        ctx.ones128c = ctx.sb("ones128c", [128, 1])
        ctx.ones1x128 = ctx.sb("ones1x128", [1, 128])
        if need_q:
            ctx.in_qsrat_sb = ctx.sb("qsrat_sb", [1, 16])
            ctx.in_qseps_sb = ctx.sb("qseps_sb", [1, 16])

        ctok = None
        for tens, v in ((ctx.eps5, 1e-5), (ctx.eps6, 1e-6), (ctx.onecol, 1.0),
                        (ctx.ones128c, 1.0), (ctx.ones1x128, 1.0)):
            ctok = ctx.op("vector", lambda e, tens=tens, v=v: e.memset(tens[:], v),
                          inc="sV")
        ctx.consts_tok = ctok
        itok = ctx.dma("scalar", ctx.identb[:], ctx.in_ident[:], "SS")
        xoff = HALO if first else 0
        for t in range(NT):
            ctx.dma("scalar", ctx.xres_ap(t),
                    ctx.in_x[xoff + t * 128: xoff + (t + 1) * 128, :],
                    "SS")
        ctx.xres_tok = ctx.stream_toks("SS")
        ctx.ident_tok = itok
        for e in ("vector", "scalar", "tensor"):
            ctx.wait(e, ctx.consts_tok)
            ctx.wait(e, itok)
        if need_q:
            ctx.thscols = ctx.sb("qs_ths", [128, 16])
            ctx.qs_acc = ctx.sb("qs_acc", [128, 8])
            ctx.qs_red = ctx.sb("qs_red", [128, 1])
            ctx.qs_cr = ctx.sb("qs_cr", [1, 8])
            ctx.qs_tot = ctx.sb("qs_tot", [1, 16])
            q1 = ctx.dma("scalar", ctx.in_qsrat_sb[:], ctx.in_qsrat[:], "SS")
            q2 = ctx.dma("scalar", ctx.in_qseps_sb[:], ctx.in_qseps[:], "SS")
            ctx.qsrat_tok, ctx.qseps_tok = q1, q2
            lis = [li for li, bb in ((0, "l0f"), (1, "l1f")) if bb in blocks]
            gsc = gen_quant_scales(ctx, qwhich)
            gsh = gen_shard_all(ctx, lis)
            if "l0a" in blocks:
                ctx.gens["scales"] = gsc
                ctx.gens["shard"] = gsh
            else:
                qs_stack = ExitStack()
                old_scope, ctx.scope = ctx.scope, qs_stack
                for _ in gsc:
                    pass
                for _ in gsh:
                    pass
                ctx.scope = old_scope
                qs_stack.close()
                ctx.barrier_all()

        final_toks = [ctx.xres_tok]
        for b in blocks:
            if b == "l0a":
                final_toks = emit_l0_attn(ctx)
                assert not any(ctx.gens.values()), "prologue gens not drained"
            elif b == "l0f":
                final_toks = emit_ffn(ctx, 0)
            elif b == "l1a":
                final_toks = emit_l1_attn(ctx)
            elif b == "l1f":
                final_toks = emit_ffn(ctx, 1)
            ctx.xres_tok = final_toks[-1]
            ctx.barrier_all()

        ctx.waits("scalar", final_toks)
        ctx.waits("scalar", ctx.dbg_toks)
        for t in range(NT):
            ctx.dma("scalar", ctx.out_x[t * 128:(t + 1) * 128, :],
                    ctx.xres_ap(t), "SS")
        ctx.waits("scalar", ctx.stream_toks("SS"))

        block = stack.enter_context(nc.Block())
        ctx.run(block)
    return nc


def _pret_upgate(W, EC, ECpad):
    H = W.reshape(EC, 128, 16, 128).transpose(0, 3, 2, 1)
    H = np.ascontiguousarray(H.reshape(EC, 128, 2048))
    if ECpad > EC:
        H = np.concatenate(
            [H, np.zeros((ECpad - EC, 128, 2048), np.float32)], 0)
    return H


def _pret_down(Wd, EC, E2pad):
    H = Wd.reshape(4, 512, EC, 128).transpose(0, 2, 3, 1)
    H = H.reshape(4, EC // 2, 2, 128, 512).transpose(1, 0, 3, 2, 4)
    H = np.ascontiguousarray(H.reshape(EC // 2, 4, 128, 1024))
    if E2pad > EC // 2:
        H = np.concatenate(
            [H, np.zeros((E2pad - EC // 2, 4, 128, 1024), np.float32)], 0)
    return H


def _qslices(core, mats):
    out = np.zeros((21, 128, 4096), np.float32)
    idx = 0
    for arr, npc in mats:
        rows = arr.shape[0] // NCORES
        sl = np.ascontiguousarray(arr[core * rows:(core + 1) * rows]).reshape(-1)
        buf = np.zeros(npc * 128 * 4096, np.float32)
        buf[:sl.size] = sl
        out[idx:idx + npc] = buf.reshape(npc, 128, 4096)
        idx += npc
    return out


def host_prep(inputs):
    wqkv = np.asarray(inputs["L0_attn_w_qkv"], np.float32)
    G = np.ascontiguousarray(wqkv.reshape(3072, 16, 128).transpose(2, 1, 0))
    hA = [h for h in range(NH) if (h // 4) % 2 == 0]
    hB = [h for h in range(NH) if (h // 4) % 2 == 1]
    perm = []
    for p in range(16):
        perm.extend(range(hA[p] * 64, hA[p] * 64 + 64))
        perm.extend(range(hB[p] * 64, hB[p] * 64 + 64))
    wq_t = np.ascontiguousarray(
        G[:, :, :2048][:, :, perm].reshape(128, 16, 16, 128)
        .transpose(2, 0, 1, 3).reshape(16, 128, 2048))
    wkv_t = np.ascontiguousarray(
        G[:, :, 2048:].reshape(128, 16, 8, 128).transpose(2, 0, 1, 3)
        .reshape(8, 128, 2048))
    wout_t = np.ascontiguousarray(
        np.asarray(inputs["L0_attn_w_out"], np.float32).T.reshape(16, 128, 2048))
    qnw_t = np.tile(np.asarray(inputs["L0_attn_q_norm_w"], np.float32), 32)[None, :]
    knw_t = np.tile(np.asarray(inputs["L0_attn_k_norm_w"], np.float32), 8)[None, :]
    inv = 1.0 / 10000.0 ** (np.arange(0, 64, 2, dtype=np.float32) / 64.0)
    wup0 = _pret_upgate(np.asarray(inputs["L0_ffn_W_up"], np.float32), EC0, 64)
    wgt0 = _pret_upgate(np.asarray(inputs["L0_ffn_W_gate"], np.float32), EC0, 64)
    wdn0 = _pret_down(np.asarray(inputs["L0_ffn_W_down"], np.float32), EC0, 32)
    wup1 = _pret_upgate(np.asarray(inputs["L1_ffn_w1"], np.float32), EC1, 48)
    wgt1 = _pret_upgate(np.asarray(inputs["L1_ffn_w2"], np.float32), EC1, 48)
    wdn1 = _pret_down(np.asarray(inputs["L1_ffn_w3"], np.float32), EC1, 24)
    ident = np.eye(128, dtype=BF)

    rat = np.zeros((1, 16), np.float32)
    epsr = np.zeros((1, 16), np.float32)
    for m, nel in ((0, E0 * D), (1, E0 * D), (2, E0 * D),
                   (3, E1 * D), (4, E1 * D), (5, E1 * D)):
        rat[0, m] = 0.5 / nel
        rat[0, 8 + m] = 1.0 / nel
        epsr[0, m] = 0.5e-8 / NCORES
        epsr[0, 8 + m] = 1e-8 / NCORES

    qs_mats = [(np.asarray(inputs["L0_ffn_W_up"], np.float32), 4),
               (np.asarray(inputs["L0_ffn_W_gate"], np.float32), 4),
               (np.asarray(inputs["L0_ffn_W_down"], np.float32), 4),
               (np.asarray(inputs["L1_ffn_w1"], np.float32), 3),
               (np.asarray(inputs["L1_ffn_w2"], np.float32), 3),
               (np.asarray(inputs["L1_ffn_w3"], np.float32), 3)]

    def _ptile(W, cols):
        return np.ascontiguousarray(
            W.reshape(16, 128, cols).transpose(1, 0, 2).reshape(128, 16 * cols))

    l1wq = _ptile(np.asarray(inputs["L1_attn_Wq"], np.float32), 64)
    l1wk = _ptile(np.asarray(inputs["L1_attn_Wk"], np.float32), 64)
    l1wb = _ptile(np.asarray(inputs["L1_attn_Wb"], np.float32), 64)
    l1wd = _ptile(np.asarray(inputs["L1_attn_Wd"], np.float32), 64)
    l1wv = _ptile(np.asarray(inputs["L1_attn_Wv"], np.float32), 128)
    l1wo = np.ascontiguousarray(np.asarray(inputs["L1_attn_Wo"], np.float32))
    trim = (np.arange(64)[:, None] <= np.arange(64)[None, :]).astype(BF)

    percore = []
    x = np.asarray(inputs["L0_attn_x"], np.float32)
    for c in range(NCORES):
        b, qt = c // 4, c % 4
        xcore = np.zeros((HALO + NTOK, D), np.float32)
        lo = qt * NTOK - HALO
        src_lo = max(lo, 0)
        xcore[src_lo - lo:, :] = x[b, src_lo: qt * NTOK + NTOK, :]
        pos = np.arange(qt * NTOK - HALO, qt * NTOK + NTOK, dtype=np.float32)
        ang = pos[:, None] * inv[None, :]
        cos_t = np.ascontiguousarray(
            np.concatenate([np.cos(ang), np.cos(ang)], 1).astype(np.float32))
        sin_t = np.ascontiguousarray(
            np.concatenate([-np.sin(ang), np.sin(ang)], 1).astype(np.float32))
        masks = np.full((NT, 128, 640), NEG, np.float32)
        for i in range(NT):
            r = np.arange(128)[:, None]
            cc = np.arange(640)[None, :]
            keyg = qt * NTOK - HALO + 128 * i + cc
            valid = (cc >= r + 1) & (cc <= r + 512) & (keyg >= 0)
            masks[i][valid.squeeze() if valid.ndim > 2 else valid] = 0.0
        percore.append({
            "x_first": xcore,
            "ident": ident,
            "l1wq": l1wq, "l1wk": l1wk, "l1wb": l1wb, "l1wd": l1wd,
            "l1wv": l1wv, "l1wo": l1wo, "l1trim": trim,
            "l1bb": np.asarray(inputs["L1_attn_bb"], np.float32),
            "l1bd": np.asarray(inputs["L1_attn_bd"], np.float32),
            "l1use": np.concatenate([
                (np.arange(3) < qt).astype(np.float32),
                1.0 - (np.arange(3) < qt).astype(np.float32)])[None, :]
                .repeat(64, 0).copy(),
            "wq_t": wq_t, "wkv_t": wkv_t, "wout_t": wout_t,
            "qnw_t": qnw_t, "knw_t": knw_t,
            "cos_t": cos_t, "sin_t": sin_t,
            "masks_t": masks.astype(BF),
            "qslices": _qslices(c, qs_mats),
            "qsrat": rat, "qseps": epsr,
            "wup0s": np.ascontiguousarray(wup0[c * 8:(c + 1) * 8]),
            "wgt0s": np.ascontiguousarray(wgt0[c * 8:(c + 1) * 8]),
            "wdn0s": np.ascontiguousarray(wdn0[c * 4:(c + 1) * 4]),
            "wup1s": np.ascontiguousarray(wup1[c * 6:(c + 1) * 6]),
            "wgt1s": np.ascontiguousarray(wgt1[c * 6:(c + 1) * 6]),
            "wdn1s": np.ascontiguousarray(wdn1[c * 3:(c + 1) * 3]),
        })
    return percore


def in_map_for(blocks, percore, x_cur):
    first = blocks[0] == "l0a"
    need_q = any(b in blocks for b in ("l0f", "l1f"))
    maps = []
    for c in range(NCORES):
        b, qt = c // 4, c % 4
        m = {"ident": percore[c]["ident"]}
        if "l0a" in blocks:
            for k in ("wq_t", "wkv_t", "wout_t", "qnw_t", "knw_t", "cos_t",
                      "sin_t", "masks_t"):
                m[k] = percore[c][k]
        if first:
            m["x_in"] = percore[c]["x_first"]
        else:
            m["x_in"] = np.ascontiguousarray(x_cur[b, qt * NTOK:(qt + 1) * NTOK, :])
        if need_q:
            m["qslices"] = percore[c]["qslices"]
            m["qsrat"] = percore[c]["qsrat"]
            m["qseps"] = percore[c]["qseps"]
        if "l1a" in blocks:
            for k in ("l1wq", "l1wk", "l1wb", "l1wd", "l1wv", "l1wo",
                      "l1bb", "l1bd", "l1trim", "l1use"):
                m[k] = percore[c][k]
        if "l0f" in blocks:
            for k in ("wup0s", "wgt0s", "wdn0s"):
                m[k] = percore[c][k]
        if "l1f" in blocks:
            for k in ("wup1s", "wgt1s", "wdn1s"):
                m[k] = percore[c][k]
        maps.append(m)
    return maps


def gather_out(res):
    out = np.zeros((B, S, D), np.float32)
    for c in range(NCORES):
        b, qt = c // 4, c % 4
        out[b, qt * NTOK:(qt + 1) * NTOK, :] = res.results[c]["x_out"]
    return out


def kernel(**inputs):
    percore = host_prep(inputs)
    x_cur = np.asarray(inputs["L0_attn_x"], np.float32)
    launches = [ALL_BLOCKS] if FUSED else UNFUSED
    for blocks in launches:
        nc = build(tuple(blocks))
        maps = in_map_for(blocks, percore, x_cur)
        res = run_bass_kernel_spmd(nc, maps, core_ids=list(range(NCORES)))
        x_cur = gather_out(res)
    return x_cur


# version 19
# speedup vs baseline: 1.0062x; 1.0062x over previous
import numpy as np
import ml_dtypes
from contextlib import ExitStack

import concourse.bass as bass
import concourse.mybir as mybir
from concourse.bass_utils import run_bass_kernel_spmd

F32 = mybir.dt.float32
BF16 = mybir.dt.bfloat16
AF = mybir.ActivationFunctionType
ALU = mybir.AluOpType
AX = mybir.AxisListType
BF = ml_dtypes.bfloat16

B, S, D = 2, 4096, 2048
NCORES = 8
NTOK = 1024
HALO = 512
NT = NTOK // 128
NHT = HALO // 128
ATT = NT + NHT
NH, NKV, HD = 32, 8, 64
E0, E1 = 8192, 5632
EC0, EC1 = E0 // 128, E1 // 128
KD = 16
NEG = -1.0e30

ENGS = ("sync", "vector", "scalar", "tensor", "gpsimd")

DEBUG_DUMP = False
ALL_BLOCKS = ("l0a", "l0f", "l1a", "l1f")
UNFUSED = [("l0a",), ("l0f",), ("l1a",), ("l1f",)]
FUSED = True


class Ctx:
    def __init__(self, nc, stack):
        self.nc = nc
        self.stack = stack
        self.scope = stack
        self.ops = {e: [] for e in ENGS}
        self._slots = {}
        self.handles = {}
        self.cnt = {}
        self.seen = {e: {} for e in ENGS}
        self.gens = {}
        self.q_stash = {}

    def _sem(self, name):
        if name not in self.handles:
            self.handles[name] = self.stack.enter_context(self.nc.semaphore(name))
            self.cnt[name] = 0
        return self.handles[name]

    def op(self, eng, fn, inc=None, inc_n=1):
        if inc is None:
            self.ops[eng].append(fn)
            return None
        h = self._sem(inc)
        self.cnt[inc] += inc_n
        tgt = self.cnt[inc]

        def wrapped(e, fn=fn, h=h, n=inc_n):
            fn(e).then_inc(h, n)

        self.ops[eng].append(wrapped)
        return (inc, tgt)

    def dma(self, eng, out, in_, stream, slots=4, **kw):
        st = self._slots.setdefault(stream, {"i": 0, "prev": {}})
        i = st["i"] % slots
        st["i"] += 1
        sem = f"{stream}.{i}"
        self.wait(eng, st["prev"].get(i))
        tok = self.op(eng, lambda e: e.dma_start(out=out, in_=in_, **kw), sem, 16)
        st["prev"][i] = tok
        return tok

    def stream_toks(self, stream):
        st = self._slots.get(stream)
        return list(st["prev"].values()) if st else []

    def wait(self, eng, token):
        if token is None:
            return
        if isinstance(token, (list, tuple)) and token and isinstance(token[0], tuple):
            for t in token:
                self.wait(eng, t)
            return
        sem, v = token
        if self.seen[eng].get(sem, 0) >= v:
            return
        self.seen[eng][sem] = v
        h = self._sem(sem)
        self.ops[eng].append(lambda e, h=h, v=v: e.wait_ge(h, v))

    def waits(self, eng, tokens):
        for t in tokens:
            self.wait(eng, t)

    def tok(self, sem):
        return (sem, self.cnt.get(sem, 0))

    def barrier_all(self):
        toks = [(s, self.cnt[s]) for s in list(self.cnt)
                if not (s.startswith("g_") or s.startswith("QO."))]
        for e in ENGS:
            for tk in toks:
                self.wait(e, tk)

    def pump(self, name, n=1):
        g = self.gens.get(name)
        if g is None:
            return
        try:
            for _ in range(n):
                next(g)
        except StopIteration:
            self.gens[name] = None

    def drain(self, name):
        g = self.gens.get(name)
        if g is not None:
            for _ in g:
                pass
            self.gens[name] = None

    def dump(self, name, ap, token, eng="scalar"):
        if not DEBUG_DUMP:
            return None
        t = self.nc.dram_tensor("dbg_" + name, list(ap.shape), ap.dtype,
                                kind="ExternalOutput").ap()
        self.wait(eng, token)
        dtok = self.dma(eng, t[:], ap, "SS")
        self.dbg_toks.append(dtok)
        return dtok

    def sb(self, name, shape, dt=F32):
        return self.scope.enter_context(self.nc.sbuf_tensor(name, shape, dt))

    def ps(self, name, shape, dt=F32):
        return self.scope.enter_context(self.nc.psum_tensor(name, shape, dt))

    def run(self, block):
        for eng in ENGS:
            ops = self.ops[eng]

            def section(e, ops=ops):
                for f in ops:
                    f(e)

            getattr(block, eng)(section)


def emit_ln_tiles(ctx, name, xsrc, n, apply_out, per_tile=None, guard_lag=2,
                  pre_apply=None):
    bnb = ctx.sb(f"{name}_bnb", [128, 24 * n])
    mv = ctx.sb(f"{name}_mv", [128, 2 * n])
    rst = ctx.sb(f"{name}_rst", [128, n])
    vsem, ssem = "sV", "sS"
    tok = None
    for t in range(n):
        ap, rdy = xsrc(t)
        ctx.wait("vector", rdy)
        for i in range(4):
            tok = ctx.op("vector", lambda e, ap=ap, t=t, i=i: e.bn_stats(
                out=bnb[:, t * 24 + i * 6: t * 24 + (i + 1) * 6],
                in_=ap[:, i * 512:(i + 1) * 512]), inc=vsem)
    ctx.wait("vector", tok)
    for t in range(n):
        tok = ctx.op("vector", lambda e, t=t: e.bn_aggr(
            out=mv[:, 2 * t:2 * t + 2], in_=bnb[:, 24 * t:24 * t + 24]), inc=vsem)
    ctx.wait("scalar", tok)
    varv = mv[:].rearrange("p (t two) -> p t two", two=2)[:, :, 1]
    t1 = ctx.op("scalar", lambda e: e.activation(
        out=rst[:], in_=varv, func=AF.Ln, bias=ctx.eps5[:], scale=1.0), inc=ssem)
    ctx.wait("scalar", t1)
    t2 = ctx.op("scalar", lambda e: e.activation(
        out=rst[:], in_=rst[:], func=AF.Exp, scale=-0.5), inc=ssem)
    ctx.wait("vector", t2)
    toks, guards = [], {}
    for t in range(n):
        ap, _ = xsrc(t)
        if t == 0 and pre_apply is not None:
            ctx.wait("vector", pre_apply)
        if t - guard_lag in guards:
            ctx.wait("vector", guards[t - guard_lag])
        ot = apply_out(t)
        atok = ctx.op("vector", lambda e, ap=ap, ot=ot, t=t: e.tensor_scalar(
            out=ot, in0=ap, scalar1=mv[:, 2 * t:2 * t + 1],
            scalar2=rst[:, t:t + 1], op0=ALU.subtract, op1=ALU.mult), inc=vsem)
        toks.append(atok)
        if per_tile is not None:
            guards[t] = per_tile(t, atok)
    return toks


def emit_postln_residual(ctx, name, fsrc_list, xres_ap, tmp):
    done = []

    def per_tile(t, atok):
        xt = xres_ap(fsrc_list[t][2])
        ctx.wait("vector", atok)
        rtok = ctx.op("vector", lambda e, xt=xt: e.tensor_tensor(
            out=xt, in0=xt, in1=tmp[:], op=ALU.add), inc="sV")
        done.append(rtok)
        return rtok

    emit_ln_tiles(ctx, name, lambda t: fsrc_list[t][:2], len(fsrc_list),
                  lambda t: tmp[:], per_tile, guard_lag=1)
    return done


def gen_quant_scales(ctx, which):
    P = 3
    WS_ = 1024
    buf = [ctx.sb(f"qs_buf{i}", [128, WS_]) for i in range(P)]
    acc, red, cr, tot = ctx.qs_acc, ctx.qs_red, ctx.qs_cr, ctx.qs_tot
    z = ctx.op("vector", lambda e: e.memset(acc[:], 0.0), inc="sV")
    base = {0: 0, 1: 4, 2: 8, 3: 12, 4: 15, 5: 18}
    pieces = []
    if "l0" in which:
        pieces += [(m, i) for m in (0, 1, 2) for i in range(4)]
    if "l1" in which:
        pieces += [(m, i) for m in (3, 4, 5) for i in range(3)]
    subs = [(m, i, s) for (m, i) in pieces for s in range(4096 // WS_)]
    rtoks, dtoks = {}, {}
    atok = [z]

    def red_job(k):
        m = subs[k][0]
        par = k % P
        ctx.wait("vector", dtoks[k])
        ctx.wait("vector", atok[0])
        rt = ctx.op("vector", lambda e, par=par: e.tensor_reduce(
            out=red[:], in_=buf[par][:], axis=AX.X, op=ALU.add,
            apply_absolute_value=True), inc="sV")
        rtoks[k] = rt
        ctx.wait("vector", rt)
        atok[0] = ctx.op("vector", lambda e, m=m: e.tensor_tensor(
            out=acc[:, m:m + 1], in0=acc[:, m:m + 1], in1=red[:], op=ALU.add),
            inc="sV")

    D = 2
    for n, (m, i, s) in enumerate(subs):
        par = n % P
        ctx.wait("gpsimd", rtoks.get(n - P))
        dtoks[n] = ctx.dma("gpsimd", buf[par][:],
                           ctx.in_qsl[base[m] + i][:, s * WS_:(s + 1) * WS_],
                           "QW", slots=P)
        if n >= D:
            red_job(n - D)
        yield
    for k in range(max(0, len(subs) - D), len(subs)):
        red_job(k)
    ctx.wait("gpsimd", atok[0])
    crt = ctx.op("gpsimd", lambda e: e.tensor_reduce(
        out=cr[:], in_=acc[:], axis=AX.C, op=ALU.add), inc="gq")
    ctx.waits("vector", [ctx.qsrat_tok, ctx.qseps_tok])
    ctx.wait("vector", crt)
    t1 = ctx.op("vector", lambda e: e.tensor_tensor(
        out=tot[:].rearrange("p (a b) -> p a b", a=2),
        in0=cr[0:1, :].unsqueeze(1).broadcast_to([1, 2, 8]),
        in1=ctx.in_qsrat_sb[:].rearrange("p (a b) -> p a b", a=2),
        op=ALU.mult), inc="sV")
    ctx.wait("vector", t1)
    t2 = ctx.op("vector", lambda e: e.tensor_tensor(
        out=tot[:], in0=tot[:], in1=ctx.in_qseps_sb[:], op=ALU.add),
        inc="sV")
    ctx.wait("gpsimd", t2)
    d1 = ctx.dma("gpsimd", ctx.cc_qs_in.ap(), tot[:], "QO")
    ctx.wait("gpsimd", d1)
    c1 = ctx.op("gpsimd", lambda e: e.collective_compute(
        "AllReduce", ALU.add, replica_groups=[list(range(NCORES))],
        ins=[ctx.cc_qs_in.ap()], outs=[ctx.cc_qs_out.ap()]), inc="g_cc",
        inc_n=1)
    ctx.wait("gpsimd", c1)
    ctx.qs_ready = ctx.dma(
        "gpsimd", ctx.thscols[:],
        ctx.cc_qs_out.ap()[0:1, :].partition_broadcast(128).squeeze(1), "QO")


def gen_shard_all(ctx, lis):
    W = 512
    wqb = [ctx.sb(f"sq_w{i}", [128, W]) for i in range(3)]
    sgq = [ctx.sb(f"sq_s{i}", [128, W], BF16) for i in range(3)]
    tqb = [ctx.sb(f"sq_t{i}", [128, W], BF16) for i in range(3)]
    jobs = []
    for li in lis:
        o = 0 if li == 0 else 3
        ths = [ctx.thscols[:, o + m:o + m + 1] for m in range(3)]
        ECb = 8 if li == 0 else 6
        E2b = 4 if li == 0 else 3
        wup = ctx.in_wup0 if li == 0 else ctx.in_wup1
        wgt = ctx.in_wgt0 if li == 0 else ctx.in_wgt1
        wdn = ctx.in_wdn0 if li == 0 else ctx.in_wdn1
        agi = (ctx.ag0_in if li == 0 else ctx.ag1_in).ap()
        for i in range(ECb):
            for h2 in range(2048 // W):
                jobs.append((wup[i][:, h2 * W:(h2 + 1) * W], ths[0],
                             agi[i][:, h2 * W:(h2 + 1) * W], None))
        for i in range(ECb):
            for h2 in range(2048 // W):
                jobs.append((wgt[i][:, h2 * W:(h2 + 1) * W], ths[1],
                             agi[ECb + i][:, h2 * W:(h2 + 1) * W], None))
        for j in range(E2b):
            for dc in range(4):
                idx = j * 4 + dc
                for h2 in range(1024 // W):
                    jobs.append((wdn[j, dc][:, h2 * W:(h2 + 1) * W], ths[2],
                                 agi[2 * ECb + idx // 2]
                                 [:, (idx % 2) * 1024 + h2 * W:
                                  (idx % 2) * 1024 + (h2 + 1) * W],
                                 None))
        jobs[-1] = jobs[-1][:3] + (li,)
    ctx.wait("sync", ctx.qs_ready)
    qtok, stok = {}, {}
    stash = []
    for n, (srcap, th, dst, li_end) in enumerate(jobs):
        par = n % 3
        ctx.wait("sync", qtok.get(n - 3))
        dw = ctx.dma("sync", wqb[par][:], srcap, "QW", slots=6)
        ctx.wait("scalar", dw)
        ctx.wait("scalar", qtok.get(n - 3))
        sg = ctx.op("scalar", lambda e, par=par, th=th: e.activation(
            out=sgq[par][:], in_=wqb[par][:], func=AF.Sign,
            bias=th, scale=1.0), inc="sS")
        ctx.wait("vector", sg)
        ctx.wait("vector", stok.get(n - 3))
        qt_ = ctx.op("vector", lambda e, par=par, th=th: e.scalar_tensor_tensor(
            out=tqb[par][:], in0=wqb[par][:], scalar=th,
            in1=sgq[par][:], op0=ALU.is_gt, op1=ALU.min), inc="sV")
        qtok[n] = qt_
        ctx.wait("scalar", qt_)
        stok[n] = ctx.dma("scalar", dst, tqb[par][:], "QB")
        stash.append(stok[n])
        if li_end is not None:
            agi_t = ctx.ag0_in if li_end == 0 else ctx.ag1_in
            ago_t = ctx.ag0_out if li_end == 0 else ctx.ag1_out
            ctx.waits("gpsimd", list(stash))
            stash = []
            ctx.op("gpsimd", lambda e, agi=agi_t, ago=ago_t: e.collective_compute(
                "AllGather", ALU.bypass, replica_groups=[list(range(NCORES))],
                ins=[agi.ap()], outs=[ago.ap()]),
                inc=f"g_ag{li_end}", inc_n=1)
        yield


def emit_ffn(ctx, li):
    nc = ctx.nc
    E = E0 if li == 0 else E1
    EC = EC0 if li == 0 else EC1
    pre = f"f{li}"
    vs, ss, ts, ds, gs = "sV", "sS", "sT", "WS", "SS"
    o = 0 if li == 0 else 3
    th_up = ctx.thscols[:, o:o + 1]
    th_gt = ctx.thscols[:, o + 1:o + 2]
    th_dn = ctx.thscols[:, o + 2:o + 3]
    s_up = ctx.thscols[:, 8 + o:9 + o]
    s_gt = ctx.thscols[:, 9 + o:10 + o]
    s_dn = ctx.thscols[:, 10 + o:11 + o]

    hbm_hid = ctx.hbm_hid0 if li == 0 else ctx.hbm_hid1
    agv = (ctx.ag0_out if li == 0 else ctx.ag1_out).ap()
    ag_tok = ("g_ag0" if li == 0 else "g_ag1", 1)
    ECb = 8 if li == 0 else 6
    E2b = 4 if li == 0 else 3
    NS = 3 * ECb

    res_toks = []
    ctx.wait("vector", ctx.qs_ready)
    ctx.wait("scalar", ctx.qs_ready)
    with ExitStack() as ph:
        old_scope, ctx.scope = ctx.scope, ph
        hid = ctx.sb(pre + "_hid", [128, EC * 256], BF16)
        aptmp = ctx.sb(pre + "_aptmp", [128, 2048])
        mls = ExitStack()
        ctx.scope = mls
        lnb = ctx.sb(pre + "_lnb", [128, 2048], BF16)
        lnT = ctx.sb(pre + "_lnT", [128, KD * 1024], BF16)
        lnTv = lnT[:].rearrange("p (k w) -> p k w", w=1024)

        with ExitStack() as trs:
            ptr = [trs.enter_context(nc.psum_tensor(pre + f"_ptr{i}", [128, 1024], BF16))
                   for i in range(2)]

            def tr_tile(t, atok):
                last_ev = None
                for g in range(2):
                    par = (2 * t + g) % 2
                    ctx.wait("tensor", atok)
                    tk = None
                    for j in range(8):
                        tk = ctx.op("tensor", lambda e, j=j, g=g, par=par: e.transpose(
                            out=ptr[par][:, j * 128:(j + 1) * 128],
                            in_=lnb[:, (8 * g + j) * 128:(8 * g + j + 1) * 128],
                            identity=ctx.identb[:]), inc=ts if j == 7 else None)
                    ctx.wait("scalar", tk)
                    last_ev = ctx.op("scalar", lambda e, t=t, g=g, par=par: e.copy(
                        out=lnTv[:, 8 * g:8 * g + 8, 128 * t:128 * (t + 1)],
                        in_=ptr[par][:].rearrange("p (k w) -> p k w", w=128)),
                        inc=ss)
                    ctx.wait("tensor", last_ev)
                return last_ev

            emit_ln_tiles(ctx, pre + "ln",
                          lambda t: (ctx.xres_ap(t), ctx.xres_tok),
                          NT, lambda t: lnb[:], tr_tile, guard_lag=1)
        lnT_ready = ctx.tok(ss)
        ctx.dump(pre + "lnT", lnT[:], lnT_ready)

        with ExitStack() as ugs:
            ctx.scope = ugs
            pps = lambda n: ugs.enter_context(nc.psum_tensor(pre + n, [128, 512], F32))
            psu = [[pps(f"_pu{p}{hf}") for hf in range(2)] for p in range(2)]
            psg = [[pps(f"_pg{p}{hf}") for hf in range(2)] for p in range(2)]
            wub = [ctx.sb(pre + f"_wub{i}", [128, 2048], BF16) for i in range(3)]
            wgb = [ctx.sb(pre + f"_wgb{i}", [128, 2048], BF16) for i in range(3)]
            sil = [ctx.sb(pre + f"_sil{i}", [128, 512]) for i in range(2)]
            hstg = [ctx.sb(pre + f"_hst{i}", [128, 1024], BF16) for i in range(3)]

            ctx.wait("sync", ctx.qs_ready)
            ctx.wait("sync", ag_tok)
            mm_par = {}
            glu_toks = {}
            glu_par = {}
            stash_par = {}
            hid_last = None
            for ec in range(EC):
                par = ec % 3
                r, i = divmod(ec, ECb)
                ctx.wait("sync", mm_par.get(par))
                du = ctx.dma("sync", wub[par][:], agv[r * NS + i], ds)
                dg = ctx.dma("sync", wgb[par][:], agv[r * NS + ECb + i], ds)
                ctx.wait("tensor", du)
                ctx.wait("tensor", dg)
                ctx.wait("tensor", glu_toks.get(ec - 2))
                if ec == 0:
                    ctx.wait("tensor", lnT_ready)
                mt = None
                for kk in range(16):
                    st = (kk == 0)
                    sp = (kk == 15)
                    for hf in range(2):
                        ctx.op("tensor", lambda e, par=par, kk=kk, st=st, sp=sp,
                               pb=psu[ec % 2][hf], hf=hf: e.matmul(
                            out=pb[:], lhsT=wub[par][:, kk * 128:(kk + 1) * 128],
                            rhs=lnTv[:, kk, hf * 512:(hf + 1) * 512],
                            start=st, stop=sp), inc=ts)
                    for hf in range(2):
                        mt = ctx.op("tensor", lambda e, par=par, kk=kk, st=st,
                                    sp=sp, pb=psg[ec % 2][hf], hf=hf: e.matmul(
                            out=pb[:], lhsT=wgb[par][:, kk * 128:(kk + 1) * 128],
                            rhs=lnTv[:, kk, hf * 512:(hf + 1) * 512],
                            start=st, stop=sp), inc=ts)
                mm_par[par] = mt
                mm_done = mt
                gt = None
                for hf in range(2):
                    pb_u = psu[ec % 2][hf]
                    pb_g = psg[ec % 2][hf]
                    parS = hf
                    ctx.wait("scalar", mm_done)
                    ctx.wait("scalar", glu_par.get(parS))
                    if li == 0:
                        sv = ctx.op("scalar", lambda e, pb=pb_u, parS=parS: e.activation(
                            out=sil[parS][:], in_=pb[:], func=AF.Silu,
                            scale=s_up), inc=ss)
                    else:
                        sv = ctx.op("scalar", lambda e, pb=pb_u, parS=parS: e.activation(
                            out=sil[parS][:], in_=pb[:], func=AF.Sigmoid), inc=ss)
                    ctx.wait("vector", sv)
                    for sub in range(2):
                        qi = hf * 2 + sub
                        qof = sub * 256
                        if qi == 0:
                            dst = hid[:, ec * 256:(ec + 1) * 256]
                        else:
                            if ec % 4 == 0:
                                ctx.wait("vector", stash_par.get(qi - 1))
                            dst = hstg[qi - 1][:, (ec % 4) * 256:(ec % 4 + 1) * 256]
                        gt = ctx.op("vector", lambda e, pb=pb_g, qof=qof, parS=parS,
                                    dst=dst, li=li: e.scalar_tensor_tensor(
                            out=dst, in0=pb[:, qof:qof + 256],
                            scalar=(s_gt if li == 0 else 1.0),
                            in1=sil[parS][:, qof:qof + 256],
                            op0=ALU.mult, op1=ALU.mult), inc=vs)
                        glu_par[parS] = gt
                        if qi == 0:
                            hid_last = gt
                        elif ec % 4 == 3:
                            ctx.wait("scalar", gt)
                            stash_par[qi - 1] = ctx.dma(
                                "scalar", hbm_hid[qi - 1, ec // 4], hstg[qi - 1][:], gs)
                glu_toks[ec] = gt
            hid_ready = ctx.tok(vs)
            stash_done = ctx.stream_toks(gs)
            hd = ctx.dump(pre + "hid", hid[:], hid_ready)
            if hd is not None:
                hid_ready = hd
        ctx.barrier_all()
        ctx.scope = ph
        mls.close()

        with ExitStack() as dns:
            ctx.scope = dns
            pps = lambda n: dns.enter_context(nc.psum_tensor(pre + n, [128, 512], F32))
            psd = [[[pps(f"_pd{q_}{t_}{d_}") for d_ in range(2)]
                    for t_ in range(2)] for q_ in range(2)]
            dwq = [ctx.sb(pre + f"_dwq{i}", [128, 1024], BF16) for i in range(3)]
            hidB = ctx.sb(pre + "_hidB", [128, EC * 256], BF16)
            fout = [[ctx.sb(pre + f"_fo{q_}{t_}", [128, 2048], BF16)
                     for t_ in range(2)] for q_ in range(2)]
            hids = [hid, hidB]
            EC2 = EC // 2

            dmm3 = {}
            prev_ev = None
            dmm3_ec = {}
            for qp in range(2):
                hload = {0: {}, 1: {}}
                for qq in range(2):
                    quar = 2 * qp + qq
                    if quar == 0:
                        continue
                    for eb in range(EC // 4):
                        if quar == 1 and eb == 0:
                            ctx.wait("sync", stash_done)
                        ctx.wait("sync", dmm3_ec.get(2 * eb + 1))
                        hload[qq][eb] = ctx.dma(
                            "sync", hids[qq][:, eb * 1024:(eb + 1) * 1024],
                            hbm_hid[quar - 1, eb], ds)
                mt = None
                for dh in range(2):
                    for dc2 in range(2):
                        dc = 2 * dh + dc2
                        for ecp in range(EC2):
                            n = dc * EC2 + ecp
                            par3 = n % 3
                            r, j = divmod(ecp, E2b)
                            idx = j * 4 + dc
                            dsrc = agv[r * NS + 2 * ECb + idx // 2][
                                :, (idx % 2) * 1024:(idx % 2) * 1024 + 1024]
                            ctx.wait("sync", dmm3.get(par3))
                            dq = ctx.dma("sync", dwq[par3][:], dsrc, ds)
                            ctx.wait("tensor", dq)
                            if qp == 0 and dh == 0 and dc2 == 0 and ecp == 0:
                                ctx.wait("tensor", hid_ready)
                            if ecp == 0:
                                ctx.wait("tensor", prev_ev)
                            lt = None
                            for e2 in range(2):
                                ec = 2 * ecp + e2
                                st = (ecp == 0 and e2 == 0)
                                sp = (ecp == EC2 - 1 and e2 == 1)
                                for qq in range(2):
                                    if 2 * qp + qq > 0:
                                        ctx.wait("tensor", hload[qq].get(ec // 4))
                                    for tt in range(2):
                                        lt = ctx.op("tensor", lambda e, par3=par3,
                                                    qq=qq, tt=tt, dc2=dc2, ec=ec,
                                                    e2=e2, st=st, sp=sp: e.matmul(
                                            out=psd[qq][tt][dc2][:],
                                            lhsT=hids[qq][:, ec * 256 + tt * 128:
                                                          ec * 256 + (tt + 1) * 128],
                                            rhs=dwq[par3][:, e2 * 512:(e2 + 1) * 512],
                                            start=st, stop=sp), inc=ts)
                            mt = lt
                            dmm3[par3] = lt
                            if dh == 1 and dc2 == 1:
                                dmm3_ec[ecp] = lt
                    ev = None
                    if res_toks:
                        ctx.wait("scalar", res_toks[-1])
                    for qq in range(2):
                        for tt in range(2):
                            for dc2 in range(2):
                                dc = 2 * dh + dc2
                                ctx.wait("scalar", mt)
                                if li == 0:
                                    ev = ctx.op("scalar", lambda e, qq=qq, tt=tt,
                                                dc=dc, dc2=dc2: e.activation(
                                        out=fout[qq][tt][:, dc * 512:(dc + 1) * 512],
                                        in_=psd[qq][tt][dc2][:], func=AF.Copy,
                                        scale=s_dn, bias=0.0), inc=ss)
                                else:
                                    ev = ctx.op("scalar", lambda e, qq=qq, tt=tt,
                                                dc=dc, dc2=dc2: e.copy(
                                        out=fout[qq][tt][:, dc * 512:(dc + 1) * 512],
                                        in_=psd[qq][tt][dc2][:]), inc=ss)
                    prev_ev = ev
                for qq in range(2):
                    quar = 2 * qp + qq
                    fsrc = [(fout[qq][0][:], prev_ev, quar * 2 + 0),
                            (fout[qq][1][:], prev_ev, quar * 2 + 1)]
                    rt = emit_postln_residual(ctx, f"{pre}_po{quar}", fsrc,
                                              ctx.xres_ap, aptmp)
                    res_toks.extend(rt)
                    ctx.xres_tok = rt[-1]
        ctx.scope = old_scope
    return res_toks


def emit_l0_attn(ctx):
    nc = ctx.nc
    vs, ss, ts, ds, gs = "sV", "sS", "sT", "WS", "SS"

    res_toks = []
    with ExitStack() as ph:
        old_scope, ctx.scope = ctx.scope, ph
        vtm = ctx.sb("a_vtm", [128, ATT * 512], BF16)
        kT = ctx.sb("a_kT", [128, 4 * ATT * 128], BF16)
        kTv = kT[:].rearrange("p (g w) -> p g w", w=ATT * 128)
        cosb = ctx.sb("a_cos", [128, ATT * 64])
        sinb = ctx.sb("a_sin", [128, ATT * 64])
        knwb = ctx.sb("a_knw", [128, 512])
        msq = ctx.sb("a_msq", [128, 32])
        rst8 = ctx.sb("a_rst8", [128, 32])

        c1 = ctx.dma("scalar", knwb[:],
                     ctx.in_knw[0:1, :].partition_broadcast(128).squeeze(1), gs)
        c3 = None
        for j in range(ATT):
            ctx.dma("scalar", cosb[:, j * 64:(j + 1) * 64],
                    ctx.in_cos[j * 128:(j + 1) * 128, :], gs)
            c3 = ctx.dma("scalar", sinb[:, j * 64:(j + 1) * 64],
                         ctx.in_sin[j * 128:(j + 1) * 128, :], gs)
        consts_rdy = ctx.stream_toks(gs)

        with ExitStack() as s2:
            ctx.scope = s2
            ln0T = ctx.sb("a_ln0T", [128, ATT * 2048], BF16)
            with ExitStack() as s1:
                ctx.scope = s1
                xhalo = ctx.sb("a_xh", [128, 2048])
                lnb = ctx.sb("a_lnb", [128, 2048], BF16)
                ktmp = ctx.sb("a_ktmp", [128, ATT * 512], BF16)
                kbf = ctx.sb("a_kbf", [128, 512], BF16)
                k32 = ctx.sb("a_k32", [128, 512])
                ksq = ctx.sb("a_ksq", [128, 512])

                with ExitStack() as trs:
                    ptr2 = [trs.enter_context(nc.psum_tensor(
                        "a_ptr" + str(i), [128, 1024], BF16)) for i in range(2)]

                    def tr0(j, atok):
                        ev = None
                        for g in range(2):
                            par = (2 * j + g) % 2
                            ctx.wait("tensor", atok)
                            tk = None
                            for p in range(8):
                                tk = ctx.op("tensor", lambda e, p=p, g=g, par=par: e.transpose(
                                    out=ptr2[par][:, p * 128:(p + 1) * 128],
                                    in_=lnb[:, (8 * g + p) * 128:(8 * g + p + 1) * 128],
                                    identity=ctx.identb[:]), inc=ts if p == 7 else None)
                            ctx.wait("scalar", tk)
                            ev = ctx.op("scalar", lambda e, j=j, g=g, par=par: e.copy(
                                out=ln0T[:, j * 2048 + g * 1024: j * 2048 + (g + 1) * 1024],
                                in_=ptr2[par][:]), inc=ss)
                            ctx.wait("tensor", ev)
                        return ev

                    tr_cell = [None]

                    def tr0g(j, a):
                        tr_cell[0] = tr0(j, a)
                        return tr_cell[0]

                    prev_apply = [None]
                    for j in range(NHT):
                        if prev_apply[0] is not None:
                            ctx.wait("scalar", prev_apply[0])
                        dtok = ctx.dma("scalar", xhalo[:],
                                       ctx.in_x[j * 128:(j + 1) * 128, :], gs)
                        toks = emit_ln_tiles(ctx, f"a0lnh{j}",
                                             lambda t, dtok=dtok: (xhalo[:], dtok),
                                             1, lambda t: lnb[:],
                                             lambda t, a, j=j: tr0g(j, a),
                                             guard_lag=1, pre_apply=tr_cell[0])
                        prev_apply[0] = toks[-1]
                    emit_ln_tiles(ctx, "a0ln",
                                  lambda t: (ctx.xres_ap(t), ctx.xres_tok),
                                  NT, lambda t: lnb[:],
                                  lambda t, a: tr0g(t + NHT, a), guard_lag=1,
                                  pre_apply=tr_cell[0])
                ln0T_ready = ctx.tok(ss)

                with ExitStack() as kvps:
                    pkv = [kvps.enter_context(nc.psum_tensor(
                        f"a_pkv{i}", [128, 128], F32)) for i in range(2)]
                    wkvb = ctx.sb("a_wkv0", [128, 2048])
                    wkvc = [ctx.sb(f"a_wkvc{i}", [128, 2048], BF16) for i in range(2)]
                    cast_tok, ev_tok, mm_tok = {}, {}, {}
                    for hc in range(8):
                        par = hc % 2
                        ctx.wait("sync", cast_tok.get(hc - 1))
                        dw = ctx.dma("sync", wkvb[:], ctx.in_wkv[hc], ds)
                        ctx.wait("scalar", dw)
                        ctx.wait("scalar", mm_tok.get(hc - 2))
                        ck = ctx.op("scalar", lambda e, par=par: e.copy(
                            out=wkvc[par][:], in_=wkvb[:]), inc=ss)
                        cast_tok[hc] = ck
                        ctx.wait("tensor", ck)
                        if hc == 0:
                            ctx.wait("tensor", ln0T_ready)
                        wv = wkvc[par][:].rearrange("p (k w) -> p k w", w=128)
                        for j in range(ATT):
                            ctx.pump("scales")
                            pp = (hc * ATT + j) % 2
                            ctx.wait("tensor", ev_tok.get((hc * ATT + j) - 2))
                            mm = None
                            for k in range(16):
                                mm = ctx.op("tensor", lambda e, k=k, j=j, pp=pp, wv=wv: e.matmul(
                                    out=pkv[pp][:],
                                    lhsT=ln0T[:, j * 2048 + k * 128: j * 2048 + (k + 1) * 128],
                                    rhs=wv[:, k, :], start=(k == 0), stop=(k == 15)),
                                    inc=ts)
                            mm_tok[hc] = mm
                            ctx.wait("scalar", mm)
                            if hc < 4:
                                ev = ctx.op("scalar", lambda e, j=j, hc=hc, pp=pp: e.copy(
                                    out=ktmp[:, j * 512 + hc * 128: j * 512 + (hc + 1) * 128],
                                    in_=pkv[pp][:]), inc=ss)
                            else:
                                ev = ctx.op("scalar", lambda e, j=j, hc=hc, pp=pp: e.copy(
                                    out=vtm[:, j * 512 + (hc - 4) * 128: j * 512 + (hc - 3) * 128],
                                    in_=pkv[pp][:]), inc=ss)
                            ev_tok[hc * ATT + j] = ev
                ctx.drain("scales")
                v_ready = ctx.tok(ss)

                ctx.wait("vector", consts_rdy)
                with ExitStack() as kps:
                    pktr = kps.enter_context(nc.psum_tensor("a_pktr", [128, 512], BF16))
                    tr_ev = None
                    for j in range(ATT):
                        kap = ktmp[:, j * 512:(j + 1) * 512]
                        ctx.wait("scalar", v_ready)
                        sqk = ctx.op("scalar", lambda e, kap=kap: e.activation(
                            out=ksq[:], in_=kap, func=AF.Square), inc=ss)
                        ctx.wait("vector", sqk)
                        rd = ctx.op("vector", lambda e: e.tensor_reduce(
                            out=msq[:, 0:8],
                            in_=ksq[:].rearrange("p (h w) -> p h w", w=64),
                            axis=AX.X, op=ALU.add), inc=vs)
                        ctx.wait("scalar", rd)
                        l1 = ctx.op("scalar", lambda e: e.activation(
                            out=rst8[:, 0:8], in_=msq[:, 0:8], func=AF.Ln,
                            bias=ctx.eps6[:], scale=1.0 / 64), inc=ss)
                        ctx.wait("scalar", l1)
                        l2 = ctx.op("scalar", lambda e: e.activation(
                            out=rst8[:, 0:8], in_=rst8[:, 0:8], func=AF.Exp,
                            scale=-0.5), inc=ss)
                        ctx.wait("vector", l2)
                        a1 = ctx.op("vector", lambda e, kap=kap: e.tensor_tensor(
                            out=k32[:].rearrange("p (h w) -> p h w", w=64),
                            in0=kap.rearrange("p (h w) -> p h w", w=64),
                            in1=rst8[:, 0:8].unsqueeze(2).broadcast_to([128, 8, 64]),
                            op=ALU.mult), inc=vs)
                        ctx.wait("vector", a1)
                        a2 = ctx.op("vector", lambda e: e.tensor_tensor(
                            out=k32[:], in0=k32[:], in1=knwb[:], op=ALU.mult), inc=vs)
                        ctx.wait("vector", a2)
                        kv4 = k32[:].rearrange("p (h two w) -> p h two w", two=2, w=32)
                        t4 = ksq[:].rearrange("p (h two w) -> p h two w", two=2, w=32)
                        sin4 = sinb[:, j * 64:(j + 1) * 64].rearrange(
                            "p (two w) -> p two w", two=2).unsqueeze(1).broadcast_to(
                            [128, 8, 2, 32])
                        cos3 = cosb[:, j * 64:(j + 1) * 64].unsqueeze(1).broadcast_to(
                            [128, 8, 64])
                        r1 = ctx.op("vector", lambda e, t4=t4, kv4=kv4, sin4=sin4: e.tensor_tensor(
                            out=t4[:, :, 0, :], in0=kv4[:, :, 1, :],
                            in1=sin4[:, :, 0, :], op=ALU.mult), inc=vs)
                        r2 = ctx.op("vector", lambda e, t4=t4, kv4=kv4, sin4=sin4: e.tensor_tensor(
                            out=t4[:, :, 1, :], in0=kv4[:, :, 0, :],
                            in1=sin4[:, :, 1, :], op=ALU.mult), inc=vs)
                        ctx.wait("vector", r2)
                        r3 = ctx.op("vector", lambda e, cos3=cos3: e.tensor_tensor(
                            out=k32[:].rearrange("p (h w) -> p h w", w=64),
                            in0=k32[:].rearrange("p (h w) -> p h w", w=64),
                            in1=cos3, op=ALU.mult), inc=vs)
                        ctx.wait("vector", r3)
                        r4 = ctx.op("vector", lambda e: e.tensor_tensor(
                            out=k32[:], in0=k32[:], in1=ksq[:], op=ALU.add),
                            inc=vs)
                        ctx.wait("scalar", r4)
                        cst = ctx.op("scalar", lambda e: e.copy(
                            out=kbf[:], in_=k32[:]), inc=ss)
                        ctx.wait("tensor", cst)
                        ctx.wait("tensor", tr_ev)
                        tk = None
                        for g in range(4):
                            tk = ctx.op("tensor", lambda e, g=g: e.transpose(
                                out=pktr[:, g * 128:(g + 1) * 128],
                                in_=kbf[:, g * 128:(g + 1) * 128],
                                identity=ctx.identb[:]), inc=ts if g == 3 else None)
                        ctx.wait("scalar", tk)
                        tr_ev = ctx.op("scalar", lambda e, j=j: e.copy(
                            out=kTv[:, :, j * 128:(j + 1) * 128],
                            in_=pktr[:].rearrange("p (g w) -> p g w", w=128)),
                            inc=ss)
                        ctx.wait("tensor", tr_ev)
                kT_ready = ctx.tok(ss)
                ctx.scope = s2
                ctx.barrier_all()

            with ExitStack() as qps:
                ctx.scope = qps
                pq = [qps.enter_context(nc.psum_tensor(
                    f"a_pq{i}", [128, 128], F32)) for i in range(2)]
                qtmp = ctx.sb("a_qtmp", [128, NT * 2048], BF16)
                wqb = [ctx.sb(f"a_wqb{i}", [128, 2048]) for i in range(2)]
                wqc = [ctx.sb(f"a_wqc{i}", [128, 2048], BF16) for i in range(2)]
                cast_tok, ev_tok, mm_tok = {}, {}, {}
                for oc in range(16):
                    par = oc % 2
                    ctx.wait("sync", cast_tok.get(oc - 2))
                    dw = ctx.dma("sync", wqb[par][:], ctx.in_wq[oc], ds)
                    ctx.wait("scalar", dw)
                    ctx.wait("scalar", mm_tok.get(oc - 2))
                    ck = ctx.op("scalar", lambda e, par=par: e.copy(
                        out=wqc[par][:], in_=wqb[par][:]), inc=ss)
                    cast_tok[oc] = ck
                    ctx.wait("tensor", ck)
                    wv = wqc[par][:].rearrange("p (k w) -> p k w", w=128)
                    for t in range(NT):
                        pp = (oc * NT + t) % 2
                        ctx.wait("tensor", ev_tok.get((oc * NT + t) - 2))
                        mm = None
                        for k in range(16):
                            j = t + NHT
                            mm = ctx.op("tensor", lambda e, k=k, j=j, pp=pp, wv=wv: e.matmul(
                                out=pq[pp][:],
                                lhsT=ln0T[:, j * 2048 + k * 128: j * 2048 + (k + 1) * 128],
                                rhs=wv[:, k, :], start=(k == 0), stop=(k == 15)),
                                inc=ts)
                        mm_tok[oc] = mm
                        ctx.wait("scalar", mm)
                        ev = ctx.op("scalar", lambda e, t=t, oc=oc, pp=pp: e.copy(
                            out=qtmp[:, t * 2048 + oc * 128: t * 2048 + (oc + 1) * 128],
                            in_=pq[pp][:]), inc=ss)
                        ev_tok[oc * NT + t] = ev
                qmm_done = ctx.tok(ss)
                ctx.dump("a_qtmp", qtmp[:], qmm_done)
                ctx.dump("a_ln0T", ln0T[:], qmm_done)
                ctx.wait("scalar", qmm_done)
                for t in range(NT):
                    ctx.dma("scalar", ctx.hbm_qtmp[t],
                            qtmp[:, t * 2048:(t + 1) * 2048], gs)
                qstash = ctx.stream_toks(gs)
            ctx.scope = ph
        ctx.barrier_all()

        attnT = ctx.sb("a_attnT", [128, KD * 1024], BF16)
        attnTv = attnT[:].rearrange("p (k w) -> p k w", w=1024)

        with ExitStack() as ats:
            ctx.scope = ats
            rop = ctx.sb("a_rop", [128, 2048])
            sq = ctx.sb("a_sq", [128, 2048])
            qnwb = ctx.sb("a_qnw", [128, 2048])
            masks = ctx.sb("a_msk", [128, NT * 640], BF16)
            qb = ctx.sb("a_qb", [128, 2048], BF16)
            qT = ctx.sb("a_qT", [128, 2048], BF16)
            probs = [ctx.sb(f"a_probs{i}", [128, 640], BF16) for i in range(3)]
            probsT = [ctx.sb(f"a_probsT{i}", [128, 640], BF16) for i in range(2)]
            scmax = ctx.sb("a_scmax", [128, 4])
            ssum = ctx.sb("a_ssum", [128, 8])
            rec = ctx.sb("a_rec", [128, 8])
            atm = ctx.sb("a_atm", [128, 2048], BF16)
            qcur = [ctx.sb(f"a_qcur{i}", [128, 2048], BF16) for i in range(2)]
            cq = ctx.dma("scalar", qnwb[:],
                         ctx.in_qnw[0:1, :].partition_broadcast(128).squeeze(1), gs)
            cm = None
            for i in range(NT):
                cm = ctx.dma("scalar", masks[:, i * 640:(i + 1) * 640],
                             ctx.in_masks[i], gs)
            ctx.wait("vector", cm)
            ctx.wait("vector", cq)
            ctx.op("vector", lambda e: e.tensor_scalar(
                out=qnwb[:], in0=qnwb[:], scalar1=0.125, scalar2=None,
                op0=ALU.mult), inc="sV")
            psc = [ats.enter_context(nc.psum_tensor(
                f"a_psc{i}", [128, 1024], F32)) for i in range(2)]
            ppv = [ats.enter_context(nc.psum_tensor(
                f"a_ppv{i}", [128, 512], F32)) for i in range(2)]
            ptr3 = [ats.enter_context(nc.psum_tensor(
                f"a_ptr3{i}", [128, 1024], BF16)) for i in range(2)]
            exp_tok, ptr_done, pv_ev, grp_ev, rc_tok = {}, {}, {}, {}, {}
            pvm_tok, ptr3_free, pev_tok = {}, {}, {}
            qload, qfree = {}, {}
            if DEBUG_DUMP:
                dbg_qcur = ctx.sb("dbgb_qcur", [128, 2048], BF16)
                dbg_sq = ctx.sb("dbgb_sq", [128, 2048])
                dbg_msq = ctx.sb("dbgb_msq", [128, 32 * NT])
                dbg_rst = ctx.sb("dbgb_rst", [128, 32 * NT])
                dbg_ssum = ctx.sb("dbgb_ssum", [128, NH])
                dbg_smax = ctx.sb("dbgb_smax", [128, NH])
            for t in range(NT):
                qap = qcur[t % 2][:]
                ctx.wait("scalar", qstash)
                ctx.wait("scalar", qfree.get(t - 2))
                qload[t] = ctx.dma("scalar", qap, ctx.hbm_qtmp[t], gs)
                ctx.wait("scalar", qload[t])
                sqk = ctx.op("scalar", lambda e, qap=qap: e.activation(
                    out=sq[:], in_=qap, func=AF.Square), inc=ss)
                ctx.wait("vector", sqk)
                if DEBUG_DUMP and t == 0:
                    ctx.op("vector", lambda e, qap=qap: e.tensor_copy(
                        out=dbg_qcur[:], in_=qap), inc=vs)
                    ctx.op("vector", lambda e: e.tensor_copy(
                        out=dbg_sq[:], in_=sq[:]), inc=vs)
                rd = ctx.op("vector", lambda e: e.tensor_reduce(
                    out=msq[:], in_=sq[:].rearrange("p (h w) -> p h w", w=64),
                    axis=AX.X, op=ALU.add), inc=vs)
                ctx.wait("scalar", rd)
                l1 = ctx.op("scalar", lambda e: e.activation(
                    out=rst8[:], in_=msq[:], func=AF.Ln, bias=ctx.eps6[:],
                    scale=1.0 / 64), inc=ss)
                ctx.wait("scalar", l1)
                l2 = ctx.op("scalar", lambda e: e.activation(
                    out=rst8[:], in_=rst8[:], func=AF.Exp, scale=-0.5), inc=ss)
                ctx.wait("vector", l2)
                if DEBUG_DUMP:
                    ctx.op("vector", lambda e, t=t: e.tensor_copy(
                        out=dbg_msq[:, t * 32:(t + 1) * 32], in_=msq[:]), inc=vs)
                    ctx.op("vector", lambda e, t=t: e.tensor_copy(
                        out=dbg_rst[:, t * 32:(t + 1) * 32], in_=rst8[:]), inc=vs)
                a1 = ctx.op("vector", lambda e, qap=qap: e.tensor_tensor(
                    out=rop[:].rearrange("p (h w) -> p h w", w=64),
                    in0=qap.rearrange("p (h w) -> p h w", w=64),
                    in1=rst8[:].unsqueeze(2).broadcast_to([128, 32, 64]),
                    op=ALU.mult), inc=vs)
                ctx.wait("vector", a1)
                qfree[t] = a1
                a2 = ctx.op("vector", lambda e: e.tensor_tensor(
                    out=rop[:], in0=rop[:], in1=qnwb[:], op=ALU.mult), inc=vs)
                ctx.wait("vector", a2)
                q4 = rop[:].rearrange("p (h two w) -> p h two w", two=2, w=32)
                t4 = sq[:].rearrange("p (h two w) -> p h two w", two=2, w=32)
                jj = t + NHT
                sin4 = sinb[:, jj * 64:(jj + 1) * 64].rearrange(
                    "p (two w) -> p two w", two=2).unsqueeze(1).broadcast_to(
                    [128, 32, 2, 32])
                cos3 = cosb[:, jj * 64:(jj + 1) * 64].unsqueeze(1).broadcast_to(
                    [128, 32, 64])
                r1 = ctx.op("vector", lambda e, t4=t4, q4=q4, sin4=sin4: e.tensor_tensor(
                    out=t4[:, :, 0, :], in0=q4[:, :, 1, :], in1=sin4[:, :, 0, :],
                    op=ALU.mult), inc=vs)
                r2 = ctx.op("vector", lambda e, t4=t4, q4=q4, sin4=sin4: e.tensor_tensor(
                    out=t4[:, :, 1, :], in0=q4[:, :, 0, :], in1=sin4[:, :, 1, :],
                    op=ALU.mult), inc=vs)
                ctx.wait("vector", r2)
                r3 = ctx.op("vector", lambda e, cos3=cos3: e.tensor_tensor(
                    out=rop[:].rearrange("p (h w) -> p h w", w=64),
                    in0=rop[:].rearrange("p (h w) -> p h w", w=64),
                    in1=cos3, op=ALU.mult), inc=vs)
                ctx.wait("vector", r3)
                r4 = ctx.op("vector", lambda e: e.tensor_tensor(
                    out=rop[:], in0=rop[:], in1=sq[:], op=ALU.add), inc=vs)
                ctx.wait("scalar", r4)
                cst = ctx.op("scalar", lambda e: e.copy(out=qb[:], in_=rop[:]),
                             inc=ss)
                ctx.wait("tensor", cst)
                for g in range(2):
                    ctx.wait("tensor", ptr3_free.get(g))
                    tk = None
                    for p in range(8):
                        tk = ctx.op("tensor", lambda e, p=p, g=g: e.transpose(
                            out=ptr3[g][:, p * 128:(p + 1) * 128],
                            in_=qb[:, (8 * g + p) * 128:(8 * g + p + 1) * 128],
                            identity=ctx.identb[:]), inc=ts if p == 7 else None)
                    ctx.wait("scalar", tk)
                    ptr3_free[g] = ctx.op("scalar", lambda e, g=g: e.copy(
                        out=qT[:, g * 1024:(g + 1) * 1024], in_=ptr3[g][:]),
                        inc=ss)
                qT_ready = ptr3_free[1]

                def emit_scores(h):
                    hg = t * NH + h
                    par = hg % 2
                    s4 = hg % 4
                    g = h // 4
                    base = 64 * (g % 2)
                    pc = (h // 8) * 4 + h % 4
                    lq = qT[base:base + 64, 128 * pc:128 * pc + 128]
                    rk = kTv[64 * (g % 2):64 * (g % 2) + 64, g // 2,
                             128 * t:128 * t + 640]
                    ctx.wait("tensor", exp_tok.get(hg - 2))
                    if h == 0:
                        ctx.wait("tensor", qT_ready)
                    if hg == 0:
                        ctx.wait("tensor", kT_ready)
                    ctx.op("tensor", lambda e, lq=lq, rk=rk, par=par: e.matmul(
                        out=psc[par][:, 0:512], lhsT=lq, rhs=rk[:, 0:512],
                        start=True, stop=False), inc=ts)
                    ctx.op("tensor", lambda e, lq=lq, rk=rk, par=par: e.matmul(
                        out=psc[par][:, 512:640], lhsT=lq, rhs=rk[:, 512:640],
                        start=True, stop=False), inc=ts)
                    ctx.op("tensor", lambda e, par=par, t=t: e.matmul(
                        out=psc[par][:, 0:512], lhsT=ctx.identb[:],
                        rhs=masks[:, t * 640:t * 640 + 512],
                        start=False, stop=True), inc=ts)
                    m2 = ctx.op("tensor", lambda e, par=par, t=t: e.matmul(
                        out=psc[par][:, 512:640], lhsT=ctx.identb[:],
                        rhs=masks[:, t * 640 + 512:t * 640 + 640],
                        start=False, stop=True), inc=ts)
                    ctx.wait("vector", m2)
                    ctx.wait("vector", exp_tok.get(hg - 4))
                    mx = ctx.op("vector", lambda e, par=par, s4=s4: e.tensor_reduce(
                        out=scmax[:, s4:s4 + 1], in_=psc[par][:, 0:640],
                        axis=AX.X, op=ALU.max, negate=True), inc=vs)
                    return mx

                def emit_exp(h, mx):
                    hg = t * NH + h
                    s4 = hg % 4
                    pb = hg % 3
                    par = hg % 2
                    ctx.wait("scalar", ptr_done.get(hg - 3))
                    ctx.wait("scalar", rc_tok.get(hg - 8))
                    ctx.wait("scalar", mx)
                    ex = ctx.op("scalar", lambda e, s4=s4, pb=pb, par=par,
                                h=h: e.activation(
                        out=probs[pb][:], in_=psc[par][:, 0:640], func=AF.Exp,
                        scale=1.0, bias=scmax[:, s4:s4 + 1],
                        accum_out=ssum[:, h % 8:h % 8 + 1]), inc=ss)
                    exp_tok[hg] = ex
                    return ex

                def emit_rc(h):
                    hg = t * NH + h
                    ctx.wait("vector", exp_tok[hg])
                    rc = ctx.op("vector", lambda e, h=h: e.reciprocal(
                        out=rec[:, h % 8:h % 8 + 1],
                        in_=ssum[:, h % 8:h % 8 + 1]), inc=vs)
                    rc_tok[hg] = rc

                def emit_use_a(h):
                    hg = t * NH + h
                    pb = hg % 3
                    pb2 = hg % 2
                    ctx.wait("tensor", exp_tok[hg])
                    ctx.wait("tensor", ptr3_free.get(pb2))
                    tk = None
                    for c in range(5):
                        tk = ctx.op("tensor", lambda e, c=c, pb=pb, pb2=pb2: e.transpose(
                            out=ptr3[pb2][:, c * 128:(c + 1) * 128],
                            in_=probs[pb][:, c * 128:(c + 1) * 128],
                            identity=ctx.identb[:]), inc=ts if c == 4 else None)
                    ptr_done[hg] = tk
                    ctx.wait("vector", tk)
                    ctx.wait("vector", pvm_tok.get(hg - 2))
                    pev = ctx.op("vector", lambda e, pb2=pb2: e.tensor_copy(
                        out=probsT[pb2][:], in_=ptr3[pb2][:, 0:640]), inc=vs)
                    ptr3_free[pb2] = pev
                    pev_tok[hg] = pev

                def emit_use_b(h):
                    hg = t * NH + h
                    g = h // 4
                    grp = (hg // 8) % 2
                    pb2 = hg % 2
                    ctx.wait("tensor", pev_tok[hg])
                    ctx.wait("tensor", grp_ev.get(hg // 8 - 2))
                    if hg == 0:
                        ctx.wait("tensor", v_ready)
                    pv = None
                    for c in range(5):
                        pv = ctx.op("tensor", lambda e, c=c, g=g, grp=grp, h=h,
                                    t=t, pb2=pb2: e.matmul(
                            out=ppv[grp][:, 64 * (h % 8):64 * (h % 8) + 64],
                            lhsT=probsT[pb2][:, c * 128:(c + 1) * 128],
                            rhs=vtm[:, (t + c) * 512 + g * 64:(t + c) * 512 + g * 64 + 64],
                            start=(c == 0), stop=(c == 4)),
                            inc=ts)
                    pvm_tok[hg] = pv
                    ctx.wait("vector", pv)
                    ctx.wait("vector", rc_tok[hg])
                    evv = ctx.op("vector", lambda e, h=h, grp=grp: e.tensor_scalar(
                        out=atm[:, 64 * h:64 * h + 64],
                        in0=ppv[grp][:, 64 * (h % 8):64 * (h % 8) + 64],
                        scalar1=rec[:, h % 8:h % 8 + 1], scalar2=None,
                        op0=ALU.mult), inc=vs)
                    pv_ev[hg] = evv
                    if h % 8 == 7:
                        grp_ev[hg // 8] = evv
                    return evv

                for h in range(NH):
                    if t >= 1:
                        ctx.pump("shard")
                    mx_h = emit_scores(h)
                    if h >= 1:
                        emit_rc(h - 1)
                    emit_exp(h, mx_h)
                    if h >= 3:
                        emit_use_b(h - 3)
                    if h >= 2:
                        emit_use_a(h - 2)
                emit_rc(NH - 1)
                emit_use_b(NH - 3)
                emit_use_a(NH - 2)
                emit_use_b(NH - 2)
                emit_use_a(NH - 1)
                last_ev = emit_use_b(NH - 1)

                ctx.wait("tensor", last_ev)
                for g in range(2):
                    ctx.wait("tensor", ptr3_free.get(g))
                    tk = None
                    for p in range(8):
                        tk = ctx.op("tensor", lambda e, p=p, g=g: e.transpose(
                            out=ptr3[g][:, p * 128:(p + 1) * 128],
                            in_=atm[:, (8 * g + p) * 128:(8 * g + p + 1) * 128],
                            identity=ctx.identb[:]), inc=ts if p == 7 else None)
                    ctx.wait("scalar", tk)
                    ptr3_free[g] = ctx.op("scalar", lambda e, g=g, t=t: e.copy(
                        out=attnTv[:, 8 * g:8 * g + 8, 128 * t:128 * (t + 1)],
                        in_=ptr3[g][:].rearrange("p (k w) -> p k w", w=128)),
                        inc=ss)
            ctx.drain("shard")
        attnT_ready = ctx.tok(ss)
        if DEBUG_DUMP:
            vtok = ctx.tok("sV")
            ctx.dump("a_qcur", dbg_qcur[:], vtok)
            ctx.dump("a_sq", dbg_sq[:], vtok)
            ctx.dump("a_msq", dbg_msq[:], vtok)
            ctx.dump("a_rst", dbg_rst[:], vtok)
            ctx.dump("a_ssumD", dbg_ssum[:], vtok)
            ctx.dump("a_smaxD", dbg_smax[:], vtok)
        ctx.dump("a_attnT", attnT[:], attnT_ready)
        ctx.dump("a_kT", kT[:], attnT_ready)
        ctx.dump("a_vtm", vtm[:], attnT_ready)
        ctx.barrier_all()

        with ExitStack() as pjs:
            ctx.scope = pjs
            ppj = [pjs.enter_context(nc.psum_tensor(
                f"a_ppj{i}", [128, 512], F32)) for i in range(8)]
            wob = [ctx.sb(f"a_wob{i}", [128, 512]) for i in range(2)]
            woc = [ctx.sb(f"a_woc{i}", [128, 512], BF16) for i in range(2)]
            fat = ctx.sb("a_fat", [128, NT * 2048], BF16)
            aptmp = ctx.sb("a_aptmp", [128, 2048])
            cast_tok, mm_tok = {}, {}
            prev_ev = None
            for dc in range(4):
                mt = None
                for p in range(16):
                    n = dc * 16 + p
                    par = n % 2
                    ctx.wait("sync", cast_tok.get(n - 2))
                    dw = ctx.dma("sync", wob[par][:],
                                 ctx.in_wout[p][:, dc * 512:(dc + 1) * 512], ds)
                    ctx.wait("scalar", dw)
                    ctx.wait("scalar", mm_tok.get(n - 2))
                    ck = ctx.op("scalar", lambda e, par=par: e.copy(
                        out=woc[par][:], in_=wob[par][:]), inc=ss)
                    cast_tok[n] = ck
                    ctx.wait("tensor", ck)
                    if n == 0:
                        ctx.wait("tensor", attnT_ready)
                    if p == 0:
                        ctx.wait("tensor", prev_ev)
                    for t in range(NT):
                        mt = ctx.op("tensor", lambda e, p=p, t=t, par=par: e.matmul(
                            out=ppj[t][:], lhsT=attnTv[:, p, 128 * t:128 * (t + 1)],
                            rhs=woc[par][:], start=(p == 0), stop=(p == 15)),
                            inc=ts)
                    mm_tok[n] = mt
                ctx.wait("scalar", mt)
                if res_toks:
                    ctx.wait("scalar", res_toks[-1])
                ev = None
                for t in range(NT):
                    ev = ctx.op("scalar", lambda e, t=t, dc=dc: e.copy(
                        out=fat[:, t * 2048 + dc * 512: t * 2048 + (dc + 1) * 512],
                        in_=ppj[t][:]), inc=ss)
                prev_ev = ev
            ctx.dump("a_fat", fat[:], prev_ev)
            fsrc = [(fat[:, t * 2048:(t + 1) * 2048], prev_ev, t) for t in range(NT)]
            res_toks = emit_postln_residual(ctx, "a0_po", fsrc, ctx.xres_ap, aptmp)
            ctx.xres_tok = res_toks[-1]
        ctx.scope = old_scope
    return res_toks


def emit_l1_attn(ctx):
    nc = ctx.nc
    vs, ss, ts, ds, gs = "sV", "sS", "sT", "WS", "SS"
    NC = 16

    res_toks = []
    with ExitStack() as ph:
        old_scope, ctx.scope = ctx.scope, ph
        lnb = ctx.sb("l1_lnb", [128, 2048], BF16)
        ln2T = ctx.sb("l1_ln2T", [128, KD * 1024], BF16)
        lnTv = ln2T[:].rearrange("p (k w) -> p k w", w=1024)
        qf = ctx.sb("l1_qf", [64, 1024])
        kf = ctx.sb("l1_kf", [64, 1024])
        zb = ctx.sb("l1_zb", [64, 1024])
        zd = ctx.sb("l1_zd", [64, 1024])
        az = ctx.sb("l1_az", [64, 1024])
        ex1 = ctx.sb("l1_ex1", [64, 1024])
        lgb = ctx.sb("l1_lgb", [64, 1024])
        dlt = ctx.sb("l1_dlt", [64, 1024])
        Lc = ctx.sb("l1_L", [64, 1024])
        v64 = ctx.sb("l1_v64", [64, NC * 128], BF16)
        atst = ctx.sb("l1_at", [64, NC * 64], BF16)
        qst = ctx.sb("l1_qst", [64, NC * 64], BF16)
        tmat = ctx.sb("l1_tmat", [64, NC * 128])
        pcs = ctx.sb("l1_pc", [64, NC])
        Lrel = ctx.sb("l1_lrel", [64, 64])
        Pfm = ctx.sb("l1_P", [64, 64])
        Png = ctx.sb("l1_Png", [64, 64])
        Pr2 = ctx.sb("l1_Pr2", [64, 64])
        kd_ = ctx.sb("l1_kd", [64, 64])
        qtb = ctx.sb("l1_qtb", [64, 64], BF16)
        ktb = ctx.sb("l1_ktb", [64, 64], BF16)
        k2b = ctx.sb("l1_k2b", [64, 64], BF16)
        k2t = ctx.sb("l1_k2t", [64, 64], BF16)
        Sa = ctx.sb("l1_Sa", [64, 128])
        Sbf = ctx.sb("l1_Sbf", [64, 128], BF16)
        pay = ctx.sb("l1_pay", [64, 129])
        gath = ctx.sb("l1_gath", [64, 3 * 129])
        aj = ctx.sb("l1_aj", [64, 1])
        mpj = ctx.sb("l1_mpj", [64, 128])
        obf = ctx.sb("l1_obf", [64, 128], BF16)
        ofm = ctx.sb("l1_ofm", [128, NC * 64], BF16)
        trim = ctx.sb("l1_trim", [64, 64], BF16)
        bbc = ctx.sb("l1_bbc", [64, 1])
        bdc = ctx.sb("l1_bdc", [64, 1])
        usec = ctx.sb("l1_use", [64, 6])
        wqc = ctx.sb("l1_wqc", [128, 1024], BF16)
        wkc = ctx.sb("l1_wkc", [128, 1024], BF16)
        wbc = ctx.sb("l1_wbc", [128, 1024], BF16)
        wdc = ctx.sb("l1_wdc", [128, 1024], BF16)
        wvc = ctx.sb("l1_wvc", [128, 2048], BF16)
        woc2 = ctx.sb("l1_woc", [128, 2048], BF16)
        wstg = [ctx.sb("l1_wstg0", [128, 2048])]
        fat = [ctx.sb(f"l1_fat{i}", [128, 2048], BF16) for i in range(2)]
        aptmp = ctx.sb("l1_aptmp", [128, 2048])

        dtr = ctx.dma("scalar", trim[:], ctx.in_trim[:], gs)
        db1 = ctx.dma("scalar", bbc[:], ctx.in_bb.unsqueeze(1), gs,
                      allow_slow_non_contiguous=True)
        db2 = ctx.dma("scalar", bdc[:], ctx.in_bd.unsqueeze(1), gs,
                      allow_slow_non_contiguous=True)
        du = ctx.dma("scalar", usec[:], ctx.in_l1use[:], gs)
        wlist = [(ctx.in_l1wq, wqc, 1024), (ctx.in_l1wk, wkc, 1024),
                 (ctx.in_l1wb, wbc, 1024), (ctx.in_l1wd, wdc, 1024),
                 (ctx.in_l1wv, wvc, 2048), (ctx.in_l1wo, woc2, 2048)]
        ctok = {}
        for i, (inp, dst, w_) in enumerate(wlist):
            ctx.wait("sync", ctok.get(i - 1))
            dw = ctx.dma("sync", wstg[0][:, 0:w_], inp[:], ds)
            ctx.wait("scalar", dw)
            ctok[i] = ctx.op("scalar", lambda e, d=dst, w_=w_: e.copy(
                out=d[:], in_=wstg[0][:, 0:w_]), inc=ss)
        w_rdy = ctok[len(wlist) - 1]

        with ExitStack() as trs:
            ptr2 = [trs.enter_context(nc.psum_tensor(
                "l1_ptr" + str(i), [128, 1024], BF16)) for i in range(2)]

            def trt(t, atok):
                ev = None
                for g in range(2):
                    par = (2 * t + g) % 2
                    ctx.wait("tensor", atok)
                    tk = None
                    for p in range(8):
                        tk = ctx.op("tensor", lambda e, p=p, g=g, par=par: e.transpose(
                            out=ptr2[par][:, p * 128:(p + 1) * 128],
                            in_=lnb[:, (8 * g + p) * 128:(8 * g + p + 1) * 128],
                            identity=ctx.identb[:]), inc=ts if p == 7 else None)
                    ctx.wait("scalar", tk)
                    ev = ctx.op("scalar", lambda e, t=t, g=g, par=par: e.copy(
                        out=lnTv[:, 8 * g:8 * g + 8, 128 * t:128 * (t + 1)],
                        in_=ptr2[par][:].rearrange("p (k w) -> p k w", w=128)),
                        inc=ss)
                    ctx.wait("tensor", ev)
                return ev

            emit_ln_tiles(ctx, "l1ln", lambda t: (ctx.xres_ap(t), ctx.xres_tok),
                          NT, lambda t: lnb[:], trt, guard_lag=1)
        lnT_rdy = ctx.tok(ss)

        with ExitStack() as prs:
            psp = prs.enter_context(nc.psum_tensor("l1_psp", [64, 512], F32))
            psp2 = prs.enter_context(nc.psum_tensor("l1_psp2", [64, 512], F32))
            psv = [prs.enter_context(nc.psum_tensor(
                f"l1_psv{i}", [64, 128], F32)) for i in range(2)]
            ctx.wait("tensor", lnT_rdy)
            ctx.wait("tensor", w_rdy)
            evp = None
            for dst, wsrc in ((qf, wqc), (kf, wkc), (zb, wbc), (zd, wdc)):
                for half in range(2):
                    pp = psp if half == 0 else psp2
                    ctx.wait("tensor", evp)
                    mm = None
                    for k in range(16):
                        mm = ctx.op("tensor", lambda e, k=k, half=half, pp=pp,
                                    w=wsrc: e.matmul(
                            out=pp[:], lhsT=w[:, k * 64:(k + 1) * 64],
                            rhs=lnTv[:, k, half * 512:(half + 1) * 512],
                            start=(k == 0), stop=(k == 15)), inc=ts)
                    ctx.wait("scalar", mm)
                    evp = ctx.op("scalar", lambda e, dst=dst, half=half, pp=pp: e.copy(
                        out=dst[:, half * 512:(half + 1) * 512], in_=pp[:]),
                        inc=ss)
            for c in range(NC):
                pp = psv[c % 2]
                ctx.wait("tensor", evp if c < 2 else vev[c - 2])
                mm = None
                for k in range(16):
                    mm = ctx.op("tensor", lambda e, k=k, c=c, pp=pp: e.matmul(
                        out=pp[:], lhsT=lnTv[:, k, c * 64:(c + 1) * 64],
                        rhs=wvc[:, k * 128:(k + 1) * 128],
                        start=(k == 0), stop=(k == 15)), inc=ts)
                ctx.wait("scalar", mm)
                t_ = ctx.op("scalar", lambda e, c=c, pp=pp: e.copy(
                    out=v64[:, c * 128:(c + 1) * 128], in_=pp[:]), inc=ss)
                if c == 0:
                    vev = {}
                vev[c] = t_
            proj_done = ctx.tok(ss)

        ctx.wait("vector", proj_done)
        ctx.wait("vector", db1)
        ctx.wait("vector", db2)
        g1 = ctx.op("vector", lambda e: e.tensor_scalar(
            out=zb[:], in0=zb[:], scalar1=bbc[:], scalar2=None, op0=ALU.add),
            inc=vs)
        g2 = ctx.op("vector", lambda e: e.tensor_scalar(
            out=zd[:], in0=zd[:], scalar1=bdc[:], scalar2=None, op0=ALU.add),
            inc=vs)
        for zsrc, dst, gtok, isb in ((zb, lgb, g1, True), (zd, dlt, g2, False)):
            ctx.wait("scalar", gtok)
            a1 = ctx.op("scalar", lambda e, z=zsrc: e.activation(
                out=az[:], in_=z[:], func=AF.Abs), inc=ss)
            ctx.wait("scalar", a1)
            a2 = ctx.op("scalar", lambda e: e.activation(
                out=ex1[:], in_=az[:], func=AF.Exp, scale=-1.0), inc=ss)
            ctx.wait("scalar", a2)
            a3 = ctx.op("scalar", lambda e: e.activation(
                out=az[:], in_=ex1[:], func=AF.Ln, bias=ctx.onecol[0:64, :],
                scale=1.0), inc=ss)
            ctx.wait("vector", a3)
            m1 = ctx.op("vector", lambda e, z=zsrc, isb=isb: e.tensor_scalar(
                out=ex1[:], in0=z[:], scalar1=0.0, scalar2=None,
                op0=(ALU.min if isb else ALU.max)), inc=vs)
            ctx.wait("vector", m1)
            m2 = ctx.op("vector", lambda e, d=dst, isb=isb: e.tensor_tensor(
                out=d[:], in0=ex1[:], in1=az[:],
                op=(ALU.subtract if isb else ALU.add)), inc=vs)
            ctx.wait("scalar", m2)
        lgb_rdy = m2
        ctx.wait("vector", lgb_rdy)
        mz = ctx.op("vector", lambda e: e.memset(az[:], 1.0), inc=vs)
        ctx.wait("vector", mz)
        Ltok = ctx.op("vector", lambda e: e.tensor_tensor_scan(
            out=Lc[:], data0=az[:], data1=lgb[:], initial=0.0,
            op0=ALU.mult, op1=ALU.add), inc=vs)

        with ExitStack() as cas:
            pat = cas.enter_context(nc.psum_tensor("l1_pat", [64, 64], F32))
            ptm = cas.enter_context(nc.psum_tensor("l1_ptm", [64, 128], F32))
            ptk = cas.enter_context(nc.psum_tensor("l1_ptk", [64, 64], BF16))
            ctx.wait("vector", Ltok)
            prev = {}
            for c in range(NC):
                cs = slice(c * 64, (c + 1) * 64)
                ctx.wait("vector", prev.get(("p3", c - 1)))
                if c == 0:
                    l1_ = ctx.op("vector", lambda e, cs=cs: e.tensor_scalar(
                        out=Lrel[:], in0=Lc[:, cs], scalar1=0.0, scalar2=None,
                        op0=ALU.subtract), inc=vs)
                else:
                    l1_ = ctx.op("vector", lambda e, cs=cs, c=c: e.tensor_scalar(
                        out=Lrel[:], in0=Lc[:, cs],
                        scalar1=Lc[:, c * 64 - 1:c * 64], scalar2=None,
                        op0=ALU.subtract), inc=vs)
                ctx.wait("scalar", l1_)
                ctx.wait("scalar", prev.get(("dve", c - 1)))
                p1 = ctx.op("scalar", lambda e: e.activation(
                    out=Pfm[:], in_=Lrel[:], func=AF.Exp), inc=ss)
                p2 = ctx.op("scalar", lambda e: e.activation(
                    out=Png[:], in_=Lrel[:], func=AF.Exp, scale=-1.0), inc=ss)
                p3 = ctx.op("scalar", lambda e: e.activation(
                    out=Pr2[:], in_=Lrel[:], func=AF.Exp, scale=-1.0,
                    bias=Lrel[:, 63:64]), inc=ss)
                prev[("p3", c)] = p3
                ctx.wait("vector", p3)
                kd1 = ctx.op("vector", lambda e, cs=cs: e.tensor_tensor(
                    out=kd_[:], in0=kf[:, cs], in1=dlt[:, cs], op=ALU.mult),
                    inc=vs)
                q1_ = ctx.op("vector", lambda e, cs=cs: e.tensor_tensor(
                    out=qtb[:], in0=qf[:, cs], in1=Pfm[:], op=ALU.mult), inc=vs)
                ctx.wait("vector", q1_)
                q2_ = ctx.op("vector", lambda e, cs=cs, c=c: e.tensor_copy(
                    out=qst[:, c * 64:(c + 1) * 64], in_=qtb[:]), inc=vs)
                ctx.wait("vector", kd1)
                k1_ = ctx.op("vector", lambda e: e.tensor_tensor(
                    out=ktb[:], in0=kd_[:], in1=Png[:], op=ALU.mult), inc=vs)
                ctx.wait("vector", prev.get(("k2ev", c - 1)))
                k2_ = ctx.op("vector", lambda e: e.tensor_tensor(
                    out=k2b[:], in0=kd_[:], in1=Pr2[:], op=ALU.mult), inc=vs)
                pcw = ctx.op("vector", lambda e, c=c: e.tensor_copy(
                    out=pcs[:, c:c + 1], in_=Pfm[:, 63:64]), inc=vs)
                prev[("pcw", c)] = pcw
                ctx.wait("tensor", k1_)
                ctx.wait("tensor", q1_)
                ctx.wait("tensor", prev.get(("atev", c - 1)))
                at_ = ctx.op("tensor", lambda e: e.matmul(
                    out=pat[:], lhsT=ktb[:], rhs=qtb[:], start=True, stop=True),
                    inc=ts)
                ctx.wait("vector", at_)
                ctx.wait("vector", dtr)
                atev = ctx.op("vector", lambda e, c=c: e.tensor_tensor(
                    out=atst[:, c * 64:(c + 1) * 64], in0=pat[:], in1=trim[:],
                    op=ALU.mult), inc=vs)
                prev[("atev", c)] = atev
                ctx.wait("tensor", k2_)
                ctx.wait("tensor", prev.get(("k2ev", c - 1)))
                tk2 = ctx.op("tensor", lambda e: e.transpose(
                    out=ptk[:], in_=k2b[:], identity=ctx.identb[0:64, 0:64]),
                    inc=ts)
                ctx.wait("scalar", tk2)
                k2ev = ctx.op("scalar", lambda e: e.copy(
                    out=k2t[:], in_=ptk[:]), inc=ss)
                prev[("k2ev", c)] = k2ev
                prev[("qtb", c)] = k2ev
                ctx.wait("tensor", k2ev)
                ctx.wait("tensor", proj_done)
                ctx.wait("tensor", prev.get(("tmev", c - 1)))
                tm_ = ctx.op("tensor", lambda e, c=c: e.matmul(
                    out=ptm[:], lhsT=k2t[:], rhs=v64[:, c * 128:(c + 1) * 128],
                    start=True, stop=True), inc=ts)
                ctx.wait("scalar", tm_)
                tmev = ctx.op("scalar", lambda e, c=c: e.copy(
                    out=tmat[:, c * 128:(c + 1) * 128], in_=ptm[:]), inc=ss)
                prev[("tmev", c)] = tmev
                ctx.wait("vector", tmev)
                ctx.wait("vector", prev.get(("pcw", c)))
                if c == 0:
                    sz = ctx.op("vector", lambda e: e.memset(Sa[:], 0.0), inc=vs)
                    ctx.wait("vector", sz)
                s1_ = ctx.op("vector", lambda e, c=c: e.tensor_scalar(
                    out=Sa[:], in0=Sa[:], scalar1=pcs[:, c:c + 1], scalar2=None,
                    op0=ALU.mult), inc=vs)
                ctx.wait("vector", s1_)
                s2_ = ctx.op("vector", lambda e, c=c: e.tensor_tensor(
                    out=Sa[:], in0=Sa[:], in1=tmat[:, c * 128:(c + 1) * 128],
                    op=ALU.add), inc=vs)
                ctx.wait("vector", s2_)
                prev[("dve", c)] = s2_
            pm = ctx.op("vector", lambda e: e.tensor_copy(
                out=pay[:, 0:128], in_=Sa[:]), inc=vs)
            ctx.wait("scalar", Ltok)
            pe_ = ctx.op("scalar", lambda e: e.activation(
                out=pay[:, 128:129], in_=Lc[:, 1023:1024], func=AF.Exp), inc=ss)
            ctx.wait("scalar", pm)
            ctx.wait("scalar", pe_)
            dpay = ctx.dma("scalar", ctx.cc_l1_in.ap(), pay[:], gs)
            ctx.wait("gpsimd", dpay)
            cc = ctx.op("gpsimd", lambda e: e.collective_compute(
                "AllGather", ALU.bypass,
                replica_groups=[[0, 1, 2, 3], [4, 5, 6, 7]],
                ins=[ctx.cc_l1_in.ap()], outs=[ctx.cc_l1_out.ap()]),
                inc="g_cc2", inc_n=1)
            ctx.wait("scalar", cc)
            dg_toks = []
            for j in range(3):
                dg_toks.append(ctx.dma("scalar", gath[:, j * 129:(j + 1) * 129],
                               ctx.cc_l1_out.ap()[j * 64:(j + 1) * 64, :], gs))
            ctx.waits("vector", dg_toks)
            ctx.wait("vector", du)
            sz2 = ctx.op("vector", lambda e: e.memset(Sa[:], 0.0), inc=vs)
            ctx.wait("vector", sz2)
            for j in range(3):
                a1_ = ctx.op("vector", lambda e, j=j: e.scalar_tensor_tensor(
                    out=aj[:], in0=gath[:, j * 129 + 128:j * 129 + 129],
                    scalar=usec[:, j:j + 1], in1=usec[:, 3 + j:4 + j],
                    op0=ALU.mult, op1=ALU.add), inc=vs)
                m1_ = ctx.op("vector", lambda e, j=j: e.tensor_scalar(
                    out=mpj[:], in0=gath[:, j * 129:j * 129 + 128],
                    scalar1=usec[:, j:j + 1], scalar2=None, op0=ALU.mult),
                    inc=vs)
                ctx.wait("vector", m1_)
                s1_ = ctx.op("vector", lambda e: e.tensor_scalar(
                    out=Sa[:], in0=Sa[:], scalar1=aj[:], scalar2=None,
                    op0=ALU.mult), inc=vs)
                ctx.wait("vector", s1_)
                s2_ = ctx.op("vector", lambda e: e.tensor_tensor(
                    out=Sa[:], in0=Sa[:], in1=mpj[:], op=ALU.add), inc=vs)
                ctx.wait("vector", s2_)
            combine_done = s2_

            pso = cas.enter_context(nc.psum_tensor("l1_pso", [64, 128], F32))
            pto = cas.enter_context(nc.psum_tensor("l1_pto", [128, 64], BF16))
            obev = None
            ctx.wait("vector", prev.get(("pcw", NC - 1)))
            ctx.wait("scalar", combine_done)
            for c in range(NC):
                sb_ = ctx.op("scalar", lambda e: e.copy(
                    out=Sbf[:], in_=Sa[:]), inc=ss)
                ctx.wait("tensor", sb_)
                ctx.wait("tensor", obev)
                o1 = ctx.op("tensor", lambda e, c=c: e.matmul(
                    out=pso[:], lhsT=atst[:, c * 64:(c + 1) * 64],
                    rhs=v64[:, c * 128:(c + 1) * 128], start=True, stop=False),
                    inc=ts)
                o2 = ctx.op("tensor", lambda e, c=c: e.matmul(
                    out=pso[:], lhsT=qst[:, c * 64:(c + 1) * 64], rhs=Sbf[:],
                    start=False, stop=True), inc=ts)
                ctx.wait("scalar", o2)
                oev = ctx.op("scalar", lambda e: e.copy(
                    out=obf[:], in_=pso[:]), inc=ss)
                ctx.wait("tensor", oev)
                to_ = ctx.op("tensor", lambda e: e.transpose(
                    out=pto[:, 0:64], in_=obf[:],
                    identity=ctx.identb[0:64, 0:64]), inc=ts)
                ctx.wait("scalar", to_)
                obev = ctx.op("scalar", lambda e, c=c: e.copy(
                    out=ofm[:, c * 64:(c + 1) * 64], in_=pto[:, 0:64]), inc=ss)
                ctx.wait("vector", o2)
                s1_ = ctx.op("vector", lambda e, c=c: e.tensor_scalar(
                    out=Sa[:], in0=Sa[:], scalar1=pcs[:, c:c + 1], scalar2=None,
                    op0=ALU.mult), inc=vs)
                ctx.wait("vector", s1_)
                s2_ = ctx.op("vector", lambda e, c=c: e.tensor_tensor(
                    out=Sa[:], in0=Sa[:], in1=tmat[:, c * 128:(c + 1) * 128],
                    op=ALU.add), inc=vs)
                ctx.wait("scalar", s2_)
            ofm_rdy = ctx.tok(ss)

        with ExitStack() as pjs:
            ctx.scope = pjs
            ppj = [pjs.enter_context(nc.psum_tensor(
                f"l1_ppj{i}", [128, 512], F32)) for i in range(2)]
            ev = None
            fguard = {}
            for t in range(NT):
                fpar = t % 2
                for dcw in range(4):
                    n = t * 4 + dcw
                    par = n % 2
                    ctx.wait("tensor", ofm_rdy)
                    ctx.wait("tensor", ev)
                    mm = ctx.op("tensor", lambda e, t=t, dcw=dcw, par=par: e.matmul(
                        out=ppj[par][:], lhsT=ofm[:, t * 128:(t + 1) * 128],
                        rhs=woc2[:, dcw * 512:(dcw + 1) * 512],
                        start=True, stop=True), inc=ts)
                    ctx.wait("scalar", mm)
                    if dcw == 0:
                        ctx.wait("scalar", fguard.get(t - 2))
                    ev = ctx.op("scalar", lambda e, dcw=dcw, fpar=fpar, par=par: e.copy(
                        out=fat[fpar][:, dcw * 512:(dcw + 1) * 512],
                        in_=ppj[par][:]), inc=ss)
                rt = emit_postln_residual(ctx, f"l1_po{t}",
                                          [(fat[fpar][:], ev, t)],
                                          ctx.xres_ap, aptmp)
                fguard[t] = rt[-1]
                res_toks.extend(rt)
            ctx.xres_tok = res_toks[-1]
        ctx.scope = old_scope
    return res_toks


def build(blocks):
    nc = bass.Bass("TRN2", target_bir_lowering=False)
    first = blocks[0] == "l0a"
    need_q = any(b in blocks for b in ("l0f", "l1f"))
    qwhich = [w for w, b in (("l0", "l0f"), ("l1", "l1f")) if b in blocks]

    with ExitStack() as stack:
        ctx = Ctx(nc, stack)
        ctx.dbg_toks = []
        dr = lambda n, s, d=F32, k="ExternalInput": nc.dram_tensor(n, s, d, kind=k).ap()

        if first:
            ctx.in_x = dr("x_in", [HALO + NTOK, D])
        else:
            ctx.in_x = dr("x_in", [NTOK, D])
        ctx.out_x = dr("x_out", [NTOK, D], F32, "ExternalOutput")
        ctx.in_ident = dr("ident", [128, 128], BF16)
        if "l0a" in blocks:
            ctx.in_wq = dr("wq_t", [16, 128, 2048])
            ctx.in_wkv = dr("wkv_t", [8, 128, 2048])
            ctx.in_wout = dr("wout_t", [16, 128, 2048])
            ctx.in_cos = dr("cos_t", [ATT * 128, 64])
            ctx.in_sin = dr("sin_t", [ATT * 128, 64])
            ctx.in_qnw = dr("qnw_t", [1, 2048])
            ctx.in_knw = dr("knw_t", [1, 512])
            ctx.in_masks = dr("masks_t", [NT, 128, 640], BF16)
            ctx.hbm_qtmp = nc.dram_tensor("hbm_qtmp", [NT, 128, 2048], BF16).ap()
        if "l1a" in blocks:
            ctx.in_l1wq = dr("l1wq", [128, 1024])
            ctx.in_l1wk = dr("l1wk", [128, 1024])
            ctx.in_l1wb = dr("l1wb", [128, 1024])
            ctx.in_l1wd = dr("l1wd", [128, 1024])
            ctx.in_l1wv = dr("l1wv", [128, 2048])
            ctx.in_l1wo = dr("l1wo", [128, 2048])
            ctx.in_bb = dr("l1bb", [64])
            ctx.in_bd = dr("l1bd", [64])
            ctx.in_trim = dr("l1trim", [64, 64], BF16)
            ctx.in_l1use = dr("l1use", [64, 6])
            ctx.cc_l1_in = nc.dram_tensor("cc_l1_in", [64, 129], F32)
            ctx.cc_l1_out = nc.dram_tensor("cc_l1_out", [256, 129], F32)
        if need_q:
            ctx.in_qsl = dr("qslices", [21, 128, 4096])
            ctx.in_qsrat = dr("qsrat", [1, 16])
            ctx.in_qseps = dr("qseps", [1, 16])
            ctx.cc_qs_in = nc.dram_tensor("cc_qs_in", [1, 16], F32)
            ctx.cc_qs_out = nc.dram_tensor("cc_qs_out", [1, 16], F32)
        if "l0f" in blocks:
            ctx.in_wup0 = dr("wup0s", [8, 128, 2048])
            ctx.in_wgt0 = dr("wgt0s", [8, 128, 2048])
            ctx.in_wdn0 = dr("wdn0s", [4, 4, 128, 1024])
            ctx.hbm_hid0 = nc.dram_tensor("hbm_hid0", [3, EC0 // 4, 128, 1024], BF16).ap()
            ctx.ag0_in = nc.dram_tensor("ag0_in", [24, 128, 2048], BF16)
            ctx.ag0_out = nc.dram_tensor("ag0_out", [192, 128, 2048], BF16,
                                         addr_space="Shared")
        if "l1f" in blocks:
            ctx.in_wup1 = dr("wup1s", [6, 128, 2048])
            ctx.in_wgt1 = dr("wgt1s", [6, 128, 2048])
            ctx.in_wdn1 = dr("wdn1s", [3, 4, 128, 1024])
            ctx.hbm_hid1 = nc.dram_tensor("hbm_hid1", [3, EC1 // 4, 128, 1024], BF16).ap()
            ctx.ag1_in = nc.dram_tensor("ag1_in", [18, 128, 2048], BF16)
            ctx.ag1_out = nc.dram_tensor("ag1_out", [144, 128, 2048], BF16,
                                         addr_space="Shared")

        xres = ctx.sb("xres", [128, NT * 2048])
        ctx.xres_ap = lambda t: xres[:, t * 2048:(t + 1) * 2048]
        ctx.identb = ctx.sb("identb", [128, 128], BF16)
        ctx.eps5 = ctx.sb("eps5", [128, 1])
        ctx.eps6 = ctx.sb("eps6", [128, 1])
        ctx.onecol = ctx.sb("onecol", [128, 1])
        ctx.ones128c = ctx.sb("ones128c", [128, 1])
        ctx.ones1x128 = ctx.sb("ones1x128", [1, 128])
        if need_q:
            ctx.in_qsrat_sb = ctx.sb("qsrat_sb", [1, 16])
            ctx.in_qseps_sb = ctx.sb("qseps_sb", [1, 16])

        ctok = None
        for tens, v in ((ctx.eps5, 1e-5), (ctx.eps6, 1e-6), (ctx.onecol, 1.0),
                        (ctx.ones128c, 1.0), (ctx.ones1x128, 1.0)):
            ctok = ctx.op("vector", lambda e, tens=tens, v=v: e.memset(tens[:], v),
                          inc="sV")
        ctx.consts_tok = ctok
        itok = ctx.dma("scalar", ctx.identb[:], ctx.in_ident[:], "SS")
        xoff = HALO if first else 0
        for t in range(NT):
            ctx.dma("scalar", ctx.xres_ap(t),
                    ctx.in_x[xoff + t * 128: xoff + (t + 1) * 128, :],
                    "SS")
        ctx.xres_tok = ctx.stream_toks("SS")
        ctx.ident_tok = itok
        for e in ("vector", "scalar", "tensor"):
            ctx.wait(e, ctx.consts_tok)
            ctx.wait(e, itok)
        if need_q:
            ctx.thscols = ctx.sb("qs_ths", [128, 16])
            ctx.qs_acc = ctx.sb("qs_acc", [128, 8])
            ctx.qs_red = ctx.sb("qs_red", [128, 1])
            ctx.qs_cr = ctx.sb("qs_cr", [1, 8])
            ctx.qs_tot = ctx.sb("qs_tot", [1, 16])
            q1 = ctx.dma("scalar", ctx.in_qsrat_sb[:], ctx.in_qsrat[:], "SS")
            q2 = ctx.dma("scalar", ctx.in_qseps_sb[:], ctx.in_qseps[:], "SS")
            ctx.qsrat_tok, ctx.qseps_tok = q1, q2
            lis = [li for li, bb in ((0, "l0f"), (1, "l1f")) if bb in blocks]
            gsc = gen_quant_scales(ctx, qwhich)
            gsh = gen_shard_all(ctx, lis)
            if "l0a" in blocks:
                ctx.gens["scales"] = gsc
                ctx.gens["shard"] = gsh
            else:
                qs_stack = ExitStack()
                old_scope, ctx.scope = ctx.scope, qs_stack
                for _ in gsc:
                    pass
                for _ in gsh:
                    pass
                ctx.scope = old_scope
                qs_stack.close()
                ctx.barrier_all()

        final_toks = [ctx.xres_tok]
        for b in blocks:
            if b == "l0a":
                final_toks = emit_l0_attn(ctx)
                assert not any(ctx.gens.values()), "prologue gens not drained"
            elif b == "l0f":
                final_toks = emit_ffn(ctx, 0)
            elif b == "l1a":
                final_toks = emit_l1_attn(ctx)
            elif b == "l1f":
                final_toks = emit_ffn(ctx, 1)
            ctx.xres_tok = final_toks[-1]
            ctx.barrier_all()

        ctx.waits("scalar", final_toks)
        ctx.waits("scalar", ctx.dbg_toks)
        for t in range(NT):
            ctx.dma("scalar", ctx.out_x[t * 128:(t + 1) * 128, :],
                    ctx.xres_ap(t), "SS")
        ctx.waits("scalar", ctx.stream_toks("SS"))

        block = stack.enter_context(nc.Block())
        ctx.run(block)
    return nc


def _pret_upgate(W, EC, ECpad):
    H = W.reshape(EC, 128, 16, 128).transpose(0, 3, 2, 1)
    H = np.ascontiguousarray(H.reshape(EC, 128, 2048))
    if ECpad > EC:
        H = np.concatenate(
            [H, np.zeros((ECpad - EC, 128, 2048), np.float32)], 0)
    return H


def _pret_down(Wd, EC, E2pad):
    H = Wd.reshape(4, 512, EC, 128).transpose(0, 2, 3, 1)
    H = H.reshape(4, EC // 2, 2, 128, 512).transpose(1, 0, 3, 2, 4)
    H = np.ascontiguousarray(H.reshape(EC // 2, 4, 128, 1024))
    if E2pad > EC // 2:
        H = np.concatenate(
            [H, np.zeros((E2pad - EC // 2, 4, 128, 1024), np.float32)], 0)
    return H


def _qslices(core, mats):
    out = np.zeros((21, 128, 4096), np.float32)
    idx = 0
    for arr, npc in mats:
        rows = arr.shape[0] // NCORES
        sl = np.ascontiguousarray(arr[core * rows:(core + 1) * rows]).reshape(-1)
        buf = np.zeros(npc * 128 * 4096, np.float32)
        buf[:sl.size] = sl
        out[idx:idx + npc] = buf.reshape(npc, 128, 4096)
        idx += npc
    return out


def host_prep(inputs):
    wqkv = np.asarray(inputs["L0_attn_w_qkv"], np.float32)
    G = np.ascontiguousarray(wqkv.reshape(3072, 16, 128).transpose(2, 1, 0))
    hA = [h for h in range(NH) if (h // 4) % 2 == 0]
    hB = [h for h in range(NH) if (h // 4) % 2 == 1]
    perm = []
    for p in range(16):
        perm.extend(range(hA[p] * 64, hA[p] * 64 + 64))
        perm.extend(range(hB[p] * 64, hB[p] * 64 + 64))
    wq_t = np.ascontiguousarray(
        G[:, :, :2048][:, :, perm].reshape(128, 16, 16, 128)
        .transpose(2, 0, 1, 3).reshape(16, 128, 2048))
    wkv_t = np.ascontiguousarray(
        G[:, :, 2048:].reshape(128, 16, 8, 128).transpose(2, 0, 1, 3)
        .reshape(8, 128, 2048))
    wout_t = np.ascontiguousarray(
        np.asarray(inputs["L0_attn_w_out"], np.float32).T.reshape(16, 128, 2048))
    qnw_t = np.tile(np.asarray(inputs["L0_attn_q_norm_w"], np.float32), 32)[None, :]
    knw_t = np.tile(np.asarray(inputs["L0_attn_k_norm_w"], np.float32), 8)[None, :]
    inv = 1.0 / 10000.0 ** (np.arange(0, 64, 2, dtype=np.float32) / 64.0)
    wup0 = _pret_upgate(np.asarray(inputs["L0_ffn_W_up"], np.float32), EC0, 64)
    wgt0 = _pret_upgate(np.asarray(inputs["L0_ffn_W_gate"], np.float32), EC0, 64)
    wdn0 = _pret_down(np.asarray(inputs["L0_ffn_W_down"], np.float32), EC0, 32)
    wup1 = _pret_upgate(np.asarray(inputs["L1_ffn_w1"], np.float32), EC1, 48)
    wgt1 = _pret_upgate(np.asarray(inputs["L1_ffn_w2"], np.float32), EC1, 48)
    wdn1 = _pret_down(np.asarray(inputs["L1_ffn_w3"], np.float32), EC1, 24)
    ident = np.eye(128, dtype=BF)

    rat = np.zeros((1, 16), np.float32)
    epsr = np.zeros((1, 16), np.float32)
    for m, nel in ((0, E0 * D), (1, E0 * D), (2, E0 * D),
                   (3, E1 * D), (4, E1 * D), (5, E1 * D)):
        rat[0, m] = 0.5 / nel
        rat[0, 8 + m] = 1.0 / nel
        epsr[0, m] = 0.5e-8 / NCORES
        epsr[0, 8 + m] = 1e-8 / NCORES

    qs_mats = [(np.asarray(inputs["L0_ffn_W_up"], np.float32), 4),
               (np.asarray(inputs["L0_ffn_W_gate"], np.float32), 4),
               (np.asarray(inputs["L0_ffn_W_down"], np.float32), 4),
               (np.asarray(inputs["L1_ffn_w1"], np.float32), 3),
               (np.asarray(inputs["L1_ffn_w2"], np.float32), 3),
               (np.asarray(inputs["L1_ffn_w3"], np.float32), 3)]

    def _ptile(W, cols):
        return np.ascontiguousarray(
            W.reshape(16, 128, cols).transpose(1, 0, 2).reshape(128, 16 * cols))

    l1wq = _ptile(np.asarray(inputs["L1_attn_Wq"], np.float32), 64)
    l1wk = _ptile(np.asarray(inputs["L1_attn_Wk"], np.float32), 64)
    l1wb = _ptile(np.asarray(inputs["L1_attn_Wb"], np.float32), 64)
    l1wd = _ptile(np.asarray(inputs["L1_attn_Wd"], np.float32), 64)
    l1wv = _ptile(np.asarray(inputs["L1_attn_Wv"], np.float32), 128)
    l1wo = np.ascontiguousarray(np.asarray(inputs["L1_attn_Wo"], np.float32))
    trim = (np.arange(64)[:, None] <= np.arange(64)[None, :]).astype(BF)

    percore = []
    x = np.asarray(inputs["L0_attn_x"], np.float32)
    for c in range(NCORES):
        b, qt = c // 4, c % 4
        xcore = np.zeros((HALO + NTOK, D), np.float32)
        lo = qt * NTOK - HALO
        src_lo = max(lo, 0)
        xcore[src_lo - lo:, :] = x[b, src_lo: qt * NTOK + NTOK, :]
        pos = np.arange(qt * NTOK - HALO, qt * NTOK + NTOK, dtype=np.float32)
        ang = pos[:, None] * inv[None, :]
        cos_t = np.ascontiguousarray(
            np.concatenate([np.cos(ang), np.cos(ang)], 1).astype(np.float32))
        sin_t = np.ascontiguousarray(
            np.concatenate([-np.sin(ang), np.sin(ang)], 1).astype(np.float32))
        masks = np.full((NT, 128, 640), NEG, np.float32)
        for i in range(NT):
            r = np.arange(128)[:, None]
            cc = np.arange(640)[None, :]
            keyg = qt * NTOK - HALO + 128 * i + cc
            valid = (cc >= r + 1) & (cc <= r + 512) & (keyg >= 0)
            masks[i][valid.squeeze() if valid.ndim > 2 else valid] = 0.0
        percore.append({
            "x_first": xcore,
            "ident": ident,
            "l1wq": l1wq, "l1wk": l1wk, "l1wb": l1wb, "l1wd": l1wd,
            "l1wv": l1wv, "l1wo": l1wo, "l1trim": trim,
            "l1bb": np.asarray(inputs["L1_attn_bb"], np.float32),
            "l1bd": np.asarray(inputs["L1_attn_bd"], np.float32),
            "l1use": np.concatenate([
                (np.arange(3) < qt).astype(np.float32),
                1.0 - (np.arange(3) < qt).astype(np.float32)])[None, :]
                .repeat(64, 0).copy(),
            "wq_t": wq_t, "wkv_t": wkv_t, "wout_t": wout_t,
            "qnw_t": qnw_t, "knw_t": knw_t,
            "cos_t": cos_t, "sin_t": sin_t,
            "masks_t": masks.astype(BF),
            "qslices": _qslices(c, qs_mats),
            "qsrat": rat, "qseps": epsr,
            "wup0s": np.ascontiguousarray(wup0[c * 8:(c + 1) * 8]),
            "wgt0s": np.ascontiguousarray(wgt0[c * 8:(c + 1) * 8]),
            "wdn0s": np.ascontiguousarray(wdn0[c * 4:(c + 1) * 4]),
            "wup1s": np.ascontiguousarray(wup1[c * 6:(c + 1) * 6]),
            "wgt1s": np.ascontiguousarray(wgt1[c * 6:(c + 1) * 6]),
            "wdn1s": np.ascontiguousarray(wdn1[c * 3:(c + 1) * 3]),
        })
    return percore


def in_map_for(blocks, percore, x_cur):
    first = blocks[0] == "l0a"
    need_q = any(b in blocks for b in ("l0f", "l1f"))
    maps = []
    for c in range(NCORES):
        b, qt = c // 4, c % 4
        m = {"ident": percore[c]["ident"]}
        if "l0a" in blocks:
            for k in ("wq_t", "wkv_t", "wout_t", "qnw_t", "knw_t", "cos_t",
                      "sin_t", "masks_t"):
                m[k] = percore[c][k]
        if first:
            m["x_in"] = percore[c]["x_first"]
        else:
            m["x_in"] = np.ascontiguousarray(x_cur[b, qt * NTOK:(qt + 1) * NTOK, :])
        if need_q:
            m["qslices"] = percore[c]["qslices"]
            m["qsrat"] = percore[c]["qsrat"]
            m["qseps"] = percore[c]["qseps"]
        if "l1a" in blocks:
            for k in ("l1wq", "l1wk", "l1wb", "l1wd", "l1wv", "l1wo",
                      "l1bb", "l1bd", "l1trim", "l1use"):
                m[k] = percore[c][k]
        if "l0f" in blocks:
            for k in ("wup0s", "wgt0s", "wdn0s"):
                m[k] = percore[c][k]
        if "l1f" in blocks:
            for k in ("wup1s", "wgt1s", "wdn1s"):
                m[k] = percore[c][k]
        maps.append(m)
    return maps


def gather_out(res):
    out = np.zeros((B, S, D), np.float32)
    for c in range(NCORES):
        b, qt = c // 4, c % 4
        out[b, qt * NTOK:(qt + 1) * NTOK, :] = res.results[c]["x_out"]
    return out


def kernel(**inputs):
    percore = host_prep(inputs)
    x_cur = np.asarray(inputs["L0_attn_x"], np.float32)
    launches = [ALL_BLOCKS] if FUSED else UNFUSED
    for blocks in launches:
        nc = build(tuple(blocks))
        maps = in_map_for(blocks, percore, x_cur)
        res = run_bass_kernel_spmd(nc, maps, core_ids=list(range(NCORES)))
        x_cur = gather_out(res)
    return x_cur


# version 20 (final)
# speedup vs baseline: 1.0103x; 1.0041x over previous
import numpy as np
import ml_dtypes
from contextlib import ExitStack

import concourse.bass as bass
import concourse.mybir as mybir
from concourse.bass_utils import run_bass_kernel_spmd

F32 = mybir.dt.float32
BF16 = mybir.dt.bfloat16
AF = mybir.ActivationFunctionType
ALU = mybir.AluOpType
AX = mybir.AxisListType
BF = ml_dtypes.bfloat16

B, S, D = 2, 4096, 2048
NCORES = 8
NTOK = 1024
HALO = 512
NT = NTOK // 128
NHT = HALO // 128
ATT = NT + NHT
NH, NKV, HD = 32, 8, 64
E0, E1 = 8192, 5632
EC0, EC1 = E0 // 128, E1 // 128
KD = 16
NEG = -1.0e30

ENGS = ("sync", "vector", "scalar", "tensor", "gpsimd")

DEBUG_DUMP = False
ALL_BLOCKS = ("l0a", "l0f", "l1a", "l1f")
UNFUSED = [("l0a",), ("l0f",), ("l1a",), ("l1f",)]
FUSED = True


class Ctx:
    def __init__(self, nc, stack):
        self.nc = nc
        self.stack = stack
        self.scope = stack
        self.ops = {e: [] for e in ENGS}
        self._slots = {}
        self.handles = {}
        self.cnt = {}
        self.seen = {e: {} for e in ENGS}
        self.gens = {}
        self.q_stash = {}

    def _sem(self, name):
        if name not in self.handles:
            self.handles[name] = self.stack.enter_context(self.nc.semaphore(name))
            self.cnt[name] = 0
        return self.handles[name]

    def op(self, eng, fn, inc=None, inc_n=1):
        if inc is None:
            self.ops[eng].append(fn)
            return None
        h = self._sem(inc)
        self.cnt[inc] += inc_n
        tgt = self.cnt[inc]

        def wrapped(e, fn=fn, h=h, n=inc_n):
            fn(e).then_inc(h, n)

        self.ops[eng].append(wrapped)
        return (inc, tgt)

    def dma(self, eng, out, in_, stream, slots=4, **kw):
        st = self._slots.setdefault(stream, {"i": 0, "prev": {}})
        i = st["i"] % slots
        st["i"] += 1
        sem = f"{stream}.{i}"
        self.wait(eng, st["prev"].get(i))
        tok = self.op(eng, lambda e: e.dma_start(out=out, in_=in_, **kw), sem, 16)
        st["prev"][i] = tok
        return tok

    def stream_toks(self, stream):
        st = self._slots.get(stream)
        return list(st["prev"].values()) if st else []

    def wait(self, eng, token):
        if token is None:
            return
        if isinstance(token, (list, tuple)) and token and isinstance(token[0], tuple):
            for t in token:
                self.wait(eng, t)
            return
        sem, v = token
        if self.seen[eng].get(sem, 0) >= v:
            return
        self.seen[eng][sem] = v
        h = self._sem(sem)
        self.ops[eng].append(lambda e, h=h, v=v: e.wait_ge(h, v))

    def waits(self, eng, tokens):
        for t in tokens:
            self.wait(eng, t)

    def tok(self, sem):
        return (sem, self.cnt.get(sem, 0))

    def barrier_all(self):
        toks = [(s, self.cnt[s]) for s in list(self.cnt)
                if not (s.startswith("g_") or s.startswith("QO."))]
        for e in ENGS:
            for tk in toks:
                self.wait(e, tk)

    def pump(self, name, n=1):
        g = self.gens.get(name)
        if g is None:
            return
        try:
            for _ in range(n):
                next(g)
        except StopIteration:
            self.gens[name] = None

    def drain(self, name):
        g = self.gens.get(name)
        if g is not None:
            for _ in g:
                pass
            self.gens[name] = None

    def dump(self, name, ap, token, eng="scalar"):
        if not DEBUG_DUMP:
            return None
        t = self.nc.dram_tensor("dbg_" + name, list(ap.shape), ap.dtype,
                                kind="ExternalOutput").ap()
        self.wait(eng, token)
        dtok = self.dma(eng, t[:], ap, "SS")
        self.dbg_toks.append(dtok)
        return dtok

    def sb(self, name, shape, dt=F32):
        return self.scope.enter_context(self.nc.sbuf_tensor(name, shape, dt))

    def ps(self, name, shape, dt=F32):
        return self.scope.enter_context(self.nc.psum_tensor(name, shape, dt))

    def run(self, block):
        for eng in ENGS:
            ops = self.ops[eng]

            def section(e, ops=ops):
                for f in ops:
                    f(e)

            getattr(block, eng)(section)


def emit_ln_tiles(ctx, name, xsrc, n, apply_out, per_tile=None, guard_lag=2,
                  pre_apply=None):
    bnb = ctx.sb(f"{name}_bnb", [128, 24 * n])
    mv = ctx.sb(f"{name}_mv", [128, 2 * n])
    rst = ctx.sb(f"{name}_rst", [128, n])
    vsem, ssem = "sV", "sS"
    tok = None
    for t in range(n):
        ap, rdy = xsrc(t)
        ctx.wait("vector", rdy)
        for i in range(4):
            tok = ctx.op("vector", lambda e, ap=ap, t=t, i=i: e.bn_stats(
                out=bnb[:, t * 24 + i * 6: t * 24 + (i + 1) * 6],
                in_=ap[:, i * 512:(i + 1) * 512]), inc=vsem)
    ctx.wait("vector", tok)
    for t in range(n):
        tok = ctx.op("vector", lambda e, t=t: e.bn_aggr(
            out=mv[:, 2 * t:2 * t + 2], in_=bnb[:, 24 * t:24 * t + 24]), inc=vsem)
    ctx.wait("scalar", tok)
    varv = mv[:].rearrange("p (t two) -> p t two", two=2)[:, :, 1]
    t1 = ctx.op("scalar", lambda e: e.activation(
        out=rst[:], in_=varv, func=AF.Ln, bias=ctx.eps5[:], scale=1.0), inc=ssem)
    ctx.wait("scalar", t1)
    t2 = ctx.op("scalar", lambda e: e.activation(
        out=rst[:], in_=rst[:], func=AF.Exp, scale=-0.5), inc=ssem)
    ctx.wait("vector", t2)
    toks, guards = [], {}
    for t in range(n):
        ap, _ = xsrc(t)
        if t == 0 and pre_apply is not None:
            ctx.wait("vector", pre_apply)
        if t - guard_lag in guards:
            ctx.wait("vector", guards[t - guard_lag])
        ot = apply_out(t)
        atok = ctx.op("vector", lambda e, ap=ap, ot=ot, t=t: e.tensor_scalar(
            out=ot, in0=ap, scalar1=mv[:, 2 * t:2 * t + 1],
            scalar2=rst[:, t:t + 1], op0=ALU.subtract, op1=ALU.mult), inc=vsem)
        toks.append(atok)
        if per_tile is not None:
            guards[t] = per_tile(t, atok)
    return toks


def emit_postln_residual(ctx, name, fsrc_list, xres_ap, tmp):
    done = []

    def per_tile(t, atok):
        xt = xres_ap(fsrc_list[t][2])
        ctx.wait("vector", atok)
        rtok = ctx.op("vector", lambda e, xt=xt: e.tensor_tensor(
            out=xt, in0=xt, in1=tmp[:], op=ALU.add), inc="sV")
        done.append(rtok)
        return rtok

    emit_ln_tiles(ctx, name, lambda t: fsrc_list[t][:2], len(fsrc_list),
                  lambda t: tmp[:], per_tile, guard_lag=1)
    return done


def gen_quant_scales(ctx, which):
    P = 4
    buf = [ctx.sb(f"qs_buf{i}", [128, 1024]) for i in range(P)]
    acc, red, cr, tot = ctx.qs_acc, ctx.qs_red, ctx.qs_cr, ctx.qs_tot
    z = ctx.op("vector", lambda e: e.memset(acc[:], 0.0), inc="sV")
    base = {0: 0, 1: 4, 2: 8, 3: 12, 4: 15, 5: 18}
    pieces = []
    if "l0" in which:
        pieces += [(m, i) for m in (0, 1, 2) for i in range(4)]
    if "l1" in which:
        pieces += [(m, i) for m in (3, 4, 5) for i in range(3)]
    subs = [(m, i, s) for (m, i) in pieces for s in range(4)]
    rtoks, dtoks = {}, {}
    atok = [z]

    def red_job(k):
        m = subs[k][0]
        par = k % P
        ctx.wait("vector", dtoks[k])
        ctx.wait("vector", atok[0])
        rt = ctx.op("vector", lambda e, par=par: e.tensor_reduce(
            out=red[:], in_=buf[par][:], axis=AX.X, op=ALU.add,
            apply_absolute_value=True), inc="sV")
        rtoks[k] = rt
        ctx.wait("vector", rt)
        atok[0] = ctx.op("vector", lambda e, m=m: e.tensor_tensor(
            out=acc[:, m:m + 1], in0=acc[:, m:m + 1], in1=red[:], op=ALU.add),
            inc="sV")

    D = 2
    for n, (m, i, s) in enumerate(subs):
        par = n % P
        ctx.wait("gpsimd", rtoks.get(n - P))
        dtoks[n] = ctx.dma("gpsimd", buf[par][:],
                           ctx.in_qsl[base[m] + i][:, s * 1024:(s + 1) * 1024],
                           "QW", slots=P)
        if n >= D:
            red_job(n - D)
        yield
    for k in range(max(0, len(subs) - D), len(subs)):
        red_job(k)
    lastv = ctx.tok("sV")
    ctx.wait("gpsimd", lastv)
    crt = ctx.op("gpsimd", lambda e: e.tensor_reduce(
        out=cr[:], in_=acc[:], axis=AX.C, op=ALU.add), inc="gq")
    ctx.wait("vector", crt)
    t1 = ctx.op("vector", lambda e: e.tensor_tensor(
        out=tot[:].rearrange("p (a b) -> p a b", a=2),
        in0=cr[0:1, :].unsqueeze(1).broadcast_to([1, 2, 8]),
        in1=ctx.in_qsrat_sb[:].rearrange("p (a b) -> p a b", a=2),
        op=ALU.mult), inc="sV")
    ctx.wait("vector", t1)
    t2 = ctx.op("vector", lambda e: e.tensor_tensor(
        out=tot[:], in0=tot[:], in1=ctx.in_qseps_sb[:], op=ALU.add),
        inc="sV")
    ctx.wait("scalar", t2)
    d1 = ctx.dma("scalar", ctx.cc_qs_in.ap(), tot[:], "QO")
    ctx.wait("gpsimd", d1)
    c1 = ctx.op("gpsimd", lambda e: e.collective_compute(
        "AllReduce", ALU.add, replica_groups=[list(range(NCORES))],
        ins=[ctx.cc_qs_in.ap()], outs=[ctx.cc_qs_out.ap()]), inc="g_cc",
        inc_n=1)
    ctx.wait("gpsimd", c1)
    ctx.qs_ready = ctx.dma(
        "gpsimd", ctx.thscols[:],
        ctx.cc_qs_out.ap()[0:1, :].partition_broadcast(128).squeeze(1), "QO")


def gen_shard_all(ctx, lis):
    W = 512
    wqb = [ctx.sb(f"sq_w{i}", [128, W]) for i in range(3)]
    sgq = [ctx.sb(f"sq_s{i}", [128, W], BF16) for i in range(3)]
    tqb = [ctx.sb(f"sq_t{i}", [128, W], BF16) for i in range(3)]
    jobs = []
    for li in lis:
        o = 0 if li == 0 else 3
        ths = [ctx.thscols[:, o + m:o + m + 1] for m in range(3)]
        ECb = 8 if li == 0 else 6
        E2b = 4 if li == 0 else 3
        wup = ctx.in_wup0 if li == 0 else ctx.in_wup1
        wgt = ctx.in_wgt0 if li == 0 else ctx.in_wgt1
        wdn = ctx.in_wdn0 if li == 0 else ctx.in_wdn1
        agi = (ctx.ag0_in if li == 0 else ctx.ag1_in).ap()
        for i in range(ECb):
            for h2 in range(2048 // W):
                jobs.append((wup[i][:, h2 * W:(h2 + 1) * W], ths[0],
                             agi[i][:, h2 * W:(h2 + 1) * W], None))
        for i in range(ECb):
            for h2 in range(2048 // W):
                jobs.append((wgt[i][:, h2 * W:(h2 + 1) * W], ths[1],
                             agi[ECb + i][:, h2 * W:(h2 + 1) * W], None))
        for j in range(E2b):
            for dc in range(4):
                idx = j * 4 + dc
                for h2 in range(1024 // W):
                    jobs.append((wdn[j, dc][:, h2 * W:(h2 + 1) * W], ths[2],
                                 agi[2 * ECb + idx // 2]
                                 [:, (idx % 2) * 1024 + h2 * W:
                                  (idx % 2) * 1024 + (h2 + 1) * W],
                                 None))
        jobs[-1] = jobs[-1][:3] + (li,)
    ctx.wait("sync", ctx.qs_ready)
    qtok, stok = {}, {}
    stash = []
    for n, (srcap, th, dst, li_end) in enumerate(jobs):
        par = n % 3
        ctx.wait("sync", qtok.get(n - 3))
        dw = ctx.dma("sync", wqb[par][:], srcap, "QW", slots=6)
        ctx.wait("scalar", dw)
        ctx.wait("scalar", qtok.get(n - 3))
        sg = ctx.op("scalar", lambda e, par=par, th=th: e.activation(
            out=sgq[par][:], in_=wqb[par][:], func=AF.Sign,
            bias=th, scale=1.0), inc="sS")
        ctx.wait("vector", sg)
        ctx.wait("vector", stok.get(n - 3))
        qt_ = ctx.op("vector", lambda e, par=par, th=th: e.scalar_tensor_tensor(
            out=tqb[par][:], in0=wqb[par][:], scalar=th,
            in1=sgq[par][:], op0=ALU.is_gt, op1=ALU.min), inc="sV")
        qtok[n] = qt_
        ctx.wait("scalar", qt_)
        stok[n] = ctx.dma("scalar", dst, tqb[par][:], "QB")
        stash.append(stok[n])
        if li_end is not None:
            agi_t = ctx.ag0_in if li_end == 0 else ctx.ag1_in
            ago_t = ctx.ag0_out if li_end == 0 else ctx.ag1_out
            ctx.waits("gpsimd", list(stash))
            stash = []
            ctx.op("gpsimd", lambda e, agi=agi_t, ago=ago_t: e.collective_compute(
                "AllGather", ALU.bypass, replica_groups=[list(range(NCORES))],
                ins=[agi.ap()], outs=[ago.ap()]),
                inc=f"g_ag{li_end}", inc_n=1)
        yield


def emit_ffn(ctx, li):
    nc = ctx.nc
    E = E0 if li == 0 else E1
    EC = EC0 if li == 0 else EC1
    pre = f"f{li}"
    vs, ss, ts, ds, gs = "sV", "sS", "sT", "WS", "SS"
    o = 0 if li == 0 else 3
    th_up = ctx.thscols[:, o:o + 1]
    th_gt = ctx.thscols[:, o + 1:o + 2]
    th_dn = ctx.thscols[:, o + 2:o + 3]
    s_up = ctx.thscols[:, 8 + o:9 + o]
    s_gt = ctx.thscols[:, 9 + o:10 + o]
    s_dn = ctx.thscols[:, 10 + o:11 + o]

    hbm_hid = ctx.hbm_hid0 if li == 0 else ctx.hbm_hid1
    agv = (ctx.ag0_out if li == 0 else ctx.ag1_out).ap()
    ag_tok = ("g_ag0" if li == 0 else "g_ag1", 1)
    ECb = 8 if li == 0 else 6
    E2b = 4 if li == 0 else 3
    NS = 3 * ECb

    res_toks = []
    ctx.wait("vector", ctx.qs_ready)
    ctx.wait("scalar", ctx.qs_ready)
    with ExitStack() as ph:
        old_scope, ctx.scope = ctx.scope, ph
        hid = ctx.sb(pre + "_hid", [128, EC * 256], BF16)
        aptmp = ctx.sb(pre + "_aptmp", [128, 2048])
        mls = ExitStack()
        ctx.scope = mls
        lnb = ctx.sb(pre + "_lnb", [128, 2048], BF16)
        lnT = ctx.sb(pre + "_lnT", [128, KD * 1024], BF16)
        lnTv = lnT[:].rearrange("p (k w) -> p k w", w=1024)

        with ExitStack() as trs:
            ptr = [trs.enter_context(nc.psum_tensor(pre + f"_ptr{i}", [128, 1024], BF16))
                   for i in range(2)]

            def tr_tile(t, atok):
                last_ev = None
                for g in range(2):
                    par = (2 * t + g) % 2
                    ctx.wait("tensor", atok)
                    tk = None
                    for j in range(8):
                        tk = ctx.op("tensor", lambda e, j=j, g=g, par=par: e.transpose(
                            out=ptr[par][:, j * 128:(j + 1) * 128],
                            in_=lnb[:, (8 * g + j) * 128:(8 * g + j + 1) * 128],
                            identity=ctx.identb[:]), inc=ts if j == 7 else None)
                    ctx.wait("scalar", tk)
                    last_ev = ctx.op("scalar", lambda e, t=t, g=g, par=par: e.copy(
                        out=lnTv[:, 8 * g:8 * g + 8, 128 * t:128 * (t + 1)],
                        in_=ptr[par][:].rearrange("p (k w) -> p k w", w=128)),
                        inc=ss)
                    ctx.wait("tensor", last_ev)
                return last_ev

            emit_ln_tiles(ctx, pre + "ln",
                          lambda t: (ctx.xres_ap(t), ctx.xres_tok),
                          NT, lambda t: lnb[:], tr_tile, guard_lag=1)
        lnT_ready = ctx.tok(ss)
        ctx.dump(pre + "lnT", lnT[:], lnT_ready)

        with ExitStack() as ugs:
            ctx.scope = ugs
            pps = lambda n: ugs.enter_context(nc.psum_tensor(pre + n, [128, 512], F32))
            psu = [[pps(f"_pu{p}{hf}") for hf in range(2)] for p in range(2)]
            psg = [[pps(f"_pg{p}{hf}") for hf in range(2)] for p in range(2)]
            wub = [ctx.sb(pre + f"_wub{i}", [128, 2048], BF16) for i in range(3)]
            wgb = [ctx.sb(pre + f"_wgb{i}", [128, 2048], BF16) for i in range(3)]
            sil = [ctx.sb(pre + f"_sil{i}", [128, 512]) for i in range(2)]
            hstg = [ctx.sb(pre + f"_hst{i}", [128, 1024], BF16) for i in range(3)]

            ctx.wait("sync", ctx.qs_ready)
            ctx.wait("sync", ag_tok)
            mm_par = {}
            glu_toks = {}
            glu_par = {}
            stash_par = {}
            hid_last = None
            for ec in range(EC):
                par = ec % 3
                r, i = divmod(ec, ECb)
                ctx.wait("sync", mm_par.get(par))
                du = ctx.dma("sync", wub[par][:], agv[r * NS + i], ds)
                dg = ctx.dma("sync", wgb[par][:], agv[r * NS + ECb + i], ds)
                ctx.wait("tensor", du)
                ctx.wait("tensor", dg)
                ctx.wait("tensor", glu_toks.get(ec - 2))
                if ec == 0:
                    ctx.wait("tensor", lnT_ready)
                mt = None
                for kk in range(16):
                    st = (kk == 0)
                    sp = (kk == 15)
                    for hf in range(2):
                        ctx.op("tensor", lambda e, par=par, kk=kk, st=st, sp=sp,
                               pb=psu[ec % 2][hf], hf=hf: e.matmul(
                            out=pb[:], lhsT=wub[par][:, kk * 128:(kk + 1) * 128],
                            rhs=lnTv[:, kk, hf * 512:(hf + 1) * 512],
                            start=st, stop=sp), inc=ts)
                    for hf in range(2):
                        mt = ctx.op("tensor", lambda e, par=par, kk=kk, st=st,
                                    sp=sp, pb=psg[ec % 2][hf], hf=hf: e.matmul(
                            out=pb[:], lhsT=wgb[par][:, kk * 128:(kk + 1) * 128],
                            rhs=lnTv[:, kk, hf * 512:(hf + 1) * 512],
                            start=st, stop=sp), inc=ts)
                mm_par[par] = mt
                mm_done = mt
                gt = None
                for hf in range(2):
                    pb_u = psu[ec % 2][hf]
                    pb_g = psg[ec % 2][hf]
                    parS = hf
                    ctx.wait("scalar", mm_done)
                    ctx.wait("scalar", glu_par.get(parS))
                    if li == 0:
                        sv = ctx.op("scalar", lambda e, pb=pb_u, parS=parS: e.activation(
                            out=sil[parS][:], in_=pb[:], func=AF.Silu,
                            scale=s_up), inc=ss)
                    else:
                        sv = ctx.op("scalar", lambda e, pb=pb_u, parS=parS: e.activation(
                            out=sil[parS][:], in_=pb[:], func=AF.Sigmoid), inc=ss)
                    ctx.wait("vector", sv)
                    for sub in range(2):
                        qi = hf * 2 + sub
                        qof = sub * 256
                        if qi == 0:
                            dst = hid[:, ec * 256:(ec + 1) * 256]
                        else:
                            if ec % 4 == 0:
                                ctx.wait("vector", stash_par.get(qi - 1))
                            dst = hstg[qi - 1][:, (ec % 4) * 256:(ec % 4 + 1) * 256]
                        gt = ctx.op("vector", lambda e, pb=pb_g, qof=qof, parS=parS,
                                    dst=dst, li=li: e.scalar_tensor_tensor(
                            out=dst, in0=pb[:, qof:qof + 256],
                            scalar=(s_gt if li == 0 else 1.0),
                            in1=sil[parS][:, qof:qof + 256],
                            op0=ALU.mult, op1=ALU.mult), inc=vs)
                        glu_par[parS] = gt
                        if qi == 0:
                            hid_last = gt
                        elif ec % 4 == 3:
                            ctx.wait("scalar", gt)
                            stash_par[qi - 1] = ctx.dma(
                                "scalar", hbm_hid[qi - 1, ec // 4], hstg[qi - 1][:], gs)
                glu_toks[ec] = gt
            hid_ready = ctx.tok(vs)
            stash_done = ctx.stream_toks(gs)
            hd = ctx.dump(pre + "hid", hid[:], hid_ready)
            if hd is not None:
                hid_ready = hd
        ctx.barrier_all()
        ctx.scope = ph
        mls.close()

        with ExitStack() as dns:
            ctx.scope = dns
            pps = lambda n: dns.enter_context(nc.psum_tensor(pre + n, [128, 512], F32))
            psd = [[[pps(f"_pd{q_}{t_}{d_}") for d_ in range(2)]
                    for t_ in range(2)] for q_ in range(2)]
            dwq = [ctx.sb(pre + f"_dwq{i}", [128, 1024], BF16) for i in range(3)]
            hidB = ctx.sb(pre + "_hidB", [128, EC * 256], BF16)
            fout = [[ctx.sb(pre + f"_fo{q_}{t_}", [128, 2048], BF16)
                     for t_ in range(2)] for q_ in range(2)]
            hids = [hid, hidB]
            EC2 = EC // 2

            dmm3 = {}
            prev_ev = None
            dmm3_ec = {}
            for qp in range(2):
                hload = {0: {}, 1: {}}
                for qq in range(2):
                    quar = 2 * qp + qq
                    if quar == 0:
                        continue
                    for eb in range(EC // 4):
                        if quar == 1 and eb == 0:
                            ctx.wait("sync", stash_done)
                        ctx.wait("sync", dmm3_ec.get(2 * eb + 1))
                        hload[qq][eb] = ctx.dma(
                            "sync", hids[qq][:, eb * 1024:(eb + 1) * 1024],
                            hbm_hid[quar - 1, eb], ds)
                mt = None
                for dh in range(2):
                    for dc2 in range(2):
                        dc = 2 * dh + dc2
                        for ecp in range(EC2):
                            n = dc * EC2 + ecp
                            par3 = n % 3
                            r, j = divmod(ecp, E2b)
                            idx = j * 4 + dc
                            dsrc = agv[r * NS + 2 * ECb + idx // 2][
                                :, (idx % 2) * 1024:(idx % 2) * 1024 + 1024]
                            ctx.wait("sync", dmm3.get(par3))
                            dq = ctx.dma("sync", dwq[par3][:], dsrc, ds)
                            ctx.wait("tensor", dq)
                            if qp == 0 and dh == 0 and dc2 == 0 and ecp == 0:
                                ctx.wait("tensor", hid_ready)
                            if ecp == 0:
                                ctx.wait("tensor", prev_ev)
                            lt = None
                            for e2 in range(2):
                                ec = 2 * ecp + e2
                                st = (ecp == 0 and e2 == 0)
                                sp = (ecp == EC2 - 1 and e2 == 1)
                                for qq in range(2):
                                    if 2 * qp + qq > 0:
                                        ctx.wait("tensor", hload[qq].get(ec // 4))
                                    for tt in range(2):
                                        lt = ctx.op("tensor", lambda e, par3=par3,
                                                    qq=qq, tt=tt, dc2=dc2, ec=ec,
                                                    e2=e2, st=st, sp=sp: e.matmul(
                                            out=psd[qq][tt][dc2][:],
                                            lhsT=hids[qq][:, ec * 256 + tt * 128:
                                                          ec * 256 + (tt + 1) * 128],
                                            rhs=dwq[par3][:, e2 * 512:(e2 + 1) * 512],
                                            start=st, stop=sp), inc=ts)
                            mt = lt
                            dmm3[par3] = lt
                            if dh == 1 and dc2 == 1:
                                dmm3_ec[ecp] = lt
                    ev = None
                    if res_toks:
                        ctx.wait("scalar", res_toks[-1])
                    for qq in range(2):
                        for tt in range(2):
                            for dc2 in range(2):
                                dc = 2 * dh + dc2
                                ctx.wait("scalar", mt)
                                if li == 0:
                                    ev = ctx.op("scalar", lambda e, qq=qq, tt=tt,
                                                dc=dc, dc2=dc2: e.activation(
                                        out=fout[qq][tt][:, dc * 512:(dc + 1) * 512],
                                        in_=psd[qq][tt][dc2][:], func=AF.Copy,
                                        scale=s_dn, bias=0.0), inc=ss)
                                else:
                                    ev = ctx.op("scalar", lambda e, qq=qq, tt=tt,
                                                dc=dc, dc2=dc2: e.copy(
                                        out=fout[qq][tt][:, dc * 512:(dc + 1) * 512],
                                        in_=psd[qq][tt][dc2][:]), inc=ss)
                    prev_ev = ev
                for qq in range(2):
                    quar = 2 * qp + qq
                    fsrc = [(fout[qq][0][:], prev_ev, quar * 2 + 0),
                            (fout[qq][1][:], prev_ev, quar * 2 + 1)]
                    rt = emit_postln_residual(ctx, f"{pre}_po{quar}", fsrc,
                                              ctx.xres_ap, aptmp)
                    res_toks.extend(rt)
                    ctx.xres_tok = rt[-1]
        ctx.scope = old_scope
    return res_toks


def emit_l0_attn(ctx):
    nc = ctx.nc
    vs, ss, ts, ds, gs = "sV", "sS", "sT", "WS", "SS"

    res_toks = []
    with ExitStack() as ph:
        old_scope, ctx.scope = ctx.scope, ph
        vtm = ctx.sb("a_vtm", [128, ATT * 512], BF16)
        kT = ctx.sb("a_kT", [128, 4 * ATT * 128], BF16)
        kTv = kT[:].rearrange("p (g w) -> p g w", w=ATT * 128)
        cosb = ctx.sb("a_cos", [128, ATT * 64])
        sinb = ctx.sb("a_sin", [128, ATT * 64])
        knwb = ctx.sb("a_knw", [128, 512])
        msq = ctx.sb("a_msq", [128, 32])
        rst8 = ctx.sb("a_rst8", [128, 32])

        c1 = ctx.dma("scalar", knwb[:],
                     ctx.in_knw[0:1, :].partition_broadcast(128).squeeze(1), gs)
        c3 = None
        for j in range(ATT):
            ctx.dma("scalar", cosb[:, j * 64:(j + 1) * 64],
                    ctx.in_cos[j * 128:(j + 1) * 128, :], gs)
            c3 = ctx.dma("scalar", sinb[:, j * 64:(j + 1) * 64],
                         ctx.in_sin[j * 128:(j + 1) * 128, :], gs)
        consts_rdy = ctx.stream_toks(gs)

        with ExitStack() as s2:
            ctx.scope = s2
            ln0T = ctx.sb("a_ln0T", [128, ATT * 2048], BF16)
            with ExitStack() as s1:
                ctx.scope = s1
                xhalo = ctx.sb("a_xh", [128, 2048])
                lnb = ctx.sb("a_lnb", [128, 2048], BF16)
                ktmp = ctx.sb("a_ktmp", [128, ATT * 512], BF16)
                kbf = ctx.sb("a_kbf", [128, 512], BF16)
                k32 = ctx.sb("a_k32", [128, 512])
                ksq = ctx.sb("a_ksq", [128, 512])

                with ExitStack() as trs:
                    ptr2 = [trs.enter_context(nc.psum_tensor(
                        "a_ptr" + str(i), [128, 1024], BF16)) for i in range(2)]

                    def tr0(j, atok):
                        ev = None
                        for g in range(2):
                            par = (2 * j + g) % 2
                            ctx.wait("tensor", atok)
                            tk = None
                            for p in range(8):
                                tk = ctx.op("tensor", lambda e, p=p, g=g, par=par: e.transpose(
                                    out=ptr2[par][:, p * 128:(p + 1) * 128],
                                    in_=lnb[:, (8 * g + p) * 128:(8 * g + p + 1) * 128],
                                    identity=ctx.identb[:]), inc=ts if p == 7 else None)
                            ctx.wait("scalar", tk)
                            ev = ctx.op("scalar", lambda e, j=j, g=g, par=par: e.copy(
                                out=ln0T[:, j * 2048 + g * 1024: j * 2048 + (g + 1) * 1024],
                                in_=ptr2[par][:]), inc=ss)
                            ctx.wait("tensor", ev)
                        return ev

                    tr_cell = [None]

                    def tr0g(j, a):
                        tr_cell[0] = tr0(j, a)
                        return tr_cell[0]

                    prev_apply = [None]
                    for j in range(NHT):
                        if prev_apply[0] is not None:
                            ctx.wait("scalar", prev_apply[0])
                        dtok = ctx.dma("scalar", xhalo[:],
                                       ctx.in_x[j * 128:(j + 1) * 128, :], gs)
                        toks = emit_ln_tiles(ctx, f"a0lnh{j}",
                                             lambda t, dtok=dtok: (xhalo[:], dtok),
                                             1, lambda t: lnb[:],
                                             lambda t, a, j=j: tr0g(j, a),
                                             guard_lag=1, pre_apply=tr_cell[0])
                        prev_apply[0] = toks[-1]
                    emit_ln_tiles(ctx, "a0ln",
                                  lambda t: (ctx.xres_ap(t), ctx.xres_tok),
                                  NT, lambda t: lnb[:],
                                  lambda t, a: tr0g(t + NHT, a), guard_lag=1,
                                  pre_apply=tr_cell[0])
                ln0T_ready = ctx.tok(ss)

                with ExitStack() as kvps:
                    pkv = [kvps.enter_context(nc.psum_tensor(
                        f"a_pkv{i}", [128, 128], F32)) for i in range(2)]
                    wkvb = ctx.sb("a_wkv0", [128, 2048])
                    wkvc = [ctx.sb(f"a_wkvc{i}", [128, 2048], BF16) for i in range(2)]
                    cast_tok, ev_tok, mm_tok = {}, {}, {}
                    for hc in range(8):
                        par = hc % 2
                        ctx.wait("sync", cast_tok.get(hc - 1))
                        dw = ctx.dma("sync", wkvb[:], ctx.in_wkv[hc], ds)
                        ctx.wait("scalar", dw)
                        ctx.wait("scalar", mm_tok.get(hc - 2))
                        ck = ctx.op("scalar", lambda e, par=par: e.copy(
                            out=wkvc[par][:], in_=wkvb[:]), inc=ss)
                        cast_tok[hc] = ck
                        ctx.wait("tensor", ck)
                        if hc == 0:
                            ctx.wait("tensor", ln0T_ready)
                        wv = wkvc[par][:].rearrange("p (k w) -> p k w", w=128)
                        for j in range(ATT):
                            ctx.pump("scales")
                            pp = (hc * ATT + j) % 2
                            ctx.wait("tensor", ev_tok.get((hc * ATT + j) - 2))
                            mm = None
                            for k in range(16):
                                mm = ctx.op("tensor", lambda e, k=k, j=j, pp=pp, wv=wv: e.matmul(
                                    out=pkv[pp][:],
                                    lhsT=ln0T[:, j * 2048 + k * 128: j * 2048 + (k + 1) * 128],
                                    rhs=wv[:, k, :], start=(k == 0), stop=(k == 15)),
                                    inc=ts)
                            mm_tok[hc] = mm
                            ctx.wait("scalar", mm)
                            if hc < 4:
                                ev = ctx.op("scalar", lambda e, j=j, hc=hc, pp=pp: e.copy(
                                    out=ktmp[:, j * 512 + hc * 128: j * 512 + (hc + 1) * 128],
                                    in_=pkv[pp][:]), inc=ss)
                            else:
                                ev = ctx.op("scalar", lambda e, j=j, hc=hc, pp=pp: e.copy(
                                    out=vtm[:, j * 512 + (hc - 4) * 128: j * 512 + (hc - 3) * 128],
                                    in_=pkv[pp][:]), inc=ss)
                            ev_tok[hc * ATT + j] = ev
                ctx.drain("scales")
                v_ready = ctx.tok(ss)

                ctx.wait("vector", consts_rdy)
                with ExitStack() as kps:
                    pktr = kps.enter_context(nc.psum_tensor("a_pktr", [128, 512], BF16))
                    tr_ev = None
                    for j in range(ATT):
                        kap = ktmp[:, j * 512:(j + 1) * 512]
                        ctx.wait("scalar", v_ready)
                        sqk = ctx.op("scalar", lambda e, kap=kap: e.activation(
                            out=ksq[:], in_=kap, func=AF.Square), inc=ss)
                        ctx.wait("vector", sqk)
                        rd = ctx.op("vector", lambda e: e.tensor_reduce(
                            out=msq[:, 0:8],
                            in_=ksq[:].rearrange("p (h w) -> p h w", w=64),
                            axis=AX.X, op=ALU.add), inc=vs)
                        ctx.wait("scalar", rd)
                        l1 = ctx.op("scalar", lambda e: e.activation(
                            out=rst8[:, 0:8], in_=msq[:, 0:8], func=AF.Ln,
                            bias=ctx.eps6[:], scale=1.0 / 64), inc=ss)
                        ctx.wait("scalar", l1)
                        l2 = ctx.op("scalar", lambda e: e.activation(
                            out=rst8[:, 0:8], in_=rst8[:, 0:8], func=AF.Exp,
                            scale=-0.5), inc=ss)
                        ctx.wait("vector", l2)
                        a1 = ctx.op("vector", lambda e, kap=kap: e.tensor_tensor(
                            out=k32[:].rearrange("p (h w) -> p h w", w=64),
                            in0=kap.rearrange("p (h w) -> p h w", w=64),
                            in1=rst8[:, 0:8].unsqueeze(2).broadcast_to([128, 8, 64]),
                            op=ALU.mult), inc=vs)
                        ctx.wait("vector", a1)
                        a2 = ctx.op("vector", lambda e: e.tensor_tensor(
                            out=k32[:], in0=k32[:], in1=knwb[:], op=ALU.mult), inc=vs)
                        ctx.wait("vector", a2)
                        kv4 = k32[:].rearrange("p (h two w) -> p h two w", two=2, w=32)
                        t4 = ksq[:].rearrange("p (h two w) -> p h two w", two=2, w=32)
                        sin4 = sinb[:, j * 64:(j + 1) * 64].rearrange(
                            "p (two w) -> p two w", two=2).unsqueeze(1).broadcast_to(
                            [128, 8, 2, 32])
                        cos3 = cosb[:, j * 64:(j + 1) * 64].unsqueeze(1).broadcast_to(
                            [128, 8, 64])
                        r1 = ctx.op("vector", lambda e, t4=t4, kv4=kv4, sin4=sin4: e.tensor_tensor(
                            out=t4[:, :, 0, :], in0=kv4[:, :, 1, :],
                            in1=sin4[:, :, 0, :], op=ALU.mult), inc=vs)
                        r2 = ctx.op("vector", lambda e, t4=t4, kv4=kv4, sin4=sin4: e.tensor_tensor(
                            out=t4[:, :, 1, :], in0=kv4[:, :, 0, :],
                            in1=sin4[:, :, 1, :], op=ALU.mult), inc=vs)
                        ctx.wait("vector", r2)
                        r3 = ctx.op("vector", lambda e, cos3=cos3: e.tensor_tensor(
                            out=k32[:].rearrange("p (h w) -> p h w", w=64),
                            in0=k32[:].rearrange("p (h w) -> p h w", w=64),
                            in1=cos3, op=ALU.mult), inc=vs)
                        ctx.wait("vector", r3)
                        r4 = ctx.op("vector", lambda e: e.tensor_tensor(
                            out=k32[:], in0=k32[:], in1=ksq[:], op=ALU.add),
                            inc=vs)
                        ctx.wait("scalar", r4)
                        cst = ctx.op("scalar", lambda e: e.copy(
                            out=kbf[:], in_=k32[:]), inc=ss)
                        ctx.wait("tensor", cst)
                        ctx.wait("tensor", tr_ev)
                        tk = None
                        for g in range(4):
                            tk = ctx.op("tensor", lambda e, g=g: e.transpose(
                                out=pktr[:, g * 128:(g + 1) * 128],
                                in_=kbf[:, g * 128:(g + 1) * 128],
                                identity=ctx.identb[:]), inc=ts if g == 3 else None)
                        ctx.wait("scalar", tk)
                        tr_ev = ctx.op("scalar", lambda e, j=j: e.copy(
                            out=kTv[:, :, j * 128:(j + 1) * 128],
                            in_=pktr[:].rearrange("p (g w) -> p g w", w=128)),
                            inc=ss)
                        ctx.wait("tensor", tr_ev)
                kT_ready = ctx.tok(ss)
                ctx.scope = s2
                ctx.barrier_all()

            with ExitStack() as qps:
                ctx.scope = qps
                pq = [qps.enter_context(nc.psum_tensor(
                    f"a_pq{i}", [128, 128], F32)) for i in range(2)]
                qtmp = ctx.sb("a_qtmp", [128, NT * 2048], BF16)
                wqb = [ctx.sb(f"a_wqb{i}", [128, 2048]) for i in range(2)]
                wqc = [ctx.sb(f"a_wqc{i}", [128, 2048], BF16) for i in range(2)]
                cast_tok, ev_tok, mm_tok = {}, {}, {}
                for oc in range(16):
                    par = oc % 2
                    ctx.wait("sync", cast_tok.get(oc - 2))
                    dw = ctx.dma("sync", wqb[par][:], ctx.in_wq[oc], ds)
                    ctx.wait("scalar", dw)
                    ctx.wait("scalar", mm_tok.get(oc - 2))
                    ck = ctx.op("scalar", lambda e, par=par: e.copy(
                        out=wqc[par][:], in_=wqb[par][:]), inc=ss)
                    cast_tok[oc] = ck
                    ctx.wait("tensor", ck)
                    wv = wqc[par][:].rearrange("p (k w) -> p k w", w=128)
                    for t in range(NT):
                        pp = (oc * NT + t) % 2
                        ctx.wait("tensor", ev_tok.get((oc * NT + t) - 2))
                        mm = None
                        for k in range(16):
                            j = t + NHT
                            mm = ctx.op("tensor", lambda e, k=k, j=j, pp=pp, wv=wv: e.matmul(
                                out=pq[pp][:],
                                lhsT=ln0T[:, j * 2048 + k * 128: j * 2048 + (k + 1) * 128],
                                rhs=wv[:, k, :], start=(k == 0), stop=(k == 15)),
                                inc=ts)
                        mm_tok[oc] = mm
                        ctx.wait("scalar", mm)
                        ev = ctx.op("scalar", lambda e, t=t, oc=oc, pp=pp: e.copy(
                            out=qtmp[:, t * 2048 + oc * 128: t * 2048 + (oc + 1) * 128],
                            in_=pq[pp][:]), inc=ss)
                        ev_tok[oc * NT + t] = ev
                qmm_done = ctx.tok(ss)
                ctx.dump("a_qtmp", qtmp[:], qmm_done)
                ctx.dump("a_ln0T", ln0T[:], qmm_done)
                ctx.wait("scalar", qmm_done)
                for t in range(NT):
                    ctx.dma("scalar", ctx.hbm_qtmp[t],
                            qtmp[:, t * 2048:(t + 1) * 2048], gs)
                qstash = ctx.stream_toks(gs)
            ctx.scope = ph
        ctx.barrier_all()

        attnT = ctx.sb("a_attnT", [128, KD * 1024], BF16)
        attnTv = attnT[:].rearrange("p (k w) -> p k w", w=1024)

        with ExitStack() as ats:
            ctx.scope = ats
            rop = ctx.sb("a_rop", [128, 2048])
            sq = ctx.sb("a_sq", [128, 2048])
            qnwb = ctx.sb("a_qnw", [128, 2048])
            masks = ctx.sb("a_msk", [128, NT * 640], BF16)
            qb = ctx.sb("a_qb", [128, 2048], BF16)
            qT = ctx.sb("a_qT", [128, 2048], BF16)
            probs = [ctx.sb(f"a_probs{i}", [128, 640], BF16) for i in range(3)]
            probsT = [ctx.sb(f"a_probsT{i}", [128, 640], BF16) for i in range(2)]
            scmax = ctx.sb("a_scmax", [128, 4])
            ssum = ctx.sb("a_ssum", [128, 8])
            rec = ctx.sb("a_rec", [128, 8])
            atm = ctx.sb("a_atm", [128, 2048], BF16)
            qcur = [ctx.sb(f"a_qcur{i}", [128, 2048], BF16) for i in range(2)]
            cq = ctx.dma("scalar", qnwb[:],
                         ctx.in_qnw[0:1, :].partition_broadcast(128).squeeze(1), gs)
            cm = None
            for i in range(NT):
                cm = ctx.dma("scalar", masks[:, i * 640:(i + 1) * 640],
                             ctx.in_masks[i], gs)
            ctx.wait("vector", cm)
            ctx.wait("vector", cq)
            ctx.op("vector", lambda e: e.tensor_scalar(
                out=qnwb[:], in0=qnwb[:], scalar1=0.125, scalar2=None,
                op0=ALU.mult), inc="sV")
            psc = [ats.enter_context(nc.psum_tensor(
                f"a_psc{i}", [128, 1024], F32)) for i in range(2)]
            ppv = [ats.enter_context(nc.psum_tensor(
                f"a_ppv{i}", [128, 512], F32)) for i in range(2)]
            ptr3 = [ats.enter_context(nc.psum_tensor(
                f"a_ptr3{i}", [128, 1024], BF16)) for i in range(2)]
            exp_tok, ptr_done, pv_ev, grp_ev, rc_tok = {}, {}, {}, {}, {}
            pvm_tok, ptr3_free, pev_tok = {}, {}, {}
            qload, qfree = {}, {}
            if DEBUG_DUMP:
                dbg_qcur = ctx.sb("dbgb_qcur", [128, 2048], BF16)
                dbg_sq = ctx.sb("dbgb_sq", [128, 2048])
                dbg_msq = ctx.sb("dbgb_msq", [128, 32 * NT])
                dbg_rst = ctx.sb("dbgb_rst", [128, 32 * NT])
                dbg_ssum = ctx.sb("dbgb_ssum", [128, NH])
                dbg_smax = ctx.sb("dbgb_smax", [128, NH])
            for t in range(NT):
                qap = qcur[t % 2][:]
                ctx.wait("scalar", qstash)
                ctx.wait("scalar", qfree.get(t - 2))
                qload[t] = ctx.dma("scalar", qap, ctx.hbm_qtmp[t], gs)
                ctx.wait("scalar", qload[t])
                sqk = ctx.op("scalar", lambda e, qap=qap: e.activation(
                    out=sq[:], in_=qap, func=AF.Square), inc=ss)
                ctx.wait("vector", sqk)
                if DEBUG_DUMP and t == 0:
                    ctx.op("vector", lambda e, qap=qap: e.tensor_copy(
                        out=dbg_qcur[:], in_=qap), inc=vs)
                    ctx.op("vector", lambda e: e.tensor_copy(
                        out=dbg_sq[:], in_=sq[:]), inc=vs)
                rd = ctx.op("vector", lambda e: e.tensor_reduce(
                    out=msq[:], in_=sq[:].rearrange("p (h w) -> p h w", w=64),
                    axis=AX.X, op=ALU.add), inc=vs)
                ctx.wait("scalar", rd)
                l1 = ctx.op("scalar", lambda e: e.activation(
                    out=rst8[:], in_=msq[:], func=AF.Ln, bias=ctx.eps6[:],
                    scale=1.0 / 64), inc=ss)
                ctx.wait("scalar", l1)
                l2 = ctx.op("scalar", lambda e: e.activation(
                    out=rst8[:], in_=rst8[:], func=AF.Exp, scale=-0.5), inc=ss)
                ctx.wait("vector", l2)
                if DEBUG_DUMP:
                    ctx.op("vector", lambda e, t=t: e.tensor_copy(
                        out=dbg_msq[:, t * 32:(t + 1) * 32], in_=msq[:]), inc=vs)
                    ctx.op("vector", lambda e, t=t: e.tensor_copy(
                        out=dbg_rst[:, t * 32:(t + 1) * 32], in_=rst8[:]), inc=vs)
                a1 = ctx.op("vector", lambda e, qap=qap: e.tensor_tensor(
                    out=rop[:].rearrange("p (h w) -> p h w", w=64),
                    in0=qap.rearrange("p (h w) -> p h w", w=64),
                    in1=rst8[:].unsqueeze(2).broadcast_to([128, 32, 64]),
                    op=ALU.mult), inc=vs)
                ctx.wait("vector", a1)
                qfree[t] = a1
                a2 = ctx.op("vector", lambda e: e.tensor_tensor(
                    out=rop[:], in0=rop[:], in1=qnwb[:], op=ALU.mult), inc=vs)
                ctx.wait("vector", a2)
                q4 = rop[:].rearrange("p (h two w) -> p h two w", two=2, w=32)
                t4 = sq[:].rearrange("p (h two w) -> p h two w", two=2, w=32)
                jj = t + NHT
                sin4 = sinb[:, jj * 64:(jj + 1) * 64].rearrange(
                    "p (two w) -> p two w", two=2).unsqueeze(1).broadcast_to(
                    [128, 32, 2, 32])
                cos3 = cosb[:, jj * 64:(jj + 1) * 64].unsqueeze(1).broadcast_to(
                    [128, 32, 64])
                r1 = ctx.op("vector", lambda e, t4=t4, q4=q4, sin4=sin4: e.tensor_tensor(
                    out=t4[:, :, 0, :], in0=q4[:, :, 1, :], in1=sin4[:, :, 0, :],
                    op=ALU.mult), inc=vs)
                r2 = ctx.op("vector", lambda e, t4=t4, q4=q4, sin4=sin4: e.tensor_tensor(
                    out=t4[:, :, 1, :], in0=q4[:, :, 0, :], in1=sin4[:, :, 1, :],
                    op=ALU.mult), inc=vs)
                ctx.wait("vector", r2)
                r3 = ctx.op("vector", lambda e, cos3=cos3: e.tensor_tensor(
                    out=rop[:].rearrange("p (h w) -> p h w", w=64),
                    in0=rop[:].rearrange("p (h w) -> p h w", w=64),
                    in1=cos3, op=ALU.mult), inc=vs)
                ctx.wait("vector", r3)
                r4 = ctx.op("vector", lambda e: e.tensor_tensor(
                    out=rop[:], in0=rop[:], in1=sq[:], op=ALU.add), inc=vs)
                ctx.wait("scalar", r4)
                cst = ctx.op("scalar", lambda e: e.copy(out=qb[:], in_=rop[:]),
                             inc=ss)
                ctx.wait("tensor", cst)
                for g in range(2):
                    ctx.wait("tensor", ptr3_free.get(g))
                    tk = None
                    for p in range(8):
                        tk = ctx.op("tensor", lambda e, p=p, g=g: e.transpose(
                            out=ptr3[g][:, p * 128:(p + 1) * 128],
                            in_=qb[:, (8 * g + p) * 128:(8 * g + p + 1) * 128],
                            identity=ctx.identb[:]), inc=ts if p == 7 else None)
                    ctx.wait("scalar", tk)
                    ptr3_free[g] = ctx.op("scalar", lambda e, g=g: e.copy(
                        out=qT[:, g * 1024:(g + 1) * 1024], in_=ptr3[g][:]),
                        inc=ss)
                qT_ready = ptr3_free[1]

                def emit_scores(h):
                    hg = t * NH + h
                    par = hg % 2
                    s4 = hg % 4
                    g = h // 4
                    base = 64 * (g % 2)
                    pc = (h // 8) * 4 + h % 4
                    lq = qT[base:base + 64, 128 * pc:128 * pc + 128]
                    rk = kTv[64 * (g % 2):64 * (g % 2) + 64, g // 2,
                             128 * t:128 * t + 640]
                    ctx.wait("tensor", exp_tok.get(hg - 2))
                    if h == 0:
                        ctx.wait("tensor", qT_ready)
                    if hg == 0:
                        ctx.wait("tensor", kT_ready)
                    ctx.op("tensor", lambda e, lq=lq, rk=rk, par=par: e.matmul(
                        out=psc[par][:, 0:512], lhsT=lq, rhs=rk[:, 0:512],
                        start=True, stop=False), inc=ts)
                    ctx.op("tensor", lambda e, lq=lq, rk=rk, par=par: e.matmul(
                        out=psc[par][:, 512:640], lhsT=lq, rhs=rk[:, 512:640],
                        start=True, stop=False), inc=ts)
                    ctx.op("tensor", lambda e, par=par, t=t: e.matmul(
                        out=psc[par][:, 0:512], lhsT=ctx.identb[:],
                        rhs=masks[:, t * 640:t * 640 + 512],
                        start=False, stop=True), inc=ts)
                    m2 = ctx.op("tensor", lambda e, par=par, t=t: e.matmul(
                        out=psc[par][:, 512:640], lhsT=ctx.identb[:],
                        rhs=masks[:, t * 640 + 512:t * 640 + 640],
                        start=False, stop=True), inc=ts)
                    ctx.wait("vector", m2)
                    ctx.wait("vector", exp_tok.get(hg - 4))
                    mx = ctx.op("vector", lambda e, par=par, s4=s4: e.tensor_reduce(
                        out=scmax[:, s4:s4 + 1], in_=psc[par][:, 0:640],
                        axis=AX.X, op=ALU.max, negate=True), inc=vs)
                    return mx

                def emit_exp(h, mx):
                    hg = t * NH + h
                    s4 = hg % 4
                    pb = hg % 3
                    par = hg % 2
                    ctx.wait("scalar", ptr_done.get(hg - 3))
                    ctx.wait("scalar", rc_tok.get(hg - 8))
                    ctx.wait("scalar", mx)
                    ex = ctx.op("scalar", lambda e, s4=s4, pb=pb, par=par,
                                h=h: e.activation(
                        out=probs[pb][:], in_=psc[par][:, 0:640], func=AF.Exp,
                        scale=1.0, bias=scmax[:, s4:s4 + 1],
                        accum_out=ssum[:, h % 8:h % 8 + 1]), inc=ss)
                    exp_tok[hg] = ex
                    return ex

                def emit_rc(h):
                    hg = t * NH + h
                    ctx.wait("vector", exp_tok[hg])
                    rc = ctx.op("vector", lambda e, h=h: e.reciprocal(
                        out=rec[:, h % 8:h % 8 + 1],
                        in_=ssum[:, h % 8:h % 8 + 1]), inc=vs)
                    rc_tok[hg] = rc

                def emit_use_a(h):
                    hg = t * NH + h
                    pb = hg % 3
                    pb2 = hg % 2
                    ctx.wait("tensor", exp_tok[hg])
                    ctx.wait("tensor", ptr3_free.get(pb2))
                    tk = None
                    for c in range(5):
                        tk = ctx.op("tensor", lambda e, c=c, pb=pb, pb2=pb2: e.transpose(
                            out=ptr3[pb2][:, c * 128:(c + 1) * 128],
                            in_=probs[pb][:, c * 128:(c + 1) * 128],
                            identity=ctx.identb[:]), inc=ts if c == 4 else None)
                    ptr_done[hg] = tk
                    ctx.wait("vector", tk)
                    ctx.wait("vector", pvm_tok.get(hg - 2))
                    pev = ctx.op("vector", lambda e, pb2=pb2: e.tensor_copy(
                        out=probsT[pb2][:], in_=ptr3[pb2][:, 0:640]), inc=vs)
                    ptr3_free[pb2] = pev
                    pev_tok[hg] = pev

                def emit_use_b(h):
                    hg = t * NH + h
                    g = h // 4
                    grp = (hg // 8) % 2
                    pb2 = hg % 2
                    ctx.wait("tensor", pev_tok[hg])
                    ctx.wait("tensor", grp_ev.get(hg // 8 - 2))
                    if hg == 0:
                        ctx.wait("tensor", v_ready)
                    pv = None
                    for c in range(5):
                        pv = ctx.op("tensor", lambda e, c=c, g=g, grp=grp, h=h,
                                    t=t, pb2=pb2: e.matmul(
                            out=ppv[grp][:, 64 * (h % 8):64 * (h % 8) + 64],
                            lhsT=probsT[pb2][:, c * 128:(c + 1) * 128],
                            rhs=vtm[:, (t + c) * 512 + g * 64:(t + c) * 512 + g * 64 + 64],
                            start=(c == 0), stop=(c == 4)),
                            inc=ts)
                    pvm_tok[hg] = pv
                    ctx.wait("vector", pv)
                    ctx.wait("vector", rc_tok[hg])
                    evv = ctx.op("vector", lambda e, h=h, grp=grp: e.tensor_scalar(
                        out=atm[:, 64 * h:64 * h + 64],
                        in0=ppv[grp][:, 64 * (h % 8):64 * (h % 8) + 64],
                        scalar1=rec[:, h % 8:h % 8 + 1], scalar2=None,
                        op0=ALU.mult), inc=vs)
                    pv_ev[hg] = evv
                    if h % 8 == 7:
                        grp_ev[hg // 8] = evv
                    return evv

                for h in range(NH):
                    ctx.pump("shard")
                    mx_h = emit_scores(h)
                    if h >= 1:
                        emit_rc(h - 1)
                    emit_exp(h, mx_h)
                    if h >= 3:
                        emit_use_b(h - 3)
                    if h >= 2:
                        emit_use_a(h - 2)
                emit_rc(NH - 1)
                emit_use_b(NH - 3)
                emit_use_a(NH - 2)
                emit_use_b(NH - 2)
                emit_use_a(NH - 1)
                last_ev = emit_use_b(NH - 1)

                ctx.wait("tensor", last_ev)
                for g in range(2):
                    ctx.wait("tensor", ptr3_free.get(g))
                    tk = None
                    for p in range(8):
                        tk = ctx.op("tensor", lambda e, p=p, g=g: e.transpose(
                            out=ptr3[g][:, p * 128:(p + 1) * 128],
                            in_=atm[:, (8 * g + p) * 128:(8 * g + p + 1) * 128],
                            identity=ctx.identb[:]), inc=ts if p == 7 else None)
                    ctx.wait("scalar", tk)
                    ptr3_free[g] = ctx.op("scalar", lambda e, g=g, t=t: e.copy(
                        out=attnTv[:, 8 * g:8 * g + 8, 128 * t:128 * (t + 1)],
                        in_=ptr3[g][:].rearrange("p (k w) -> p k w", w=128)),
                        inc=ss)
            ctx.drain("shard")
        attnT_ready = ctx.tok(ss)
        if DEBUG_DUMP:
            vtok = ctx.tok("sV")
            ctx.dump("a_qcur", dbg_qcur[:], vtok)
            ctx.dump("a_sq", dbg_sq[:], vtok)
            ctx.dump("a_msq", dbg_msq[:], vtok)
            ctx.dump("a_rst", dbg_rst[:], vtok)
            ctx.dump("a_ssumD", dbg_ssum[:], vtok)
            ctx.dump("a_smaxD", dbg_smax[:], vtok)
        ctx.dump("a_attnT", attnT[:], attnT_ready)
        ctx.dump("a_kT", kT[:], attnT_ready)
        ctx.dump("a_vtm", vtm[:], attnT_ready)
        ctx.barrier_all()

        with ExitStack() as pjs:
            ctx.scope = pjs
            ppj = [pjs.enter_context(nc.psum_tensor(
                f"a_ppj{i}", [128, 512], F32)) for i in range(8)]
            wob = [ctx.sb(f"a_wob{i}", [128, 512]) for i in range(2)]
            woc = [ctx.sb(f"a_woc{i}", [128, 512], BF16) for i in range(2)]
            fat = ctx.sb("a_fat", [128, NT * 2048], BF16)
            aptmp = ctx.sb("a_aptmp", [128, 2048])
            cast_tok, mm_tok = {}, {}
            prev_ev = None
            for dc in range(4):
                mt = None
                for p in range(16):
                    n = dc * 16 + p
                    par = n % 2
                    ctx.wait("sync", cast_tok.get(n - 2))
                    dw = ctx.dma("sync", wob[par][:],
                                 ctx.in_wout[p][:, dc * 512:(dc + 1) * 512], ds)
                    ctx.wait("scalar", dw)
                    ctx.wait("scalar", mm_tok.get(n - 2))
                    ck = ctx.op("scalar", lambda e, par=par: e.copy(
                        out=woc[par][:], in_=wob[par][:]), inc=ss)
                    cast_tok[n] = ck
                    ctx.wait("tensor", ck)
                    if n == 0:
                        ctx.wait("tensor", attnT_ready)
                    if p == 0:
                        ctx.wait("tensor", prev_ev)
                    for t in range(NT):
                        mt = ctx.op("tensor", lambda e, p=p, t=t, par=par: e.matmul(
                            out=ppj[t][:], lhsT=attnTv[:, p, 128 * t:128 * (t + 1)],
                            rhs=woc[par][:], start=(p == 0), stop=(p == 15)),
                            inc=ts)
                    mm_tok[n] = mt
                ctx.wait("scalar", mt)
                if res_toks:
                    ctx.wait("scalar", res_toks[-1])
                ev = None
                for t in range(NT):
                    ev = ctx.op("scalar", lambda e, t=t, dc=dc: e.copy(
                        out=fat[:, t * 2048 + dc * 512: t * 2048 + (dc + 1) * 512],
                        in_=ppj[t][:]), inc=ss)
                prev_ev = ev
            ctx.dump("a_fat", fat[:], prev_ev)
            fsrc = [(fat[:, t * 2048:(t + 1) * 2048], prev_ev, t) for t in range(NT)]
            res_toks = emit_postln_residual(ctx, "a0_po", fsrc, ctx.xres_ap, aptmp)
            ctx.xres_tok = res_toks[-1]
        ctx.scope = old_scope
    return res_toks


def emit_l1_attn(ctx):
    nc = ctx.nc
    vs, ss, ts, ds, gs = "sV", "sS", "sT", "WS", "SS"
    NC = 16

    res_toks = []
    with ExitStack() as ph:
        old_scope, ctx.scope = ctx.scope, ph
        lnb = ctx.sb("l1_lnb", [128, 2048], BF16)
        ln2T = ctx.sb("l1_ln2T", [128, KD * 1024], BF16)
        lnTv = ln2T[:].rearrange("p (k w) -> p k w", w=1024)
        qf = ctx.sb("l1_qf", [64, 1024])
        kf = ctx.sb("l1_kf", [64, 1024])
        zb = ctx.sb("l1_zb", [64, 1024])
        zd = ctx.sb("l1_zd", [64, 1024])
        az = ctx.sb("l1_az", [64, 1024])
        ex1 = ctx.sb("l1_ex1", [64, 1024])
        lgb = ctx.sb("l1_lgb", [64, 1024])
        dlt = ctx.sb("l1_dlt", [64, 1024])
        Lc = ctx.sb("l1_L", [64, 1024])
        v64 = ctx.sb("l1_v64", [64, NC * 128], BF16)
        atst = ctx.sb("l1_at", [64, NC * 64], BF16)
        qst = ctx.sb("l1_qst", [64, NC * 64], BF16)
        tmat = ctx.sb("l1_tmat", [64, NC * 128])
        pcs = ctx.sb("l1_pc", [64, NC])
        Lrel = ctx.sb("l1_lrel", [64, 64])
        Pfm = ctx.sb("l1_P", [64, 64])
        Png = ctx.sb("l1_Png", [64, 64])
        Pr2 = ctx.sb("l1_Pr2", [64, 64])
        kd_ = ctx.sb("l1_kd", [64, 64])
        qtb = ctx.sb("l1_qtb", [64, 64], BF16)
        ktb = ctx.sb("l1_ktb", [64, 64], BF16)
        k2b = ctx.sb("l1_k2b", [64, 64], BF16)
        k2t = ctx.sb("l1_k2t", [64, 64], BF16)
        Sa = ctx.sb("l1_Sa", [64, 128])
        Sbf = ctx.sb("l1_Sbf", [64, 128], BF16)
        pay = ctx.sb("l1_pay", [64, 129])
        gath = ctx.sb("l1_gath", [64, 3 * 129])
        aj = ctx.sb("l1_aj", [64, 1])
        mpj = ctx.sb("l1_mpj", [64, 128])
        obf = ctx.sb("l1_obf", [64, 128], BF16)
        ofm = ctx.sb("l1_ofm", [128, NC * 64], BF16)
        trim = ctx.sb("l1_trim", [64, 64], BF16)
        bbc = ctx.sb("l1_bbc", [64, 1])
        bdc = ctx.sb("l1_bdc", [64, 1])
        usec = ctx.sb("l1_use", [64, 6])
        wqc = ctx.sb("l1_wqc", [128, 1024], BF16)
        wkc = ctx.sb("l1_wkc", [128, 1024], BF16)
        wbc = ctx.sb("l1_wbc", [128, 1024], BF16)
        wdc = ctx.sb("l1_wdc", [128, 1024], BF16)
        wvc = ctx.sb("l1_wvc", [128, 2048], BF16)
        woc2 = ctx.sb("l1_woc", [128, 2048], BF16)
        wstg = [ctx.sb("l1_wstg0", [128, 2048])]
        fat = [ctx.sb(f"l1_fat{i}", [128, 2048], BF16) for i in range(2)]
        aptmp = ctx.sb("l1_aptmp", [128, 2048])

        dtr = ctx.dma("scalar", trim[:], ctx.in_trim[:], gs)
        db1 = ctx.dma("scalar", bbc[:], ctx.in_bb.unsqueeze(1), gs,
                      allow_slow_non_contiguous=True)
        db2 = ctx.dma("scalar", bdc[:], ctx.in_bd.unsqueeze(1), gs,
                      allow_slow_non_contiguous=True)
        du = ctx.dma("scalar", usec[:], ctx.in_l1use[:], gs)
        wlist = [(ctx.in_l1wq, wqc, 1024), (ctx.in_l1wk, wkc, 1024),
                 (ctx.in_l1wb, wbc, 1024), (ctx.in_l1wd, wdc, 1024),
                 (ctx.in_l1wv, wvc, 2048), (ctx.in_l1wo, woc2, 2048)]
        ctok = {}
        for i, (inp, dst, w_) in enumerate(wlist):
            ctx.wait("sync", ctok.get(i - 1))
            dw = ctx.dma("sync", wstg[0][:, 0:w_], inp[:], ds)
            ctx.wait("scalar", dw)
            ctok[i] = ctx.op("scalar", lambda e, d=dst, w_=w_: e.copy(
                out=d[:], in_=wstg[0][:, 0:w_]), inc=ss)
        w_rdy = ctok[len(wlist) - 1]

        with ExitStack() as trs:
            ptr2 = [trs.enter_context(nc.psum_tensor(
                "l1_ptr" + str(i), [128, 1024], BF16)) for i in range(2)]

            def trt(t, atok):
                ev = None
                for g in range(2):
                    par = (2 * t + g) % 2
                    ctx.wait("tensor", atok)
                    tk = None
                    for p in range(8):
                        tk = ctx.op("tensor", lambda e, p=p, g=g, par=par: e.transpose(
                            out=ptr2[par][:, p * 128:(p + 1) * 128],
                            in_=lnb[:, (8 * g + p) * 128:(8 * g + p + 1) * 128],
                            identity=ctx.identb[:]), inc=ts if p == 7 else None)
                    ctx.wait("scalar", tk)
                    ev = ctx.op("scalar", lambda e, t=t, g=g, par=par: e.copy(
                        out=lnTv[:, 8 * g:8 * g + 8, 128 * t:128 * (t + 1)],
                        in_=ptr2[par][:].rearrange("p (k w) -> p k w", w=128)),
                        inc=ss)
                    ctx.wait("tensor", ev)
                return ev

            emit_ln_tiles(ctx, "l1ln", lambda t: (ctx.xres_ap(t), ctx.xres_tok),
                          NT, lambda t: lnb[:], trt, guard_lag=1)
        lnT_rdy = ctx.tok(ss)

        with ExitStack() as prs:
            psp = prs.enter_context(nc.psum_tensor("l1_psp", [64, 512], F32))
            psp2 = prs.enter_context(nc.psum_tensor("l1_psp2", [64, 512], F32))
            psv = [prs.enter_context(nc.psum_tensor(
                f"l1_psv{i}", [64, 128], F32)) for i in range(2)]
            ctx.wait("tensor", lnT_rdy)
            ctx.wait("tensor", w_rdy)
            evp = None
            for dst, wsrc in ((qf, wqc), (kf, wkc), (zb, wbc), (zd, wdc)):
                for half in range(2):
                    pp = psp if half == 0 else psp2
                    ctx.wait("tensor", evp)
                    mm = None
                    for k in range(16):
                        mm = ctx.op("tensor", lambda e, k=k, half=half, pp=pp,
                                    w=wsrc: e.matmul(
                            out=pp[:], lhsT=w[:, k * 64:(k + 1) * 64],
                            rhs=lnTv[:, k, half * 512:(half + 1) * 512],
                            start=(k == 0), stop=(k == 15)), inc=ts)
                    ctx.wait("scalar", mm)
                    evp = ctx.op("scalar", lambda e, dst=dst, half=half, pp=pp: e.copy(
                        out=dst[:, half * 512:(half + 1) * 512], in_=pp[:]),
                        inc=ss)
            for c in range(NC):
                pp = psv[c % 2]
                ctx.wait("tensor", evp if c < 2 else vev[c - 2])
                mm = None
                for k in range(16):
                    mm = ctx.op("tensor", lambda e, k=k, c=c, pp=pp: e.matmul(
                        out=pp[:], lhsT=lnTv[:, k, c * 64:(c + 1) * 64],
                        rhs=wvc[:, k * 128:(k + 1) * 128],
                        start=(k == 0), stop=(k == 15)), inc=ts)
                ctx.wait("scalar", mm)
                t_ = ctx.op("scalar", lambda e, c=c, pp=pp: e.copy(
                    out=v64[:, c * 128:(c + 1) * 128], in_=pp[:]), inc=ss)
                if c == 0:
                    vev = {}
                vev[c] = t_
            proj_done = ctx.tok(ss)

        ctx.wait("vector", proj_done)
        ctx.wait("vector", db1)
        ctx.wait("vector", db2)
        g1 = ctx.op("vector", lambda e: e.tensor_scalar(
            out=zb[:], in0=zb[:], scalar1=bbc[:], scalar2=None, op0=ALU.add),
            inc=vs)
        g2 = ctx.op("vector", lambda e: e.tensor_scalar(
            out=zd[:], in0=zd[:], scalar1=bdc[:], scalar2=None, op0=ALU.add),
            inc=vs)
        for zsrc, dst, gtok, isb in ((zb, lgb, g1, True), (zd, dlt, g2, False)):
            ctx.wait("scalar", gtok)
            a1 = ctx.op("scalar", lambda e, z=zsrc: e.activation(
                out=az[:], in_=z[:], func=AF.Abs), inc=ss)
            ctx.wait("scalar", a1)
            a2 = ctx.op("scalar", lambda e: e.activation(
                out=ex1[:], in_=az[:], func=AF.Exp, scale=-1.0), inc=ss)
            ctx.wait("scalar", a2)
            a3 = ctx.op("scalar", lambda e: e.activation(
                out=az[:], in_=ex1[:], func=AF.Ln, bias=ctx.onecol[0:64, :],
                scale=1.0), inc=ss)
            ctx.wait("vector", a3)
            m1 = ctx.op("vector", lambda e, z=zsrc, isb=isb: e.tensor_scalar(
                out=ex1[:], in0=z[:], scalar1=0.0, scalar2=None,
                op0=(ALU.min if isb else ALU.max)), inc=vs)
            ctx.wait("vector", m1)
            m2 = ctx.op("vector", lambda e, d=dst, isb=isb: e.tensor_tensor(
                out=d[:], in0=ex1[:], in1=az[:],
                op=(ALU.subtract if isb else ALU.add)), inc=vs)
            ctx.wait("scalar", m2)
        lgb_rdy = m2
        ctx.wait("vector", lgb_rdy)
        mz = ctx.op("vector", lambda e: e.memset(az[:], 1.0), inc=vs)
        ctx.wait("vector", mz)
        Ltok = ctx.op("vector", lambda e: e.tensor_tensor_scan(
            out=Lc[:], data0=az[:], data1=lgb[:], initial=0.0,
            op0=ALU.mult, op1=ALU.add), inc=vs)

        with ExitStack() as cas:
            pat = cas.enter_context(nc.psum_tensor("l1_pat", [64, 64], F32))
            ptm = cas.enter_context(nc.psum_tensor("l1_ptm", [64, 128], F32))
            ptk = cas.enter_context(nc.psum_tensor("l1_ptk", [64, 64], BF16))
            ctx.wait("vector", Ltok)
            prev = {}
            for c in range(NC):
                cs = slice(c * 64, (c + 1) * 64)
                ctx.wait("vector", prev.get(("p3", c - 1)))
                if c == 0:
                    l1_ = ctx.op("vector", lambda e, cs=cs: e.tensor_scalar(
                        out=Lrel[:], in0=Lc[:, cs], scalar1=0.0, scalar2=None,
                        op0=ALU.subtract), inc=vs)
                else:
                    l1_ = ctx.op("vector", lambda e, cs=cs, c=c: e.tensor_scalar(
                        out=Lrel[:], in0=Lc[:, cs],
                        scalar1=Lc[:, c * 64 - 1:c * 64], scalar2=None,
                        op0=ALU.subtract), inc=vs)
                ctx.wait("scalar", l1_)
                ctx.wait("scalar", prev.get(("dve", c - 1)))
                p1 = ctx.op("scalar", lambda e: e.activation(
                    out=Pfm[:], in_=Lrel[:], func=AF.Exp), inc=ss)
                p2 = ctx.op("scalar", lambda e: e.activation(
                    out=Png[:], in_=Lrel[:], func=AF.Exp, scale=-1.0), inc=ss)
                p3 = ctx.op("scalar", lambda e: e.activation(
                    out=Pr2[:], in_=Lrel[:], func=AF.Exp, scale=-1.0,
                    bias=Lrel[:, 63:64]), inc=ss)
                prev[("p3", c)] = p3
                ctx.wait("vector", p3)
                kd1 = ctx.op("vector", lambda e, cs=cs: e.tensor_tensor(
                    out=kd_[:], in0=kf[:, cs], in1=dlt[:, cs], op=ALU.mult),
                    inc=vs)
                q1_ = ctx.op("vector", lambda e, cs=cs: e.tensor_tensor(
                    out=qtb[:], in0=qf[:, cs], in1=Pfm[:], op=ALU.mult), inc=vs)
                ctx.wait("vector", q1_)
                q2_ = ctx.op("vector", lambda e, cs=cs, c=c: e.tensor_copy(
                    out=qst[:, c * 64:(c + 1) * 64], in_=qtb[:]), inc=vs)
                ctx.wait("vector", kd1)
                k1_ = ctx.op("vector", lambda e: e.tensor_tensor(
                    out=ktb[:], in0=kd_[:], in1=Png[:], op=ALU.mult), inc=vs)
                ctx.wait("vector", prev.get(("k2ev", c - 1)))
                k2_ = ctx.op("vector", lambda e: e.tensor_tensor(
                    out=k2b[:], in0=kd_[:], in1=Pr2[:], op=ALU.mult), inc=vs)
                pcw = ctx.op("vector", lambda e, c=c: e.tensor_copy(
                    out=pcs[:, c:c + 1], in_=Pfm[:, 63:64]), inc=vs)
                prev[("pcw", c)] = pcw
                ctx.wait("tensor", k1_)
                ctx.wait("tensor", q1_)
                ctx.wait("tensor", prev.get(("atev", c - 1)))
                at_ = ctx.op("tensor", lambda e: e.matmul(
                    out=pat[:], lhsT=ktb[:], rhs=qtb[:], start=True, stop=True),
                    inc=ts)
                ctx.wait("vector", at_)
                ctx.wait("vector", dtr)
                atev = ctx.op("vector", lambda e, c=c: e.tensor_tensor(
                    out=atst[:, c * 64:(c + 1) * 64], in0=pat[:], in1=trim[:],
                    op=ALU.mult), inc=vs)
                prev[("atev", c)] = atev
                ctx.wait("tensor", k2_)
                ctx.wait("tensor", prev.get(("k2ev", c - 1)))
                tk2 = ctx.op("tensor", lambda e: e.transpose(
                    out=ptk[:], in_=k2b[:], identity=ctx.identb[0:64, 0:64]),
                    inc=ts)
                ctx.wait("scalar", tk2)
                k2ev = ctx.op("scalar", lambda e: e.copy(
                    out=k2t[:], in_=ptk[:]), inc=ss)
                prev[("k2ev", c)] = k2ev
                prev[("qtb", c)] = k2ev
                ctx.wait("tensor", k2ev)
                ctx.wait("tensor", proj_done)
                ctx.wait("tensor", prev.get(("tmev", c - 1)))
                tm_ = ctx.op("tensor", lambda e, c=c: e.matmul(
                    out=ptm[:], lhsT=k2t[:], rhs=v64[:, c * 128:(c + 1) * 128],
                    start=True, stop=True), inc=ts)
                ctx.wait("scalar", tm_)
                tmev = ctx.op("scalar", lambda e, c=c: e.copy(
                    out=tmat[:, c * 128:(c + 1) * 128], in_=ptm[:]), inc=ss)
                prev[("tmev", c)] = tmev
                ctx.wait("vector", tmev)
                ctx.wait("vector", prev.get(("pcw", c)))
                if c == 0:
                    sz = ctx.op("vector", lambda e: e.memset(Sa[:], 0.0), inc=vs)
                    ctx.wait("vector", sz)
                s1_ = ctx.op("vector", lambda e, c=c: e.tensor_scalar(
                    out=Sa[:], in0=Sa[:], scalar1=pcs[:, c:c + 1], scalar2=None,
                    op0=ALU.mult), inc=vs)
                ctx.wait("vector", s1_)
                s2_ = ctx.op("vector", lambda e, c=c: e.tensor_tensor(
                    out=Sa[:], in0=Sa[:], in1=tmat[:, c * 128:(c + 1) * 128],
                    op=ALU.add), inc=vs)
                ctx.wait("vector", s2_)
                prev[("dve", c)] = s2_
            pm = ctx.op("vector", lambda e: e.tensor_copy(
                out=pay[:, 0:128], in_=Sa[:]), inc=vs)
            ctx.wait("scalar", Ltok)
            pe_ = ctx.op("scalar", lambda e: e.activation(
                out=pay[:, 128:129], in_=Lc[:, 1023:1024], func=AF.Exp), inc=ss)
            ctx.wait("scalar", pm)
            ctx.wait("scalar", pe_)
            dpay = ctx.dma("scalar", ctx.cc_l1_in.ap(), pay[:], gs)
            ctx.wait("gpsimd", dpay)
            cc = ctx.op("gpsimd", lambda e: e.collective_compute(
                "AllGather", ALU.bypass,
                replica_groups=[[0, 1, 2, 3], [4, 5, 6, 7]],
                ins=[ctx.cc_l1_in.ap()], outs=[ctx.cc_l1_out.ap()]),
                inc="g_cc2", inc_n=1)
            ctx.wait("scalar", cc)
            dg_toks = []
            for j in range(3):
                dg_toks.append(ctx.dma("scalar", gath[:, j * 129:(j + 1) * 129],
                               ctx.cc_l1_out.ap()[j * 64:(j + 1) * 64, :], gs))
            ctx.waits("vector", dg_toks)
            ctx.wait("vector", du)
            sz2 = ctx.op("vector", lambda e: e.memset(Sa[:], 0.0), inc=vs)
            ctx.wait("vector", sz2)
            for j in range(3):
                a1_ = ctx.op("vector", lambda e, j=j: e.scalar_tensor_tensor(
                    out=aj[:], in0=gath[:, j * 129 + 128:j * 129 + 129],
                    scalar=usec[:, j:j + 1], in1=usec[:, 3 + j:4 + j],
                    op0=ALU.mult, op1=ALU.add), inc=vs)
                m1_ = ctx.op("vector", lambda e, j=j: e.tensor_scalar(
                    out=mpj[:], in0=gath[:, j * 129:j * 129 + 128],
                    scalar1=usec[:, j:j + 1], scalar2=None, op0=ALU.mult),
                    inc=vs)
                ctx.wait("vector", m1_)
                s1_ = ctx.op("vector", lambda e: e.tensor_scalar(
                    out=Sa[:], in0=Sa[:], scalar1=aj[:], scalar2=None,
                    op0=ALU.mult), inc=vs)
                ctx.wait("vector", s1_)
                s2_ = ctx.op("vector", lambda e: e.tensor_tensor(
                    out=Sa[:], in0=Sa[:], in1=mpj[:], op=ALU.add), inc=vs)
                ctx.wait("vector", s2_)
            combine_done = s2_

            pso = cas.enter_context(nc.psum_tensor("l1_pso", [64, 128], F32))
            pto = cas.enter_context(nc.psum_tensor("l1_pto", [128, 64], BF16))
            obev = None
            ctx.wait("vector", prev.get(("pcw", NC - 1)))
            ctx.wait("scalar", combine_done)
            for c in range(NC):
                sb_ = ctx.op("scalar", lambda e: e.copy(
                    out=Sbf[:], in_=Sa[:]), inc=ss)
                ctx.wait("tensor", sb_)
                ctx.wait("tensor", obev)
                o1 = ctx.op("tensor", lambda e, c=c: e.matmul(
                    out=pso[:], lhsT=atst[:, c * 64:(c + 1) * 64],
                    rhs=v64[:, c * 128:(c + 1) * 128], start=True, stop=False),
                    inc=ts)
                o2 = ctx.op("tensor", lambda e, c=c: e.matmul(
                    out=pso[:], lhsT=qst[:, c * 64:(c + 1) * 64], rhs=Sbf[:],
                    start=False, stop=True), inc=ts)
                ctx.wait("scalar", o2)
                oev = ctx.op("scalar", lambda e: e.copy(
                    out=obf[:], in_=pso[:]), inc=ss)
                ctx.wait("tensor", oev)
                to_ = ctx.op("tensor", lambda e: e.transpose(
                    out=pto[:, 0:64], in_=obf[:],
                    identity=ctx.identb[0:64, 0:64]), inc=ts)
                ctx.wait("scalar", to_)
                obev = ctx.op("scalar", lambda e, c=c: e.copy(
                    out=ofm[:, c * 64:(c + 1) * 64], in_=pto[:, 0:64]), inc=ss)
                ctx.wait("vector", o2)
                s1_ = ctx.op("vector", lambda e, c=c: e.tensor_scalar(
                    out=Sa[:], in0=Sa[:], scalar1=pcs[:, c:c + 1], scalar2=None,
                    op0=ALU.mult), inc=vs)
                ctx.wait("vector", s1_)
                s2_ = ctx.op("vector", lambda e, c=c: e.tensor_tensor(
                    out=Sa[:], in0=Sa[:], in1=tmat[:, c * 128:(c + 1) * 128],
                    op=ALU.add), inc=vs)
                ctx.wait("scalar", s2_)
            ofm_rdy = ctx.tok(ss)

        with ExitStack() as pjs:
            ctx.scope = pjs
            ppj = [pjs.enter_context(nc.psum_tensor(
                f"l1_ppj{i}", [128, 512], F32)) for i in range(2)]
            ev = None
            fguard = {}
            for t in range(NT):
                fpar = t % 2
                for dcw in range(4):
                    n = t * 4 + dcw
                    par = n % 2
                    ctx.wait("tensor", ofm_rdy)
                    ctx.wait("tensor", ev)
                    mm = ctx.op("tensor", lambda e, t=t, dcw=dcw, par=par: e.matmul(
                        out=ppj[par][:], lhsT=ofm[:, t * 128:(t + 1) * 128],
                        rhs=woc2[:, dcw * 512:(dcw + 1) * 512],
                        start=True, stop=True), inc=ts)
                    ctx.wait("scalar", mm)
                    if dcw == 0:
                        ctx.wait("scalar", fguard.get(t - 2))
                    ev = ctx.op("scalar", lambda e, dcw=dcw, fpar=fpar, par=par: e.copy(
                        out=fat[fpar][:, dcw * 512:(dcw + 1) * 512],
                        in_=ppj[par][:]), inc=ss)
                rt = emit_postln_residual(ctx, f"l1_po{t}",
                                          [(fat[fpar][:], ev, t)],
                                          ctx.xres_ap, aptmp)
                fguard[t] = rt[-1]
                res_toks.extend(rt)
            ctx.xres_tok = res_toks[-1]
        ctx.scope = old_scope
    return res_toks


def build(blocks):
    nc = bass.Bass("TRN2", target_bir_lowering=False)
    first = blocks[0] == "l0a"
    need_q = any(b in blocks for b in ("l0f", "l1f"))
    qwhich = [w for w, b in (("l0", "l0f"), ("l1", "l1f")) if b in blocks]

    with ExitStack() as stack:
        ctx = Ctx(nc, stack)
        ctx.dbg_toks = []
        dr = lambda n, s, d=F32, k="ExternalInput": nc.dram_tensor(n, s, d, kind=k).ap()

        if first:
            ctx.in_x = dr("x_in", [HALO + NTOK, D])
        else:
            ctx.in_x = dr("x_in", [NTOK, D])
        ctx.out_x = dr("x_out", [NTOK, D], F32, "ExternalOutput")
        ctx.in_ident = dr("ident", [128, 128], BF16)
        if "l0a" in blocks:
            ctx.in_wq = dr("wq_t", [16, 128, 2048])
            ctx.in_wkv = dr("wkv_t", [8, 128, 2048])
            ctx.in_wout = dr("wout_t", [16, 128, 2048])
            ctx.in_cos = dr("cos_t", [ATT * 128, 64])
            ctx.in_sin = dr("sin_t", [ATT * 128, 64])
            ctx.in_qnw = dr("qnw_t", [1, 2048])
            ctx.in_knw = dr("knw_t", [1, 512])
            ctx.in_masks = dr("masks_t", [NT, 128, 640], BF16)
            ctx.hbm_qtmp = nc.dram_tensor("hbm_qtmp", [NT, 128, 2048], BF16).ap()
        if "l1a" in blocks:
            ctx.in_l1wq = dr("l1wq", [128, 1024])
            ctx.in_l1wk = dr("l1wk", [128, 1024])
            ctx.in_l1wb = dr("l1wb", [128, 1024])
            ctx.in_l1wd = dr("l1wd", [128, 1024])
            ctx.in_l1wv = dr("l1wv", [128, 2048])
            ctx.in_l1wo = dr("l1wo", [128, 2048])
            ctx.in_bb = dr("l1bb", [64])
            ctx.in_bd = dr("l1bd", [64])
            ctx.in_trim = dr("l1trim", [64, 64], BF16)
            ctx.in_l1use = dr("l1use", [64, 6])
            ctx.cc_l1_in = nc.dram_tensor("cc_l1_in", [64, 129], F32)
            ctx.cc_l1_out = nc.dram_tensor("cc_l1_out", [256, 129], F32)
        if need_q:
            ctx.in_qsl = dr("qslices", [21, 128, 4096])
            ctx.in_qsrat = dr("qsrat", [1, 16])
            ctx.in_qseps = dr("qseps", [1, 16])
            ctx.cc_qs_in = nc.dram_tensor("cc_qs_in", [1, 16], F32)
            ctx.cc_qs_out = nc.dram_tensor("cc_qs_out", [1, 16], F32)
        if "l0f" in blocks:
            ctx.in_wup0 = dr("wup0s", [8, 128, 2048])
            ctx.in_wgt0 = dr("wgt0s", [8, 128, 2048])
            ctx.in_wdn0 = dr("wdn0s", [4, 4, 128, 1024])
            ctx.hbm_hid0 = nc.dram_tensor("hbm_hid0", [3, EC0 // 4, 128, 1024], BF16).ap()
            ctx.ag0_in = nc.dram_tensor("ag0_in", [24, 128, 2048], BF16)
            ctx.ag0_out = nc.dram_tensor("ag0_out", [192, 128, 2048], BF16,
                                         addr_space="Shared")
        if "l1f" in blocks:
            ctx.in_wup1 = dr("wup1s", [6, 128, 2048])
            ctx.in_wgt1 = dr("wgt1s", [6, 128, 2048])
            ctx.in_wdn1 = dr("wdn1s", [3, 4, 128, 1024])
            ctx.hbm_hid1 = nc.dram_tensor("hbm_hid1", [3, EC1 // 4, 128, 1024], BF16).ap()
            ctx.ag1_in = nc.dram_tensor("ag1_in", [18, 128, 2048], BF16)
            ctx.ag1_out = nc.dram_tensor("ag1_out", [144, 128, 2048], BF16,
                                         addr_space="Shared")

        xres = ctx.sb("xres", [128, NT * 2048])
        ctx.xres_ap = lambda t: xres[:, t * 2048:(t + 1) * 2048]
        ctx.identb = ctx.sb("identb", [128, 128], BF16)
        ctx.eps5 = ctx.sb("eps5", [128, 1])
        ctx.eps6 = ctx.sb("eps6", [128, 1])
        ctx.onecol = ctx.sb("onecol", [128, 1])
        if need_q:
            ctx.in_qsrat_sb = ctx.sb("qsrat_sb", [1, 16])
            ctx.in_qseps_sb = ctx.sb("qseps_sb", [1, 16])

        ctok = None
        for tens, v in ((ctx.eps5, 1e-5), (ctx.eps6, 1e-6), (ctx.onecol, 1.0)):
            ctok = ctx.op("vector", lambda e, tens=tens, v=v: e.memset(tens[:], v),
                          inc="sV")
        ctx.consts_tok = ctok
        itok = ctx.dma("scalar", ctx.identb[:], ctx.in_ident[:], "SS")
        xoff = HALO if first else 0
        for t in range(NT):
            ctx.dma("scalar", ctx.xres_ap(t),
                    ctx.in_x[xoff + t * 128: xoff + (t + 1) * 128, :],
                    "SS")
        ctx.xres_tok = ctx.stream_toks("SS")
        ctx.ident_tok = itok
        for e in ("vector", "scalar", "tensor"):
            ctx.wait(e, ctx.consts_tok)
            ctx.wait(e, itok)
        if need_q:
            ctx.thscols = ctx.sb("qs_ths", [128, 16])
            ctx.qs_acc = ctx.sb("qs_acc", [128, 8])
            ctx.qs_red = ctx.sb("qs_red", [128, 1])
            ctx.qs_cr = ctx.sb("qs_cr", [1, 8])
            ctx.qs_tot = ctx.sb("qs_tot", [1, 16])
            q1 = ctx.dma("scalar", ctx.in_qsrat_sb[:], ctx.in_qsrat[:], "SS")
            q2 = ctx.dma("scalar", ctx.in_qseps_sb[:], ctx.in_qseps[:], "SS")
            ctx.waits("vector", [q1, q2])
            lis = [li for li, bb in ((0, "l0f"), (1, "l1f")) if bb in blocks]
            gsc = gen_quant_scales(ctx, qwhich)
            gsh = gen_shard_all(ctx, lis)
            if "l0a" in blocks:
                ctx.gens["scales"] = gsc
                ctx.gens["shard"] = gsh
            else:
                qs_stack = ExitStack()
                old_scope, ctx.scope = ctx.scope, qs_stack
                for _ in gsc:
                    pass
                for _ in gsh:
                    pass
                ctx.scope = old_scope
                qs_stack.close()
                ctx.barrier_all()

        final_toks = [ctx.xres_tok]
        for b in blocks:
            if b == "l0a":
                final_toks = emit_l0_attn(ctx)
                assert not any(ctx.gens.values()), "prologue gens not drained"
            elif b == "l0f":
                final_toks = emit_ffn(ctx, 0)
            elif b == "l1a":
                final_toks = emit_l1_attn(ctx)
            elif b == "l1f":
                final_toks = emit_ffn(ctx, 1)
            ctx.xres_tok = final_toks[-1]
            ctx.barrier_all()

        ctx.waits("scalar", final_toks)
        ctx.waits("scalar", ctx.dbg_toks)
        for t in range(NT):
            ctx.dma("scalar", ctx.out_x[t * 128:(t + 1) * 128, :],
                    ctx.xres_ap(t), "SS")
        ctx.waits("scalar", ctx.stream_toks("SS"))

        block = stack.enter_context(nc.Block())
        ctx.run(block)
    return nc


def _pret_upgate(W, EC, ECpad):
    H = W.reshape(EC, 128, 16, 128).transpose(0, 3, 2, 1)
    H = np.ascontiguousarray(H.reshape(EC, 128, 2048))
    if ECpad > EC:
        H = np.concatenate(
            [H, np.zeros((ECpad - EC, 128, 2048), np.float32)], 0)
    return H


def _pret_down(Wd, EC, E2pad):
    H = Wd.reshape(4, 512, EC, 128).transpose(0, 2, 3, 1)
    H = H.reshape(4, EC // 2, 2, 128, 512).transpose(1, 0, 3, 2, 4)
    H = np.ascontiguousarray(H.reshape(EC // 2, 4, 128, 1024))
    if E2pad > EC // 2:
        H = np.concatenate(
            [H, np.zeros((E2pad - EC // 2, 4, 128, 1024), np.float32)], 0)
    return H


def _qslices(core, mats):
    out = np.zeros((21, 128, 4096), np.float32)
    idx = 0
    for arr, npc in mats:
        rows = arr.shape[0] // NCORES
        sl = np.ascontiguousarray(arr[core * rows:(core + 1) * rows]).reshape(-1)
        buf = np.zeros(npc * 128 * 4096, np.float32)
        buf[:sl.size] = sl
        out[idx:idx + npc] = buf.reshape(npc, 128, 4096)
        idx += npc
    return out


def host_prep(inputs):
    wqkv = np.asarray(inputs["L0_attn_w_qkv"], np.float32)
    G = np.ascontiguousarray(wqkv.reshape(3072, 16, 128).transpose(2, 1, 0))
    hA = [h for h in range(NH) if (h // 4) % 2 == 0]
    hB = [h for h in range(NH) if (h // 4) % 2 == 1]
    perm = []
    for p in range(16):
        perm.extend(range(hA[p] * 64, hA[p] * 64 + 64))
        perm.extend(range(hB[p] * 64, hB[p] * 64 + 64))
    wq_t = np.ascontiguousarray(
        G[:, :, :2048][:, :, perm].reshape(128, 16, 16, 128)
        .transpose(2, 0, 1, 3).reshape(16, 128, 2048))
    wkv_t = np.ascontiguousarray(
        G[:, :, 2048:].reshape(128, 16, 8, 128).transpose(2, 0, 1, 3)
        .reshape(8, 128, 2048))
    wout_t = np.ascontiguousarray(
        np.asarray(inputs["L0_attn_w_out"], np.float32).T.reshape(16, 128, 2048))
    qnw_t = np.tile(np.asarray(inputs["L0_attn_q_norm_w"], np.float32), 32)[None, :]
    knw_t = np.tile(np.asarray(inputs["L0_attn_k_norm_w"], np.float32), 8)[None, :]
    inv = 1.0 / 10000.0 ** (np.arange(0, 64, 2, dtype=np.float32) / 64.0)
    wup0 = _pret_upgate(np.asarray(inputs["L0_ffn_W_up"], np.float32), EC0, 64)
    wgt0 = _pret_upgate(np.asarray(inputs["L0_ffn_W_gate"], np.float32), EC0, 64)
    wdn0 = _pret_down(np.asarray(inputs["L0_ffn_W_down"], np.float32), EC0, 32)
    wup1 = _pret_upgate(np.asarray(inputs["L1_ffn_w1"], np.float32), EC1, 48)
    wgt1 = _pret_upgate(np.asarray(inputs["L1_ffn_w2"], np.float32), EC1, 48)
    wdn1 = _pret_down(np.asarray(inputs["L1_ffn_w3"], np.float32), EC1, 24)
    ident = np.eye(128, dtype=BF)

    rat = np.zeros((1, 16), np.float32)
    epsr = np.zeros((1, 16), np.float32)
    for m, nel in ((0, E0 * D), (1, E0 * D), (2, E0 * D),
                   (3, E1 * D), (4, E1 * D), (5, E1 * D)):
        rat[0, m] = 0.5 / nel
        rat[0, 8 + m] = 1.0 / nel
        epsr[0, m] = 0.5e-8 / NCORES
        epsr[0, 8 + m] = 1e-8 / NCORES

    qs_mats = [(np.asarray(inputs["L0_ffn_W_up"], np.float32), 4),
               (np.asarray(inputs["L0_ffn_W_gate"], np.float32), 4),
               (np.asarray(inputs["L0_ffn_W_down"], np.float32), 4),
               (np.asarray(inputs["L1_ffn_w1"], np.float32), 3),
               (np.asarray(inputs["L1_ffn_w2"], np.float32), 3),
               (np.asarray(inputs["L1_ffn_w3"], np.float32), 3)]

    def _ptile(W, cols):
        return np.ascontiguousarray(
            W.reshape(16, 128, cols).transpose(1, 0, 2).reshape(128, 16 * cols))

    l1wq = _ptile(np.asarray(inputs["L1_attn_Wq"], np.float32), 64)
    l1wk = _ptile(np.asarray(inputs["L1_attn_Wk"], np.float32), 64)
    l1wb = _ptile(np.asarray(inputs["L1_attn_Wb"], np.float32), 64)
    l1wd = _ptile(np.asarray(inputs["L1_attn_Wd"], np.float32), 64)
    l1wv = _ptile(np.asarray(inputs["L1_attn_Wv"], np.float32), 128)
    l1wo = np.ascontiguousarray(np.asarray(inputs["L1_attn_Wo"], np.float32))
    trim = (np.arange(64)[:, None] <= np.arange(64)[None, :]).astype(BF)

    percore = []
    x = np.asarray(inputs["L0_attn_x"], np.float32)
    for c in range(NCORES):
        b, qt = c // 4, c % 4
        xcore = np.zeros((HALO + NTOK, D), np.float32)
        lo = qt * NTOK - HALO
        src_lo = max(lo, 0)
        xcore[src_lo - lo:, :] = x[b, src_lo: qt * NTOK + NTOK, :]
        pos = np.arange(qt * NTOK - HALO, qt * NTOK + NTOK, dtype=np.float32)
        ang = pos[:, None] * inv[None, :]
        cos_t = np.ascontiguousarray(
            np.concatenate([np.cos(ang), np.cos(ang)], 1).astype(np.float32))
        sin_t = np.ascontiguousarray(
            np.concatenate([-np.sin(ang), np.sin(ang)], 1).astype(np.float32))
        masks = np.full((NT, 128, 640), NEG, np.float32)
        for i in range(NT):
            r = np.arange(128)[:, None]
            cc = np.arange(640)[None, :]
            keyg = qt * NTOK - HALO + 128 * i + cc
            valid = (cc >= r + 1) & (cc <= r + 512) & (keyg >= 0)
            masks[i][valid.squeeze() if valid.ndim > 2 else valid] = 0.0
        percore.append({
            "x_first": xcore,
            "ident": ident,
            "l1wq": l1wq, "l1wk": l1wk, "l1wb": l1wb, "l1wd": l1wd,
            "l1wv": l1wv, "l1wo": l1wo, "l1trim": trim,
            "l1bb": np.asarray(inputs["L1_attn_bb"], np.float32),
            "l1bd": np.asarray(inputs["L1_attn_bd"], np.float32),
            "l1use": np.concatenate([
                (np.arange(3) < qt).astype(np.float32),
                1.0 - (np.arange(3) < qt).astype(np.float32)])[None, :]
                .repeat(64, 0).copy(),
            "wq_t": wq_t, "wkv_t": wkv_t, "wout_t": wout_t,
            "qnw_t": qnw_t, "knw_t": knw_t,
            "cos_t": cos_t, "sin_t": sin_t,
            "masks_t": masks.astype(BF),
            "qslices": _qslices(c, qs_mats),
            "qsrat": rat, "qseps": epsr,
            "wup0s": np.ascontiguousarray(wup0[c * 8:(c + 1) * 8]),
            "wgt0s": np.ascontiguousarray(wgt0[c * 8:(c + 1) * 8]),
            "wdn0s": np.ascontiguousarray(wdn0[c * 4:(c + 1) * 4]),
            "wup1s": np.ascontiguousarray(wup1[c * 6:(c + 1) * 6]),
            "wgt1s": np.ascontiguousarray(wgt1[c * 6:(c + 1) * 6]),
            "wdn1s": np.ascontiguousarray(wdn1[c * 3:(c + 1) * 3]),
        })
    return percore


def in_map_for(blocks, percore, x_cur):
    first = blocks[0] == "l0a"
    need_q = any(b in blocks for b in ("l0f", "l1f"))
    maps = []
    for c in range(NCORES):
        b, qt = c // 4, c % 4
        m = {"ident": percore[c]["ident"]}
        if "l0a" in blocks:
            for k in ("wq_t", "wkv_t", "wout_t", "qnw_t", "knw_t", "cos_t",
                      "sin_t", "masks_t"):
                m[k] = percore[c][k]
        if first:
            m["x_in"] = percore[c]["x_first"]
        else:
            m["x_in"] = np.ascontiguousarray(x_cur[b, qt * NTOK:(qt + 1) * NTOK, :])
        if need_q:
            m["qslices"] = percore[c]["qslices"]
            m["qsrat"] = percore[c]["qsrat"]
            m["qseps"] = percore[c]["qseps"]
        if "l1a" in blocks:
            for k in ("l1wq", "l1wk", "l1wb", "l1wd", "l1wv", "l1wo",
                      "l1bb", "l1bd", "l1trim", "l1use"):
                m[k] = percore[c][k]
        if "l0f" in blocks:
            for k in ("wup0s", "wgt0s", "wdn0s"):
                m[k] = percore[c][k]
        if "l1f" in blocks:
            for k in ("wup1s", "wgt1s", "wdn1s"):
                m[k] = percore[c][k]
        maps.append(m)
    return maps


def gather_out(res):
    out = np.zeros((B, S, D), np.float32)
    for c in range(NCORES):
        b, qt = c // 4, c % 4
        out[b, qt * NTOK:(qt + 1) * NTOK, :] = res.results[c]["x_out"]
    return out


def kernel(**inputs):
    percore = host_prep(inputs)
    x_cur = np.asarray(inputs["L0_attn_x"], np.float32)
    launches = [ALL_BLOCKS] if FUSED else UNFUSED
    for blocks in launches:
        nc = build(tuple(blocks))
        maps = in_map_for(blocks, percore, x_cur)
        res = run_bass_kernel_spmd(nc, maps, core_ids=list(range(NCORES)))
        x_cur = gather_out(res)
    return x_cur
